# version 17
# baseline (speedup 1.0000x reference)
import numpy as np
import ml_dtypes
import concourse.bass as bass
import concourse.mybir as mybir
from concourse.bass import IndirectOffsetOnAxis
from concourse.bass_utils import run_bass_kernel_spmd
from contextlib import ExitStack

F32 = mybir.dt.float32
BF16 = mybir.dt.bfloat16
U32 = mybir.dt.uint32
AF = mybir.ActivationFunctionType
ALU = mybir.AluOpType
AX = mybir.AxisListType
AP = bass.AP

NCORES = 8
B, T, C = 4, 4096, 4096
TL = 2048
HD = 128
HPC = 4
NGT = 32
SCALE_A = float(1.0 / np.sqrt(HD))
SCALE_B = float(1.0 / np.sqrt(HD))
PB = 6144
NE, RANK, RD = 64, 8, 16
BOT = NE * RANK
EPS = 1e-5

_EXPOSE = False


class PSem:
    __slots__ = ("h", "n")

    def __init__(self, h):
        self.h = h
        self.n = 0


class Stream:
    def __init__(self, kc, parity=1, step=1, per_unit=1, cls="c"):
        self.ps = [kc.take_sem(cls) for _ in range(parity)]
        if cls != "p":
            kc.reg_stream(self)
        self.base = [p.n for p in self.ps]
        self.parity = parity
        self.step = step
        self.per_unit = per_unit
        self.issued = 0

    def inc(self, instr):
        g = self.issued // self.per_unit
        p = self.ps[g % self.parity]
        instr.then_inc(p.h, self.step)
        p.n += self.step
        self.issued += 1

    def wait(self, eng, g):
        if g < 0:
            return
        p = self.ps[g % self.parity]
        cnt = self.base[g % self.parity] + self.step * self.per_unit * (g // self.parity + 1)
        eng.wait_ge(p.h, cnt)

    def wait_done(self, eng):
        for p in self.ps:
            if p.n > 0:
                eng.wait_ge(p.h, p.n)


class KC:

    def __init__(self, nc):
        self.nc = nc
        self.stack = ExitStack()
        allsems = [PSem(self.stack.enter_context(nc.semaphore(f"s{i}"))) for i in range(92)]
        self.phase_sem = allsems[0]
        self.dummy = self.stack.enter_context(nc.sbuf_tensor("rcy_dummy", [1, 4], mybir.dt.float32))
        self.pin_pool = allsems[1:7]
        self.free = {"c": allsems[7:40], "h": allsems[40:65], "s": allsems[65:]}
        self.used = {"c": [], "h": []}
        self.live = []
        self.phase_n = 0
        self.io = {}

    def take_sem(self, cls="c"):
        if cls == "p":
            assert self.pin_pool, "out of pinned sems"
            return self.pin_pool.pop()
        if cls == "s":
            assert self.free["s"], "out of s sems"
            return self.free["s"].pop()
        assert self.free[cls], f"out of {cls} sems in phase"
        p = self.free[cls].pop()
        self.used[cls].append(p)
        return p

    def reg_stream(self, st):
        self.live.append(st)

    def recycle(self, extra_waits=()):
        for cls in ("c", "h"):
            self.free[cls] = self.used[cls] + self.free[cls]
        self.used = {"c": [], "h": []}
        self.live = []

    def sb(self, name, shape, dtype, stack=None):
        return (stack or self.stack).enter_context(self.nc.sbuf_tensor(name, shape, dtype))

    def ps_(self, name, shape, dtype, stack=None):
        return (stack or self.stack).enter_context(self.nc.psum_tensor(name, shape, dtype))

    def wait_barrier(self, eng):
        pass


def emit_ln(kc, x_src, lnT_sb, sh, tag, stk, sync_tail=None, gp_sec=None):
    nc = kc.nc
    lstk = ExitStack()
    sh.xin = kc.sb(f"{tag}_xin", [128, 2 * C], F32, lstk)
    sh.lntok = kc.sb(f"{tag}_lntok", [128, 2 * C], BF16, lstk)
    sh.stats = kc.sb(f"{tag}_stats", [128, 48], F32, lstk)
    sh.mv = kc.sb(f"{tag}_mv", [128, 2], F32, lstk)
    sh.veps = kc.sb(f"{tag}_veps", [128, 1], F32, lstk)
    sh.rstd = kc.sb(f"{tag}_rstd", [128, 1], F32, lstk)
    xld = Stream(kc, parity=2, step=16, cls="h")
    agg = Stream(kc)
    vep = Stream(kc)
    acl = Stream(kc)
    acr = Stream(kc)
    nrmv = Stream(kc)
    trp = Stream(kc)
    evc = Stream(kc)
    ps_tr0 = kc.ps_(f"{tag}_pstr0", [128, 512], BF16, lstk)
    ps_tr1 = kc.ps_(f"{tag}_pstr1", [128, 512], BF16, lstk)
    ps_trs = (ps_tr0, ps_tr1)

    with nc.Block() as blk:
        @blk.sync
        def _(sy):
            kc.wait_barrier(sy)
            for tt in range(16):
                nrmv.wait(sy, tt - 2)
                xld.inc(sy.dma_start(sh.xin[:, (tt % 2) * C:(tt % 2 + 1) * C],
                                     x_src[tt * 128:(tt + 1) * 128, :]))
            if sync_tail is not None:
                sync_tail(sy, evc)

        @blk.gpsimd
        def _(gp):
            if gp_sec is not None:
                gp_sec(gp)

        @blk.vector
        def _(v):
            kc.wait_barrier(v)
            for tt in range(16):
                xld.wait(v, tt)
                xt = sh.xin[:, (tt % 2) * C:(tt % 2 + 1) * C]
                for cch in range(8):
                    v.bn_stats(sh.stats[:, 6 * cch:6 * cch + 6], xt[:, 512 * cch:512 * (cch + 1)])
                agg.inc(v.bn_aggr(sh.mv[:], sh.stats[:]))
                agg.wait(v, tt)
                vep.inc(v.tensor_scalar(sh.veps[:], sh.mv[:, 1:2], EPS, None, ALU.add))
                acr.wait(v, tt)
                trp.wait(v, (tt - 2) * 8 + 7)
                nrmv.inc(v.tensor_scalar(sh.lntok[:, (tt % 2) * C:(tt % 2 + 1) * C], xt,
                                         sh.mv[:, 0:1], sh.rstd[:], ALU.subtract, ALU.mult))

        @blk.scalar
        def _(s):
            kc.wait_barrier(s)
            for tt in range(16):
                vep.wait(s, tt)
                acl.inc(s.activation(sh.rstd[:], sh.veps[:], AF.Ln))
                acl.wait(s, tt)
                acr.inc(s.activation(sh.rstd[:], sh.rstd[:], AF.Exp, scale=-0.5))
                for g in range(8):
                    gg = tt * 8 + g
                    trp.wait(s, gg)
                    dst = lnT_sb.ap().rearrange("p (c t) -> p c t", c=32)[:, g * 4:(g + 1) * 4,
                                                                          tt * 128:(tt + 1) * 128]
                    src = ps_trs[gg % 2].ap().rearrange("p (k f) -> p k f", k=4)
                    evc.inc(s.copy(dst, src))

        @blk.tensor
        def _(t):
            kc.wait_barrier(t)
            for tt in range(16):
                nrmv.wait(t, tt)
                lt = sh.lntok[:, (tt % 2) * C:(tt % 2 + 1) * C]
                for g in range(8):
                    gg = tt * 8 + g
                    evc.wait(t, gg - 2)
                    for k4 in range(4):
                        ii = t.transpose(
                            ps_trs[gg % 2][:, k4 * 128:(k4 + 1) * 128],
                            lt[:, (g * 4 + k4) * 128:(g * 4 + k4 + 1) * 128], sh.ident[:])
                    trp.inc(ii)

    lstk.close()
    return [evc, nrmv, xld]


def emit_out_proj(kc, tag, sh, lhsT_dram, w_dram, resid_dram, out_dram, stk):
    nc = kc.nc
    attnT = kc.sb(f"{tag}_attnT", [128, 32 * 1024], BF16, stk)
    w_sb = kc.sb(f"{tag}_wsb", [128, 2 * 32 * 512], BF16, stk)
    xres = kc.sb(f"{tag}_xres", [128, 2 * 512], F32, stk)
    ostage = kc.sb(f"{tag}_ost", [128, 2 * 512], F32, stk)
    ps_y = kc.ps_(f"{tag}_psy", [128, 1024], F32, stk)

    hld = Stream(kc, parity=1, step=16, cls="h")
    wld = Stream(kc, parity=2, step=16, cls="s")
    xrl = Stream(kc, parity=2, step=16, cls="h")
    pe = Stream(kc)
    ev = Stream(kc)
    st = Stream(kc, parity=2, step=16, cls="h")

    def IT(th, dc, tt):
        return (th * 8 + dc) * 8 + tt

    with nc.Block() as blk:
        @blk.sync
        def _(sy):
            kc.wait_barrier(sy)
            for th in range(2):
                if th == 1:
                    pe.wait(sy, IT(0, 7, 7))
                hld.inc(sy.dma_start(
                    attnT.ap().rearrange("p (c t) -> p c t", c=32),
                    lhsT_dram.rearrange("(c p) t -> p c t", p=128)[:, :, th * 1024:(th + 1) * 1024]))
                for dc in range(8):
                    for tt in range(8):
                        it = IT(th, dc, tt)
                        ev.wait(sy, it - 2)
                        xrl.inc(sy.dma_start(
                            xres[:, (it % 2) * 512:(it % 2 + 1) * 512],
                            resid_dram[th * 1024 + tt * 128:th * 1024 + tt * 128 + 128,
                                       dc * 512:(dc + 1) * 512]))

        @blk.gpsimd
        def _(gp):
            kc.wait_barrier(gp)
            for th in range(2):
                for dc in range(8):
                    i = th * 8 + dc
                    if i >= 2:
                        pe.wait(gp, (i - 1) * 8 - 1)
                    wld.inc(gp.dma_start(
                        w_sb.ap().rearrange("p (b c d) -> p b c d", b=2, c=32)[:, i % 2, :, :],
                        w_dram.rearrange("(c p) d -> p c d", p=128)[:, :, dc * 512:(dc + 1) * 512]))

        @blk.tensor
        def _(t):
            kc.wait_barrier(t)
            for th in range(2):
                hld.wait(t, th)
                for dc in range(8):
                    i = th * 8 + dc
                    wld.wait(t, i)
                    for tt in range(8):
                        it = IT(th, dc, tt)
                        ev.wait(t, it - 2)
                        for cc in range(32):
                            ii = t.matmul(
                                ps_y[:, (it % 2) * 512:(it % 2 + 1) * 512],
                                attnT.ap().rearrange("p (c t) -> p c t", c=32)[:, cc,
                                                                               tt * 128:(tt + 1) * 128],
                                w_sb.ap().rearrange("p (b c d) -> p b c d", b=2, c=32)[:, i % 2, cc, :],
                                start=(cc == 0), stop=(cc == 31))
                        pe.inc(ii)

        @blk.vector
        def _(v):
            kc.wait_barrier(v)
            for th in range(2):
                for dc in range(8):
                    for tt in range(8):
                        it = IT(th, dc, tt)
                        pe.wait(v, it)
                        xrl.wait(v, it)
                        st.wait(v, it - 2)
                        ev.inc(v.tensor_tensor(ostage[:, (it % 2) * 512:(it % 2 + 1) * 512],
                                               ps_y[:, (it % 2) * 512:(it % 2 + 1) * 512],
                                               xres[:, (it % 2) * 512:(it % 2 + 1) * 512], ALU.add))

        @blk.scalar
        def _(s):
            kc.wait_barrier(s)
            for th in range(2):
                for dc in range(8):
                    for tt in range(8):
                        it = IT(th, dc, tt)
                        ev.wait(s, it)
                        row0 = th * 1024 + tt * 128
                        st.inc(s.dma_start(out_dram[row0:row0 + 128, dc * 512:(dc + 1) * 512],
                                           ostage[:, (it % 2) * 512:(it % 2 + 1) * 512]))
            st.wait_done(s)

    return [st, ev, xrl, hld]


def emit_ffn(kc, tag, sh, lnT_sb, resid_dram, out_dram, first_ffn, stk, final=False):
    nc = kc.nc
    wcl = Stream(kc, parity=1, step=16, per_unit=2, cls="s")
    wcp = Stream(kc)
    wce = Stream(kc)
    if first_ffn:
        wstk = ExitStack()
        wq_bf = kc.sb(f"{tag}_wqbf", [RD, C], BF16, wstk)
        keys_bf = kc.sb(f"{tag}_keysbf", [RD, NE], BF16, wstk)
        ps_wc = kc.ps_(f"{tag}_pswc", [128, 64], F32, wstk)
        with nc.Block() as blk:
            @blk.gpsimd
            def _(gp):
                wcl.inc(gp.dma_start(wq_bf[:], kc.io["Wq_raw"]))
                wcl.inc(gp.dma_start(keys_bf[:], kc.io["keysT"]))

            @blk.tensor
            def _(t):
                wcl.wait(t, 0)
                for cc in range(32):
                    wce.wait(t, cc - 1)
                    wcp.inc(t.matmul(ps_wc[:, 0:64], wq_bf[:, cc * 128:(cc + 1) * 128],
                                     keys_bf[:], start=True, stop=True))

            @blk.scalar
            def _(s):
                for cc in range(32):
                    wcp.wait(s, cc)
                    wce.inc(s.copy(sh.wcomb[:, cc * 64:(cc + 1) * 64], ps_wc[:, 0:64]))
        wstk.close()
    AT_sb = kc.sb(f"{tag}_AT", [128, 32 * 512], BF16, stk)
    BT_sb = kc.sb(f"{tag}_BT", [128, 4 * 4096], BF16, stk)
    zg_bf = kc.sb(f"{tag}_zg", [128, 512], BF16, stk)
    zgT_sb = kc.sb(f"{tag}_zgT", [128, 512], BF16, stk)
    gate = kc.sb(f"{tag}_gate", [128, 64], F32, stk)
    m1 = kc.sb(f"{tag}_m1", [128, 8], F32, stk)
    m2 = kc.sb(f"{tag}_m2", [128, 8], F32, stk)
    th1 = kc.sb(f"{tag}_th1", [128, 1], F32, stk)
    th2 = kc.sb(f"{tag}_th2", [128, 1], F32, stk)
    nmax = kc.sb(f"{tag}_nmax", [128, 1], F32, stk)
    s2b = kc.sb(f"{tag}_s2", [128, 64], F32, stk)
    esum = kc.sb(f"{tag}_es", [128, 1], F32, stk)
    erec = kc.sb(f"{tag}_er", [128, 1], F32, stk)
    sc_f = kc.sb(f"{tag}_scf", [128, 64], F32, stk)
    xres = kc.sb(f"{tag}_xres", [128, 2 * 512], F32, stk)
    ostage = kc.sb(f"{tag}_ost", [128, 2 * 512], F32, stk)
    ps_z = kc.ps_(f"{tag}_psz", [128, 1024], F32, stk)
    ps_sc0 = kc.ps_(f"{tag}_pssc0", [128, 64], F32, stk)
    ps_sc1 = kc.ps_(f"{tag}_pssc1", [128, 64], F32, stk)
    ps_scs = (ps_sc0, ps_sc1)
    ps_zt = kc.ps_(f"{tag}_pszt", [128, 512], BF16, stk)
    ps_y = kc.ps_(f"{tag}_psy", [128, 1024], F32, stk)

    wla = Stream(kc, parity=1, step=16, per_unit=2, cls="s")
    pez = Stream(kc)
    scv = Stream(kc)
    vch = Stream(kc)
    ace = Stream(kc)
    zgv = Stream(kc)
    ztr = Stream(kc)
    zev = Stream(kc)
    xrl = Stream(kc, parity=2, step=16, cls="h")
    pey = Stream(kc)
    evy = Stream(kc)
    sty = Stream(kc, parity=2, step=16, cls="h")

    def IT(tt, dc):
        return tt * 8 + dc

    with nc.Block() as blk:
        @blk.gpsimd
        def _(gp):
            wla.inc(gp.dma_start(AT_sb.ap().rearrange("p (c n) -> p c n", c=32),
                                 kc.io["AT"].rearrange("(c p) n -> p c n", p=128)))
            wla.inc(gp.dma_start(BT_sb.ap().rearrange("p (c d) -> p c d", c=4),
                                 kc.io["BT"].rearrange("(c p) d -> p c d", p=128)))

        @blk.tensor
        def _(t):
            wla.wait(t, 0)

            def bpart(pt):
                zev.wait(t, pt)
                for dc in range(8):
                    it = IT(pt, dc)
                    evy.wait(t, it - 2)
                    for nch in range(4):
                        ii = t.matmul(ps_y[:, (it % 2) * 512:(it % 2 + 1) * 512],
                                      zgT_sb[:, nch * 128:(nch + 1) * 128],
                                      BT_sb.ap().rearrange("p (c d) -> p c d", c=4)[:, nch,
                                                                                    dc * 512:(dc + 1) * 512],
                                      start=(nch == 0), stop=(nch == 3))
                    pey.inc(ii)

            for tt in range(16):
                zgv.wait(t, tt - 2)
                for cc in range(32):
                    t.matmul(ps_z[:, (tt % 2) * 512:(tt % 2 + 1) * 512],
                             lnT_sb.ap().rearrange("p (c t) -> p c t", c=32)[:, cc,
                                                                             tt * 128:(tt + 1) * 128],
                             AT_sb.ap().rearrange("p (c n) -> p c n", c=32)[:, cc, :],
                             start=(cc == 0), stop=(cc == 31))
                scv.wait(t, tt - 2)
                for cc in range(32):
                    ii = t.matmul(ps_scs[tt % 2][:],
                                  lnT_sb.ap().rearrange("p (c t) -> p c t", c=32)[:, cc,
                                                                                  tt * 128:(tt + 1) * 128],
                                  sh.wcomb[:, cc * 64:(cc + 1) * 64],
                                  start=(cc == 0), stop=(cc == 31))
                pez.inc(ii)
                if tt >= 1:
                    pt = tt - 1
                    zgv.wait(t, pt)
                    zev.wait(t, pt - 1)
                    for k4 in range(4):
                        ii = t.transpose(ps_zt[:, k4 * 128:(k4 + 1) * 128],
                                         zg_bf[:, k4 * 128:(k4 + 1) * 128],
                                         sh.ident[:])
                    ztr.inc(ii)
                if tt >= 2:
                    bpart(tt - 2)
            pt = 15
            zgv.wait(t, pt)
            zev.wait(t, pt - 1)
            for k4 in range(4):
                ii = t.transpose(ps_zt[:, k4 * 128:(k4 + 1) * 128],
                                 zg_bf[:, k4 * 128:(k4 + 1) * 128],
                                 sh.ident[:])
            ztr.inc(ii)
            bpart(14)
            bpart(15)

        @blk.scalar
        def _(s):
            for tt in range(16):
                vch.wait(s, tt * 9 + 3)
                ace.inc(s.activation(s2b[:], sc_f[:], AF.Exp, bias=nmax[:]))
                if tt >= 2:
                    for dc in range(8):
                        it = IT(tt - 2, dc)
                        evy.wait(s, it)
                        sty.inc(s.dma_start(out_dram[(tt - 2) * 128:(tt - 1) * 128,
                                                     dc * 512:(dc + 1) * 512],
                                            ostage[:, (it % 2) * 512:(it % 2 + 1) * 512]))
            for ft in (14, 15):
                for dc in range(8):
                    it = IT(ft, dc)
                    evy.wait(s, it)
                    sty.inc(s.dma_start(out_dram[ft * 128:(ft + 1) * 128, dc * 512:(dc + 1) * 512],
                                        ostage[:, (it % 2) * 512:(it % 2 + 1) * 512]))
            if final:
                sty.wait_done(s)

        @blk.vector
        def _(v):
            for tt in range(16):
                vb = tt * 9
                pez.wait(v, tt)
                ace.wait(v, tt - 1)
                scv.inc(v.tensor_copy(sc_f[:], ps_scs[tt % 2][:]))
                scf = sc_f[:]
                scv.wait(v, tt)
                vch.inc(v.max(m1[:], scf))
                vch.wait(v, vb + 0)
                vch.inc(v.tensor_reduce(th1[:], m1[:], AX.X, ALU.min))
                vch.inc(v.tensor_reduce(nmax[:], m1[:, 0:1], AX.X, ALU.max, negate=True))
                vch.wait(v, vb + 1)
                v.tensor_scalar(s2b[:], scf, th1[:], None, ALU.is_ge)
                v.scalar_tensor_tensor(s2b[:], s2b[:], -1e9, scf, ALU.mult, ALU.add)
                vch.inc(v.max(m2[:], s2b[:]))
                vch.wait(v, vb + 3)
                vch.inc(v.tensor_reduce(th2[:], m2[:], AX.X, ALU.min))
                vch.wait(v, vb + 4)
                ace.wait(v, tt)
                vch.inc(v.tensor_scalar(sc_f[:], scf, th2[:], None, ALU.is_ge))
                v.tensor_tensor(s2b[:], s2b[:], sc_f[:], ALU.mult)
                vch.inc(v.tensor_reduce(esum[:], s2b[:], AX.X, ALU.add))
                vch.wait(v, vb + 6)
                vch.inc(v.reciprocal(erec[:], esum[:]))
                vch.wait(v, vb + 7)
                vch.inc(v.tensor_scalar(gate[:], s2b[:], erec[:], None, ALU.mult))
                vch.wait(v, vb + 8)
                ztr.wait(v, tt - 1)
                gb = AP(gate.ap().tensor, gate.ap().offset, [[64, 128], [1, 64], [0, 8]])
                zin = ps_z.ap().rearrange("p (b e r) -> p b e r", b=2, r=8)[:, tt % 2, :, :]
                if tt >= 1:
                    ztr.wait(v, tt - 1)
                zout = zg_bf.ap().rearrange("p (e r) -> p e r", r=8)
                zgv.inc(v.tensor_tensor(zout, zin, gb, ALU.mult))
                if tt >= 2:
                    for dc in range(8):
                        it = IT(tt - 2, dc)
                        pey.wait(v, it)
                        xrl.wait(v, it)
                        sty.wait(v, it - 2)
                        evy.inc(v.scalar_tensor_tensor(ostage[:, (it % 2) * 512:(it % 2 + 1) * 512],
                                                       ps_y[:, (it % 2) * 512:(it % 2 + 1) * 512], 2.0,
                                                       xres[:, (it % 2) * 512:(it % 2 + 1) * 512],
                                                       ALU.mult, ALU.add))
                if tt >= 1:
                    pt = tt - 1
                    ztr.wait(v, pt)
                    pey.wait(v, IT(pt - 1, 7))
                    zev.inc(v.tensor_copy(zgT_sb[:], ps_zt[:]))
            for dc in range(8):
                it = IT(14, dc)
                pey.wait(v, it)
                xrl.wait(v, it)
                sty.wait(v, it - 2)
                evy.inc(v.scalar_tensor_tensor(ostage[:, (it % 2) * 512:(it % 2 + 1) * 512],
                                               ps_y[:, (it % 2) * 512:(it % 2 + 1) * 512], 2.0,
                                               xres[:, (it % 2) * 512:(it % 2 + 1) * 512],
                                               ALU.mult, ALU.add))
            ztr.wait(v, 15)
            pey.wait(v, IT(14, 7))
            zev.inc(v.tensor_copy(zgT_sb[:], ps_zt[:]))
            for dc in range(8):
                it = IT(15, dc)
                pey.wait(v, it)
                xrl.wait(v, it)
                sty.wait(v, it - 2)
                evy.inc(v.scalar_tensor_tensor(ostage[:, (it % 2) * 512:(it % 2 + 1) * 512],
                                               ps_y[:, (it % 2) * 512:(it % 2 + 1) * 512], 2.0,
                                               xres[:, (it % 2) * 512:(it % 2 + 1) * 512],
                                               ALU.mult, ALU.add))

        @blk.sync
        def _(sy):
            for tt in range(16):
                for dc in range(8):
                    it = IT(tt, dc)
                    evy.wait(sy, it - 2)
                    xrl.inc(sy.dma_start(xres[:, (it % 2) * 512:(it % 2 + 1) * 512],
                                         resid_dram[tt * 128:(tt + 1) * 128, dc * 512:(dc + 1) * 512]))

    return [sty, evy, xrl]


def build_kernel(detect_races=True):
    nc = bass.Bass("TRN2", target_bir_lowering=False, detect_race_conditions=detect_races)
    kc = KC(nc)

    def din(name, shape, dtype=F32):
        apx = nc.dram_tensor(name, shape, dtype, kind="ExternalInput").ap()
        kc.io[name] = apx
        return apx

    x_core = din("x_core", [TL, C])
    wq_sl = din("wq_sl", [C, HPC * HD])
    wk_sl = din("wk_sl", [C, HPC * HD])
    wv_sl = din("wv_sl", [C, HPC * HD])
    woT = din("woT", [C, C])
    wqkvT = din("wqkvT", [C, PB])
    wprojT = din("wprojT", [C, C])
    din("AT", [C, BOT])
    din("BT", [BOT, C])
    din("Wq_raw", [RD, C])
    din("keysT", [RD, NE])
    cosg = din("cosg", [HD, B * T])
    sing = din("sing", [HD, B * T])
    ident_in = din("ident_in", [128, 128])
    tri_in = din("tri_in", [128, 4 * 512])
    gidx_in = din("gidx_in", [128, 128], U32)

    sk = "ExternalOutput" if _EXPOSE else "Internal"

    def dsc(name, shape, dtype):
        return nc.dram_tensor(name, shape, dtype, kind=sk).ap()

    ag_ln_in_a = nc.dram_tensor("ag_ln_in_a", [C, TL // 2], BF16, kind="Internal").ap()
    ag_ln_in_b = nc.dram_tensor("ag_ln_in_b", [C, TL // 2], BF16, kind="Internal").ap()
    ag_ln_out_a = nc.dram_tensor("ag_ln_out_a", [NCORES * C, TL // 2], BF16, kind="Internal", addr_space="Shared").ap()
    ag_ln_out_b = nc.dram_tensor("ag_ln_out_b", [NCORES * C, TL // 2], BF16, kind="Internal", addr_space="Shared").ap()
    nrm_in = nc.dram_tensor("nrm_in", [1, 2], F32, kind="Internal").ap()
    nrm_out = nc.dram_tensor("nrm_out", [NCORES, 2], F32, kind="Internal").ap()
    attn_ag_in_a = nc.dram_tensor("attn_ag_in_a", [B * T, HPC * HD // 2], BF16, kind="Internal").ap()
    attn_ag_in_b = nc.dram_tensor("attn_ag_in_b", [B * T, HPC * HD // 2], BF16, kind="Internal").ap()
    attn_ag_out_a = nc.dram_tensor("attn_ag_out_a", [NCORES * B * T, HPC * HD // 2], BF16, kind="Internal", addr_space="Shared").ap()
    attn_ag_out_b = nc.dram_tensor("attn_ag_out_b", [NCORES * B * T, HPC * HD // 2], BF16, kind="Internal", addr_space="Shared").ap()
    qT_sc = dsc("qT_sc", [HPC * HD, B * T], BF16)
    kT_sc = dsc("kT_sc", [HPC * HD, B * T], BF16)
    v_sc = dsc("v_sc", [B * T, HPC * HD], BF16)
    attnT_loc = dsc("attnT_loc", [C, TL], BF16)
    x1_sc = dsc("x1_sc", [TL, C], F32)
    x2_sc = dsc("x2_sc", [TL, C], F32)
    x3_sc = dsc("x3_sc", [TL, C], F32)
    qkvb_sc = dsc("qkvb_sc", [TL, PB], BF16)
    attnbT_loc = dsc("attnbT_loc", [C, TL], BF16)
    out_y = nc.dram_tensor("out_y", [TL, C], F32, kind="ExternalOutput").ap()

    class SH:
        pass

    sh = SH()
    sh.ident = kc.sb("ident", [128, 128], BF16)
    sh.ones_c = kc.sb("ones_c", [128, 1], BF16)
    sh.ones_rf = kc.sb("ones_rf", [1, 128], F32)
    sh.negM = kc.sb("negM", [128, 1], F32)
    sh.wcomb = kc.sb("wcomb", [128, 32 * NE], BF16)

    boot = Stream(kc, step=16, cls="h")
    bootg = Stream(kc, step=16, cls="s")
    bootm = Stream(kc)
    with nc.Block() as blk:
        @blk.sync
        def _(sy):
            pass

        @blk.gpsimd
        def _(gp):
            bootg.inc(gp.dma_start(sh.ident[:], ident_in))
            gp.memset(sh.ones_c[:], 1.0)
            bootm.inc(gp.memset(sh.ones_rf[:], 1.0))

    kc.recycle()

    ln1_stk = ExitStack()
    lnT_sb = kc.sb("lnT1", [128, 32 * TL], BF16, ln1_stk)
    shp = Stream(kc, parity=2, step=16, cls="h")
    ag0 = Stream(kc, cls="p")

    def _ln1_sync_tail(sy, evc):
        evc.wait(sy, 63)
        shp.inc(sy.dma_start(ag_ln_in_a.rearrange("(c p) t -> p c t", p=128),
                             lnT_sb.ap().rearrange("p (c t) -> p c t", c=32)[:, :, 0:1024]))
        evc.wait(sy, 127)
        shp.inc(sy.dma_start(ag_ln_in_b.rearrange("(c p) t -> p c t", p=128),
                             lnT_sb.ap().rearrange("p (c t) -> p c t", c=32)[:, :, 1024:2048]))

    def _ln1_gp(gp):
        kc.wait_barrier(gp)
        shp.wait(gp, 0)
        ag0.inc(gp.collective_compute("AllGather", ALU.bypass,
                                      replica_groups=[list(range(NCORES))],
                                      ins=[ag_ln_in_a.opt()], outs=[ag_ln_out_a.opt()]))
        shp.wait(gp, 1)
        ag0.inc(gp.collective_compute("AllGather", ALU.bypass,
                                      replica_groups=[list(range(NCORES))],
                                      ins=[ag_ln_in_b.opt()], outs=[ag_ln_out_b.opt()]))

    emit_ln(kc, x_core, lnT_sb, sh, "ln1", ln1_stk, sync_tail=_ln1_sync_tail, gp_sec=_ln1_gp)

    ln1_stk.close()
    kc.recycle()

    s13 = ExitStack()
    w_qkv = kc.sb("w_qkv", [128, 32 * (3 * HPC * HD)], BF16, s13)
    rhs_sb = kc.sb("rhs_sb", [128, 2 * 32 * 512], BF16, s13)
    cs_sb = kc.sb("cs_sb", [128, 2 * 2 * 512], F32, s13)
    rot_sb = kc.sb("rot_sb", [128, 512], F32, s13)
    t1_sb = kc.sb("t1_sb", [128, 512], F32, s13)
    t2_sb = kc.sb("t2_sb", [128, 512], F32, s13)
    qk_bf = kc.sb("qk_bf", [128, 2 * 512], BF16, s13)
    sq_bf = kc.sb("sq_bf", [128, 2 * 512], BF16, s13)
    ntmp = kc.sb("ntmp", [1, 2], F32, s13)
    runmax = kc.sb("runmax", [1, 2], F32, s13)
    v_bf = kc.sb("v_bf", [128, 2 * 512], BF16, s13)
    v_stage = kc.sb("v_stage", [128, 2 * 512], BF16, s13)
    ps_mm = kc.ps_("ps_mm", [128, 1024], F32, s13)
    ps_n = kc.ps_("ps_n", [1, 1024], F32, s13)
    ps_vt0 = kc.ps_("ps_vt0", [128, 512], BF16, s13)
    ps_vt1 = kc.ps_("ps_vt1", [128, 512], BF16, s13)

    wl3 = Stream(kc, step=16, per_unit=3, cls="s")
    rl3 = Stream(kc, parity=2, step=16, per_unit=3, cls="h")
    pe3 = Stream(kc)
    rpq = Stream(kc)
    rpv = Stream(kc)
    sqr = Stream(kc)
    pnm = Stream(kc)
    nrd = Stream(kc)
    spl = Stream(kc, parity=2, step=16, cls="s")
    vtp = Stream(kc)
    vev = Stream(kc)
    vsp = Stream(kc, parity=2, step=16, cls="h")

    NDJ = 12
    TC_ORDER = [o * 4 + tkl for tkl in (0, 1) for o in range(NCORES)] + \
               [o * 4 + tkl for tkl in (2, 3) for o in range(NCORES)]

    def QI(ti, dj):
        return ti * 8 + dj

    def WI(ti, dj):
        return ti * 4 + (dj - 8)

    with nc.Block() as blk:
        @blk.gpsimd
        def _(gp):
            kc.wait_barrier(gp)
            for wi, wd in enumerate((wq_sl, wk_sl, wv_sl)):
                wl3.inc(gp.dma_start(
                    w_qkv.ap().rearrange("p (c d) -> p c d", c=32)[:, :, wi * 512:(wi + 1) * 512],
                    wd.rearrange("(c p) d -> p c d", p=128)))
            for ti, tc in enumerate(TC_ORDER):
                for dj in range(8):
                    q = QI(ti, dj)
                    rpq.wait(gp, q)
                    dst = qT_sc if dj < 4 else kT_sc
                    jj = dj % 4
                    spl.inc(gp.dma_start(dst[jj * 128:(jj + 1) * 128, tc * 512:(tc + 1) * 512],
                                         qk_bf[:, (q % 2) * 512:(q % 2 + 1) * 512]))

        @blk.sync
        def _(sy):
            kc.wait_barrier(sy)
            ag0.wait(sy, 0)

            def vspill(ti2, dj):
                tc2 = TC_ORDER[ti2]
                w = WI(ti2, dj)
                vev.wait(sy, w)
                vsp.inc(sy.dma_start(
                    v_sc.rearrange("(tc s p) f -> p tc s f", p=128, s=4)[:, tc2, :,
                                                                         (dj - 8) * 128:(dj - 7) * 128],
                    v_stage.ap().rearrange("p (b s f) -> p b s f", b=2, s=4)[:, w % 2, :, :]))

            for ti, tc in enumerate(TC_ORDER):
                o, tkl = tc // 4, tc % 4
                if tkl >= 2:
                    ag0.wait(sy, 1)
                pe3.wait(sy, (ti - 1) * NDJ - 1)
                agt = ag_ln_out_a if tkl < 2 else ag_ln_out_b
                rl3.inc(sy.dma_start(
                    rhs_sb.ap().rearrange("p (b c t) -> p b c t", b=2, c=32)[:, ti % 2, :, :],
                    agt.rearrange("(o c p) t -> p o c t", o=NCORES, p=128)[:, o, :,
                                                                           (tkl % 2) * 512:(tkl % 2 + 1) * 512]))
                rl3.inc(sy.dma_start(
                    cs_sb.ap().rearrange("p (b x t) -> p b x t", b=2, x=2)[:, ti % 2, 0, :],
                    cosg[:, tc * 512:(tc + 1) * 512]))
                rl3.inc(sy.dma_start(
                    cs_sb.ap().rearrange("p (b x t) -> p b x t", b=2, x=2)[:, ti % 2, 1, :],
                    sing[:, tc * 512:(tc + 1) * 512]))
                if ti >= 1:
                    for dj in range(8, NDJ):
                        vspill(ti - 1, dj)
            for dj in range(8, NDJ):
                vspill(NGT - 1, dj)

        @blk.tensor
        def _(t):
            kc.wait_barrier(t)
            wl3.wait(t, 0)
            for ti, tc in enumerate(TC_ORDER):
                rl3.wait(t, ti)
                for dj in range(NDJ):
                    g = ti * NDJ + dj
                    g2 = g - 2
                    if g2 >= 0:
                        dj2 = g2 % NDJ
                        if dj2 < 8:
                            rpq.wait(t, QI(g2 // NDJ, dj2))
                        else:
                            rpv.wait(t, WI(g2 // NDJ, dj2))
                    for cc in range(32):
                        ii = t.matmul(ps_mm[:, (g % 2) * 512:(g % 2 + 1) * 512],
                                      w_qkv.ap().rearrange("p (c d) -> p c d", c=32)[:, cc,
                                                                                     dj * 128:(dj + 1) * 128],
                                      rhs_sb.ap().rearrange("p (b c t) -> p b c t", b=2, c=32)[:, ti % 2, cc, :],
                                      start=(cc == 0), stop=(cc == 31))
                    pe3.inc(ii)
                    if dj < 8:
                        q = QI(ti, dj)
                        if q >= 1:
                            qq = q - 1
                            sqr.wait(t, qq)
                            nrd.wait(t, 2 * (qq - 2) + 1)
                            pnm.inc(t.matmul(ps_n[0:1, (qq % 2) * 512:(qq % 2 + 1) * 512],
                                             sh.ones_c[:], sq_bf[:, (qq % 2) * 512:(qq % 2 + 1) * 512],
                                             start=True, stop=True))
                    else:
                        w = WI(ti, dj)
                        if w >= 1:
                            ww = w - 1
                            rpv.wait(t, ww)
                            vev.wait(t, ww - 2)
                            pvt = ps_vt0 if ww % 2 == 0 else ps_vt1
                            for sub in range(4):
                                ii = t.transpose(pvt[:, sub * 128:(sub + 1) * 128],
                                                 v_bf[:, (ww % 2) * 512 + sub * 128:(ww % 2) * 512 + (sub + 1) * 128],
                                                 sh.ident[:])
                            vtp.inc(ii)
            qq = QI(NGT - 1, 7)
            sqr.wait(t, qq)
            nrd.wait(t, 2 * (qq - 2) + 1)
            pnm.inc(t.matmul(ps_n[0:1, (qq % 2) * 512:(qq % 2 + 1) * 512],
                             sh.ones_c[:], sq_bf[:, (qq % 2) * 512:(qq % 2 + 1) * 512],
                             start=True, stop=True))
            ww = WI(NGT - 1, 11)
            rpv.wait(t, ww)
            vev.wait(t, ww - 2)
            pvt = ps_vt0 if ww % 2 == 0 else ps_vt1
            for sub in range(4):
                ii = t.transpose(pvt[:, sub * 128:(sub + 1) * 128],
                                 v_bf[:, (ww % 2) * 512 + sub * 128:(ww % 2) * 512 + (sub + 1) * 128],
                                 sh.ident[:])
            vtp.inc(ii)

        @blk.vector
        def _(v):
            kc.wait_barrier(v)
            for ti, tc in enumerate(TC_ORDER):
                for dj in range(8):
                    g = ti * NDJ + dj
                    q = QI(ti, dj)
                    pe3.wait(v, g)
                    ps = ps_mm[:, (g % 2) * 512:(g % 2 + 1) * 512]
                    cos = cs_sb.ap().rearrange("p (b x t) -> p b x t", b=2, x=2)[:, ti % 2, 0, :]
                    sin = cs_sb.ap().rearrange("p (b x t) -> p b x t", b=2, x=2)[:, ti % 2, 1, :]
                    v.tensor_copy(rot_sb[0:64, :], ps[64:128, :])
                    v.tensor_copy(rot_sb[64:128, :], ps[0:64, :])
                    v.tensor_tensor(t1_sb[:], ps, cos, ALU.mult)
                    v.tensor_tensor(t2_sb[:], rot_sb[:], sin, ALU.mult)
                    spl.wait(v, q - 2)
                    sqr.wait(v, q - 2)
                    rpq.inc(v.tensor_tensor(qk_bf[:, (q % 2) * 512:(q % 2 + 1) * 512],
                                            t1_sb[:], t2_sb[:], ALU.add))
                    if q >= 1:
                        qq = q - 1
                        col = 0 if (qq % 8) < 4 else 1
                        pnm.wait(v, qq)
                        nrd.inc(v.tensor_reduce(ntmp[0:1, col:col + 1],
                                                ps_n[0:1, (qq % 2) * 512:(qq % 2 + 1) * 512],
                                                AX.X, ALU.max))
                        nrd.wait(v, 2 * qq)
                        if qq == 0:
                            ii = v.tensor_copy(runmax[0:1, 0:1], ntmp[0:1, 0:1])
                        elif qq == 4:
                            ii = v.tensor_copy(runmax[0:1, 1:2], ntmp[0:1, 1:2])
                        else:
                            ii = v.tensor_tensor(runmax[0:1, col:col + 1], runmax[0:1, col:col + 1],
                                                 ntmp[0:1, col:col + 1], ALU.max)
                        nrd.inc(ii)
                for dj in range(8, NDJ):
                    w = WI(ti, dj)
                    if w >= 1:
                        ww = w - 1
                        vtp.wait(v, ww)
                        vsp.wait(v, ww - 2)
                        pvt = ps_vt0 if ww % 2 == 0 else ps_vt1
                        vev.inc(v.tensor_copy(
                            v_stage.ap().rearrange("p (b f) -> p b f", b=2)[:, ww % 2, :], pvt[:]))
            qq = QI(NGT - 1, 7)
            pnm.wait(v, qq)
            nrd.inc(v.tensor_reduce(ntmp[0:1, 1:2], ps_n[0:1, (qq % 2) * 512:(qq % 2 + 1) * 512],
                                    AX.X, ALU.max))
            nrd.wait(v, 2 * qq)
            nrd.inc(v.tensor_tensor(runmax[0:1, 1:2], runmax[0:1, 1:2], ntmp[0:1, 1:2], ALU.max))
            ww = WI(NGT - 1, 11)
            vtp.wait(v, ww)
            vsp.wait(v, ww - 2)
            pvt = ps_vt0 if ww % 2 == 0 else ps_vt1
            vev.inc(v.tensor_copy(v_stage.ap().rearrange("p (b f) -> p b f", b=2)[:, ww % 2, :], pvt[:]))

        @blk.scalar
        def _(s):
            kc.wait_barrier(s)
            for ti, tc in enumerate(TC_ORDER):
                for dj in range(NDJ):
                    g = ti * NDJ + dj
                    if dj < 8:
                        q = QI(ti, dj)
                        rpq.wait(s, q)
                        pnm.wait(s, q - 2)
                        sqr.inc(s.activation(sq_bf[:, (q % 2) * 512:(q % 2 + 1) * 512],
                                             qk_bf[:, (q % 2) * 512:(q % 2 + 1) * 512], AF.Square))
                    else:
                        w = WI(ti, dj)
                        pe3.wait(s, g)
                        vtp.wait(s, w - 2)
                        rpv.inc(s.copy(v_bf[:, (w % 2) * 512:(w % 2 + 1) * 512],
                                       ps_mm[:, (g % 2) * 512:(g % 2 + 1) * 512]))

    nD = Stream(kc, cls="p")
    nA1 = Stream(kc, step=16, cls="s")
    nA2 = Stream(kc, step=16, cls="s")
    nCl = Stream(kc, cls="s")
    nrm8 = kc.sb("nrm8", [8, 2], F32, s13)
    gmax = kc.sb("gmax", [1, 2], F32, s13)
    lnm = kc.sb("lnm", [1, 2], F32, s13)
    msum = kc.sb("msum", [1, 1], F32, s13)
    negs = kc.sb("negs", [1, 1], F32, s13)
    ps_nm = kc.ps_("ps_nm", [128, 64], F32, s13)
    with nc.Block() as blk:
        @blk.gpsimd
        def _(gp):
            nrd.wait_done(gp)
            nA1.inc(gp.dma_start(nrm_in, runmax[0:1, :]))
            nA1.wait(gp, 0)
            nCl.inc(gp.collective_compute("AllGather", ALU.bypass,
                                          replica_groups=[list(range(NCORES))],
                                          ins=[nrm_in.opt()], outs=[nrm_out.opt()]))
            nCl.wait(gp, 0)
            nA2.inc(gp.dma_start(nrm8[:], nrm_out))
            nA2.wait(gp, 0)
            nD.inc(gp.tensor_reduce(gmax[0:1, :], nrm8[:], AX.C, ALU.max))

        @blk.scalar
        def _(s):
            nD.wait(s, 0)
            nD.inc(s.activation(lnm[:], gmax[:], AF.Ln))

        @blk.vector
        def _(v):
            nD.wait(v, 1)
            nD.inc(v.tensor_reduce(msum[:], lnm[:], AX.X, ALU.add))

        @blk.scalar
        def _(s):
            nD.wait(s, 2)
            nD.inc(s.activation(msum[:], msum[:], AF.Exp, scale=0.5))

        @blk.vector
        def _(v):
            nD.wait(v, 3)
            nD.inc(v.tensor_scalar(negs[:], msum[:], -SCALE_A, -1.0, ALU.mult, ALU.add))

        @blk.tensor
        def _(t):
            nD.wait(t, 4)
            nD.inc(t.matmul(ps_nm[:, 0:1], sh.ones_rf[:], negs[:], start=True, stop=True))

        @blk.vector
        def _(v):
            nD.wait(v, 5)
            nD.inc(v.tensor_copy(sh.negM[:], ps_nm[:, 0:1]))

    s13.close()
    kc.recycle()

    p4 = ExitStack()
    tri_sb = kc.sb("tri4", [128, 4 * 512], F32, p4)
    q_sb = kc.sb("q_sb", [128, 2 * 512], BF16, p4)
    k_sb = kc.sb("k_sb", [128, 2 * 4096], BF16, p4)
    v_sb4 = kc.sb("v_sb4", [128, 2 * 4096], BF16, p4)
    p_bf = kc.sb("p_bf", [128, 2 * 1024], BF16, p4)
    lnl = kc.sb("lnl", [1, 512], F32, p4)
    linv = kc.sb("linv", [1, 512], F32, p4)
    lcols = kc.sb("lcols", [128, 4], F32, p4)
    ou_bf = kc.sb("ou_bf", [128, 2 * 512], BF16, p4)
    o_st = kc.sb("o_st", [128, 2 * 512], BF16, p4)
    ps_s4 = kc.ps_("ps_s4", [128, 2048], F32, p4)
    ps_o4 = kc.ps_("ps_o4", [128, 512], F32, p4)
    ps_l4 = kc.ps_("ps_l4", [1, 512], F32, p4)
    ps_lt = kc.ps_("ps_lt", [128, 4], F32, p4)
    ps_t4 = kc.ps_("ps_t4", [128, 512], BF16, p4)

    g_sb = kc.sb("g_sb", [128, 4 * 512], BF16, p4)
    g_st = kc.sb("g_st", [128, 4 * 512], BF16, p4)
    sh.gidx = kc.sb("gidx", [128, 128], U32, p4)
    ag2 = Stream(kc, cls="s")
    trl = Stream(kc, step=16, per_unit=2, cls="h")
    jld = Stream(kc, parity=2, step=16, per_unit=5, cls="h")
    qld = Stream(kc, parity=2, step=16, cls="h")
    pes = Stream(kc)
    msk = Stream(kc)
    exq = Stream(kc)
    pvl = Stream(kc)
    lnl1 = Stream(kc)
    fle = Stream(kc)
    oe1 = Stream(kc)
    ltr = Stream(kc)
    lcp = Stream(kc)
    otr = Stream(kc)
    nrm4 = Stream(kc)
    osp = Stream(kc, parity=2, step=16, cls="h")

    NKT = {}
    NFIRST = {}
    DFIRST = {}
    PFIRST = {}
    n_ctr = 0
    d_ctr = 0
    p_ctr = 0
    for jb in range(16):
        for qb in range(8):
            u = jb * 8 + qb
            NFIRST[u] = n_ctr
            NKT[u] = 4 * (qb + 1)
            DFIRST[u] = d_ctr
            PFIRST[u] = p_ctr
            n_ctr += NKT[u]
            d_ctr += 4
            p_ctr += NKT[u] // 2

    def emit_otr(t, u):
        fle.wait(t, u)
        lcp.wait(t, u - 1)
        for sub in range(4):
            ii = t.transpose(ps_lt[:, sub:sub + 1], linv[0:1, sub * 128:(sub + 1) * 128], sh.ones_rf[0:1, 0:1])
        ltr.inc(ii)
        oe1.wait(t, u)
        nrm4.wait(t, u - 1)
        for sub in range(4):
            ii = t.transpose(ps_t4[:, sub * 128:(sub + 1) * 128],
                             ou_bf[:, (u % 2) * 512 + sub * 128:(u % 2) * 512 + (sub + 1) * 128],
                             sh.ident[:])
        otr.inc(ii)

    def ospill(sy, u):
        jb2, qb2 = u // 8, u % 8
        j2, b2 = jb2 // 4, jb2 % 4
        nrm4.wait(sy, u)
        tgt = attn_ag_in_a if j2 < 2 else attn_ag_in_b
        jj2 = j2 % 2
        HW = HPC * HD // 2
        osp.inc(sy.dma_start(
            AP(tgt.tensor, (qb2 * 2048 + b2 * 512) * HW + jj2 * 128,
               [[HW, 128], [128 * HW, 4], [1, 128]]),
            o_st.ap().rearrange("p (b s f) -> p b s f", b=2, s=4)[:, u % 2, :, :]))

    with nc.Block() as blk:
        @blk.gpsimd
        def _(gp):
            osp.wait(gp, 63)
            ag2.inc(gp.collective_compute("AllGather", ALU.bypass,
                                          replica_groups=[list(range(NCORES))],
                                          ins=[attn_ag_in_a.opt()], outs=[attn_ag_out_a.opt()]))
            osp.wait(gp, 126)
            osp.wait(gp, 127)
            ag2.inc(gp.collective_compute("AllGather", ALU.bypass,
                                          replica_groups=[list(range(NCORES))],
                                          ins=[attn_ag_in_b.opt()], outs=[attn_ag_out_b.opt()]))

        @blk.sync
        def _(sy):
            kc.wait_barrier(sy)
            nD.wait(sy, 6)
            trl.inc(sy.dma_start(tri_sb[:], tri_in))
            trl.inc(sy.dma_start(sh.gidx[:], gidx_in))
            for jb in range(16):
                j, b = jb // 4, jb % 4
                if jb >= 2:
                    u2 = (jb - 2) * 8 + 7
                    pvl.wait(sy, NFIRST[u2] + NKT[u2] - 1)
                jld.inc(sy.dma_start(
                    AP(k_sb.ap().tensor, (jb % 2) * 4096, [[2 * 4096, 128], [512, 8], [1, 512]]),
                    AP(kT_sc.tensor, (j * 128) * (B * T) + b * 512, [[B * T, 128], [2048, 8], [1, 512]])))
                for sub in range(4):
                    jld.inc(sy.dma_start(
                        AP(v_sb4.ap().tensor, (jb % 2) * 4096 + sub * 128,
                           [[2 * 4096, 128], [512, 8], [1, 128]]),
                        AP(v_sc.tensor, (b * 512 + sub * 128) * (HPC * HD) + j * 128,
                           [[HPC * HD, 128], [2048 * HPC * HD, 8], [1, 128]])))
                for qb in range(8):
                    u = jb * 8 + qb
                    if u >= 2:
                        u2 = u - 2
                        pes.wait(sy, NFIRST[u2] + NKT[u2] - 1)
                    qld.inc(sy.dma_start(
                        q_sb[:, (u % 2) * 512:(u % 2 + 1) * 512],
                        qT_sc[j * 128:(j + 1) * 128, qb * 2048 + b * 512:qb * 2048 + b * 512 + 512]))
                    if u >= 2:
                        ospill(sy, u - 2)
            ospill(sy, 126)
            ospill(sy, 127)

        @blk.tensor
        def _(t):
            kc.wait_barrier(t)
            nD.wait(t, 6)
            for jb in range(16):
                jld.wait(t, jb)
                for qb in range(8):
                    u = jb * 8 + qb
                    qld.wait(t, u)
                    L = NKT[u]
                    n0 = NFIRST[u]
                    M = L // 2
                    m0 = PFIRST[u]
                    for m in range(M + 1):
                        if m == 1:
                            oe1.wait(t, u - 1)
                            lnl1.wait(t, u - 1)
                        if m < M:
                            exq.wait(t, m0 + m - 2)
                            for t2 in range(2):
                                n = n0 + 2 * m + t2
                                kt = 2 * m + t2
                                pes.inc(t.matmul(ps_s4[:, ((m % 2) * 2 + t2) * 512:((m % 2) * 2 + t2 + 1) * 512],
                                                 k_sb[:, (jb % 2) * 4096 + kt * 128:(jb % 2) * 4096 + (kt + 1) * 128],
                                                 q_sb[:, (u % 2) * 512:(u % 2 + 1) * 512],
                                                 start=True, stop=True))
                        if m >= 1:
                            exq.wait(t, m0 + m - 1)
                            for t2 in range(2):
                                kt = 2 * (m - 1) + t2
                                n = n0 + kt
                                t.matmul(ps_o4[:],
                                         v_sb4[:, (jb % 2) * 4096 + kt * 128:(jb % 2) * 4096 + (kt + 1) * 128],
                                         p_bf[:, ((m - 1) % 2) * 1024 + t2 * 512:((m - 1) % 2) * 1024 + (t2 + 1) * 512],
                                         start=(kt == 0), stop=(kt == L - 1))
                                pvl.inc(t.matmul(ps_l4[0:1, 0:512],
                                                 sh.ones_c[:],
                                                 p_bf[:, ((m - 1) % 2) * 1024 + t2 * 512:((m - 1) % 2) * 1024 + (t2 + 1) * 512],
                                                 start=(kt == 0), stop=(kt == L - 1)))
                    if u >= 1:
                        emit_otr(t, u - 1)
            emit_otr(t, 127)

        @blk.vector
        def _(v):
            kc.wait_barrier(v)
            trl.wait(v, 0)

            def vfin(u):
                ltr.wait(v, u)
                lcp.inc(v.tensor_copy(lcols[:], ps_lt[:]))
                otr.wait(v, u)
                lcp.wait(v, u)
                osp.wait(v, u - 2)
                for sub in range(4):
                    ii = v.tensor_scalar(o_st[:, (u % 2) * 512 + sub * 128:(u % 2) * 512 + (sub + 1) * 128],
                                         ps_t4[:, sub * 128:(sub + 1) * 128],
                                         lcols[:, sub:sub + 1], None, ALU.mult)
                nrm4.inc(ii)

            def oevac(u):
                pvl.wait(v, NFIRST[u] + NKT[u] - 1)
                otr.wait(v, u - 2)
                oe1.inc(v.tensor_copy(ou_bf[:, (u % 2) * 512:(u % 2 + 1) * 512], ps_o4[:]))

            for jb in range(16):
                for qb in range(8):
                    u = jb * 8 + qb
                    L = NKT[u]
                    n0 = NFIRST[u]
                    if u >= 1:
                        oevac(u - 1)
                    for kt in range(4 * qb, L):
                        n = n0 + kt
                        pes.wait(v, n)
                        dd = kt - 4 * qb
                        mloc = (kt // 2) % 2
                        t2 = kt % 2
                        sl = (mloc * 2 + t2) * 512
                        msk.inc(v.tensor_tensor(ps_s4[:, sl:sl + 512],
                                                ps_s4[:, sl:sl + 512],
                                                tri_sb[:, dd * 512:(dd + 1) * 512], ALU.add))
                    if u >= 1:
                        vfin(u - 1)
            oevac(127)
            vfin(127)

        @blk.scalar
        def _(s):
            kc.wait_barrier(s)
            nD.wait(s, 6)
            for jb in range(16):
                for qb in range(8):
                    u = jb * 8 + qb
                    L = NKT[u]
                    n0 = NFIRST[u]
                    M = L // 2
                    m0 = PFIRST[u]
                    for m in range(M):
                        hi = 2 * m + 1
                        if hi >= 4 * qb:
                            msk.wait(s, DFIRST[u] + (hi - 4 * qb))
                        else:
                            pes.wait(s, n0 + hi)
                        pvl.wait(s, n0 + 2 * (m - 2) + 1)
                        exq.inc(s.activation(p_bf[:, (m % 2) * 1024:(m % 2 + 1) * 1024],
                                             ps_s4[:, (m % 2) * 1024:(m % 2 + 1) * 1024],
                                             AF.Exp, bias=sh.negM[:], scale=SCALE_A))
                    pvl.wait(s, n0 + L - 1)
                    lnl1.inc(s.activation(lnl[0:1, :], ps_l4[0:1, 0:512], AF.Ln))
                    lnl1.wait(s, u)
                    fle.inc(s.activation(linv[0:1, :], lnl[0:1, :], AF.Exp, scale=-1.0))



    p6streams = []
    prev_gev = None
    for half, (agt, agwait) in enumerate(((attn_ag_out_a, 0), (attn_ag_out_b, 1))):
        gix = Stream(kc, parity=2, step=16, cls="s")
        gtp = Stream(kc)
        gev = Stream(kc)
        gsp = Stream(kc, parity=2, step=16, cls="h")
        hb = half * 1024
        pg = prev_gev
        with nc.Block() as blk:
            @blk.gpsimd
            def _(gp):
                osp.wait_done(gp)
                trl.wait(gp, 0)
                ag2.wait(gp, agwait)
                for e in range(128):
                    gtp.wait(gp, e - 2)
                    gix.inc(gp.indirect_dma_start(
                        g_sb[:, hb + (e % 2) * 512:hb + (e % 2) * 512 + 256], None,
                        agt, IndirectOffsetOnAxis(sh.gidx[:, e:e + 1], 0)))

            @blk.tensor
            def _(t):
                nrm4.wait(t, 127)
                if pg is not None:
                    pg.wait(t, 127)
                for e in range(128):
                    gix.wait(t, e)
                    gev.wait(t, e - 1)
                    for sub in range(2):
                        ii = t.transpose(
                            ps_t4[:, sub * 128:(sub + 1) * 128],
                            g_sb[:, hb + (e % 2) * 512 + sub * 128:hb + (e % 2) * 512 + (sub + 1) * 128],
                            sh.ident[:])
                    gtp.inc(ii)

            @blk.scalar
            def _(s):
                for e in range(128):
                    gtp.wait(s, e)
                    gsp.wait(s, e - 2)
                    gev.inc(s.copy(g_st[:, hb + (e % 2) * 512:hb + (e % 2) * 512 + 256], ps_t4[:, 0:256]))

            @blk.sync
            def _(sy):
                for e in range(128):
                    op, tt = e // 16, e % 16
                    gev.wait(sy, e)
                    gsp.inc(sy.dma_start(
                        attnT_loc.rearrange("(c p) t -> p c t", p=128)[:, op * 4 + half * 2:op * 4 + half * 2 + 2,
                                                                       tt * 128:(tt + 1) * 128],
                        bass.AP(g_st.ap().tensor, hb + (e % 2) * 512, [[4 * 512, 128], [128, 2], [1, 128]])))
        p6streams += [gsp, gev]
        prev_gev = gev

    p4.close()
    kc.recycle()

    p7 = ExitStack()
    emit_out_proj(kc, "op_a", sh, attnT_loc, woT, x_core, x1_sc, p7)
    p7.close()
    kc.recycle()

    s89 = ExitStack()
    lnT2 = kc.sb("lnT2", [128, 32 * TL], BF16, s89)
    emit_ln(kc, x1_sc, lnT2, sh, "ln2", s89)
    emit_ffn(kc, "ffn1", sh, lnT2, x1_sc, x2_sc, True, s89)
    s89.close()
    kc.recycle()

    s1011 = ExitStack()
    lnT3 = kc.sb("lnT3", [128, 32 * TL], BF16, s1011)
    emit_ln(kc, x2_sc, lnT3, sh, "ln3", s1011)

    wb_sb = kc.sb("wb_sb", [128, 2 * 32 * 512], BF16, s1011)
    qb_stage = kc.sb("qb_stage", [128, 2 * 512], BF16, s1011)
    ps_qb = kc.ps_("ps_qb", [128, 1024], F32, s1011)
    wlb11 = Stream(kc, parity=2, step=16, cls="s")
    pe11 = Stream(kc)
    ev11 = Stream(kc)
    st11 = Stream(kc, parity=2, step=16, cls="h")
    with nc.Block() as blk:
        @blk.gpsimd
        def _(gp):
            kc.wait_barrier(gp)
            for pc in range(12):
                pe11.wait(gp, (pc - 1) * 16 - 1)
                wlb11.inc(gp.dma_start(
                    wb_sb.ap().rearrange("p (b c d) -> p b c d", b=2, c=32)[:, pc % 2, :, :],
                    wqkvT.rearrange("(c p) d -> p c d", p=128)[:, :, pc * 512:(pc + 1) * 512]))

        @blk.tensor
        def _(t):
            kc.wait_barrier(t)
            for pc in range(12):
                wlb11.wait(t, pc)
                for tt in range(16):
                    it = pc * 16 + tt
                    ev11.wait(t, it - 2)
                    for cc in range(32):
                        ii = t.matmul(ps_qb[:, (it % 2) * 512:(it % 2 + 1) * 512],
                                      lnT3.ap().rearrange("p (c t) -> p c t", c=32)[:, cc,
                                                                                    tt * 128:(tt + 1) * 128],
                                      wb_sb.ap().rearrange("p (b c d) -> p b c d", b=2, c=32)[:, pc % 2, cc, :],
                                      start=(cc == 0), stop=(cc == 31))
                    pe11.inc(ii)

        @blk.scalar
        def _(s):
            kc.wait_barrier(s)
            for pc in range(12):
                for tt in range(16):
                    it = pc * 16 + tt
                    pe11.wait(s, it)
                    st11.wait(s, it - 2)
                    ev11.inc(s.copy(qb_stage[:, (it % 2) * 512:(it % 2 + 1) * 512],
                                    ps_qb[:, (it % 2) * 512:(it % 2 + 1) * 512]))

        @blk.sync
        def _(sy):
            kc.wait_barrier(sy)
            for pc in range(12):
                for tt in range(16):
                    it = pc * 16 + tt
                    ev11.wait(sy, it)
                    st11.inc(sy.dma_start(qkvb_sc[tt * 128:(tt + 1) * 128, pc * 512:(pc + 1) * 512],
                                          qb_stage[:, (it % 2) * 512:(it % 2 + 1) * 512]))

    s1011.close()
    kc.recycle()

    p12 = ExitStack()
    qb_sb = kc.sb("qb_sb", [128, 4 * PB], BF16, p12)
    tmp12 = kc.sb("tmp12", [128, 1024], F32, p12)
    sA = kc.sb("sA", [128, 10 * 32], F32, p12)
    mA = kc.sb("mA", [128, 32], F32, p12)
    eA = kc.sb("eA", [128, 10 * 32], F32, p12)
    pA = kc.sb("pA", [128, 10 * 32], F32, p12)
    sumA = kc.sb("sumA", [128, 32], F32, p12)
    recA = kc.sb("recA", [128, 4 * 32], F32, p12)
    oacc = kc.sb("oacc", [128, C], F32, p12)
    ob_bf = kc.sb("ob_bf", [128, 2 * C], BF16, p12)
    ob_st = kc.sb("ob_st", [128, 2 * 512], BF16, p12)
    ps_tb = kc.ps_("ps_tb", [128, 512], BF16, p12)

    PAIRS = [(i, jb) for i in range(4) for jb in range(i + 1)]
    PIDX = {p: ix for ix, p in enumerate(PAIRS)}

    qbl = Stream(kc, parity=1, step=16, per_unit=4, cls="h")
    v12 = Stream(kc)
    vck = Stream(kc)
    a12 = Stream(kc)
    obc = Stream(kc)
    tbt = Stream(kc)
    ebt = Stream(kc)
    sbb = Stream(kc, parity=2, step=16, cls="h")

    def CK(ut, k):
        return ut * 9 + k

    with nc.Block() as blk:
        @blk.sync
        def _(sy):
            kc.wait_barrier(sy)

            def bspill(w):
                ut2, i2 = w // 4, w % 4
                for grp in range(8):
                    ebt.wait(sy, w * 8 + grp)
                    sbb.inc(sy.dma_start(
                        attnbT_loc.rearrange("(c p) t -> p c t", p=128)[:, grp * 4:(grp + 1) * 4,
                                                                        i2 * 512 + ut2 * 128:i2 * 512 + ut2 * 128 + 128],
                        ob_st.ap().rearrange("p (b s f) -> p b s f", b=2, s=4)[:, (w * 8 + grp) % 2, :, :]))

            for ut in range(4):
                if ut >= 1:
                    for w in range((ut - 1) * 4, ut * 4):
                        bspill(w)
                v12.wait(sy, CK(ut - 1, 8))
                for b in range(4):
                    qbl.inc(sy.dma_start(
                        AP(qb_sb.ap().tensor, b * PB, [[4 * PB, 128], [1, PB]]),
                        qkvb_sc[b * 512 + ut * 128:b * 512 + ut * 128 + 128, :]))
            for w in range(12, 16):
                bspill(w)

        @blk.vector
        def _(v):
            kc.wait_barrier(v)

            def edge(ii):
                vck.inc(ii)
                vck.wait(v, vck.issued - 1)
                return ii

            for ut in range(4):
                qbl.wait(v, ut)
                for (i, jb) in PAIRS:
                    px = PIDX[(i, jb)]
                    for gq in range(4):
                        v.tensor_tensor(
                            tmp12[:],
                            AP(qb_sb.ap().tensor, i * PB + gq * 2 * 768,
                               [[4 * PB, 128], [768, 2], [128, 4], [1, 128]]),
                            AP(qb_sb.ap().tensor, jb * PB + gq * 2 * 768 + 512,
                               [[4 * PB, 128], [768, 2], [0, 4], [1, 128]]),
                            ALU.mult)
                        ii = v.tensor_reduce(
                            AP(sA.ap().tensor, px * 32 + gq * 8,
                               [[10 * 32, 128], [4, 2], [1, 4]]),
                            tmp12.ap().rearrange("p (g q f) -> p g q f", g=2, q=4),
                            AX.X, ALU.add)
                v12.inc(ii)
                v12.wait(v, CK(ut, 0))
                for i in range(4):
                    msl = mA[:]
                    edge(v.tensor_copy(msl, sA[:, PIDX[(i, 0)] * 32:PIDX[(i, 0)] * 32 + 32]))
                    for jb in range(1, i + 1):
                        edge(v.tensor_tensor(msl, msl, sA[:, PIDX[(i, jb)] * 32:PIDX[(i, jb)] * 32 + 32],
                                             ALU.max))
                    for jb in range(i + 1):
                        px = PIDX[(i, jb)]
                        if i == 3 and jb == 3:
                            ii = v.tensor_tensor(eA[:, px * 32:px * 32 + 32],
                                                 sA[:, px * 32:px * 32 + 32], msl, ALU.subtract)
                        else:
                            edge(v.tensor_tensor(eA[:, px * 32:px * 32 + 32],
                                                 sA[:, px * 32:px * 32 + 32], msl, ALU.subtract))
                v12.inc(ii)
                a12.wait(v, ut)
                for i in range(4):
                    ssl = sumA[:]
                    edge(v.tensor_copy(ssl, pA[:, PIDX[(i, 0)] * 32:PIDX[(i, 0)] * 32 + 32]))
                    for jb in range(1, i + 1):
                        edge(v.tensor_tensor(ssl, ssl, pA[:, PIDX[(i, jb)] * 32:PIDX[(i, jb)] * 32 + 32],
                                             ALU.add))
                    if i == 3:
                        ii = v.tensor_copy(recA[:, i * 32:(i + 1) * 32], ssl)
                    else:
                        edge(v.tensor_copy(recA[:, i * 32:(i + 1) * 32], ssl))
                v12.inc(ii)
                v12.wait(v, CK(ut, 2))
                ii = v.reciprocal(recA[:], recA[:])
                v12.inc(ii)
                v12.wait(v, CK(ut, 3))
                for ix, (i, jb) in enumerate(PAIRS):
                    px = PIDX[(i, jb)]
                    if ix == len(PAIRS) - 1:
                        ii = v.tensor_tensor(pA[:, px * 32:px * 32 + 32], pA[:, px * 32:px * 32 + 32],
                                             recA[:, i * 32:(i + 1) * 32], ALU.mult)
                    else:
                        edge(v.tensor_tensor(pA[:, px * 32:px * 32 + 32], pA[:, px * 32:px * 32 + 32],
                                             recA[:, i * 32:(i + 1) * 32], ALU.mult))
                v12.inc(ii)
                v12.wait(v, CK(ut, 4))
                for i in range(4):
                    w = ut * 4 + i
                    obc.wait(v, w - 1)
                    for jb in range(i + 1):
                        px = PIDX[(i, jb)]
                        for gq in range(4):
                            pap = AP(pA.ap().tensor, px * 32 + gq * 8,
                                     [[10 * 32, 128], [4, 2], [1, 4], [0, 128]])
                            vap = AP(qb_sb.ap().tensor, jb * PB + gq * 2 * 768 + 640,
                                     [[4 * PB, 128], [768, 2], [0, 4], [1, 128]])
                            oap = AP(oacc.ap().tensor, gq * 1024,
                                     [[C, 128], [512, 2], [128, 4], [1, 128]])
                            if jb == 0:
                                ii = v.tensor_tensor(oap, pap, vap, ALU.mult)
                            else:
                                v.tensor_tensor(tmp12[:], pap, vap, ALU.mult)
                                ii = v.tensor_tensor(
                                    oap, oap,
                                    tmp12.ap().rearrange("p (g q f) -> p g q f", g=2, q=4),
                                    ALU.add)
                    v12.inc(ii)

        @blk.scalar
        def _(s):
            kc.wait_barrier(s)
            for ut in range(4):
                v12.wait(s, CK(ut, 1))
                a12.inc(s.activation(pA[:], eA[:], AF.Exp, scale=SCALE_B))
                for i in range(4):
                    w = ut * 4 + i
                    v12.wait(s, CK(ut, 5 + i))
                    tbt.wait(s, (w - 2) * 8 + 7)
                    obc.inc(s.copy(ob_bf[:, (w % 2) * C:(w % 2 + 1) * C], oacc[:]))
                    for grp in range(8):
                        tbt.wait(s, w * 8 + grp)
                        sbb.wait(s, w * 8 + grp - 2)
                        ebt.inc(s.copy(ob_st[:, ((w * 8 + grp) % 2) * 512:((w * 8 + grp) % 2 + 1) * 512],
                                       ps_tb[:]))

        @blk.tensor
        def _(t):
            kc.wait_barrier(t)
            for w in range(16):
                obc.wait(t, w)
                for grp in range(8):
                    ebt.wait(t, w * 8 + grp - 1)
                    for k4 in range(4):
                        cc = grp * 4 + k4
                        ii = t.transpose(ps_tb[:, k4 * 128:(k4 + 1) * 128],
                                         ob_bf[:, (w % 2) * C + cc * 128:(w % 2) * C + (cc + 1) * 128],
                                         sh.ident[:])
                    tbt.inc(ii)

    p12.close()
    kc.recycle()

    p13 = ExitStack()
    emit_out_proj(kc, "op_b", sh, attnbT_loc, wprojT, x2_sc, x3_sc, p13)
    p13.close()
    kc.recycle()

    s1415 = ExitStack()
    lnT4 = kc.sb("lnT4", [128, 32 * TL], BF16, s1415)
    emit_ln(kc, x3_sc, lnT4, sh, "ln4", s1415)
    emit_ffn(kc, "ffn2", sh, lnT4, x3_sc, out_y, False, s1415, final=True)
    s1415.close()

    kc.stack.close()
    return nc


_ROPE_CACHE = {}


def _host_tables():
    if "t" in _ROPE_CACHE:
        return _ROPE_CACHE["t"]
    pos = np.empty(B * T, dtype=np.float64)
    for o in range(NCORES):
        for tl in range(TL):
            pos[o * TL + tl] = 512 * o + (tl % 512)
    thetas = 1.0 / 10000.0 ** (np.arange(0, HD, 2, dtype=np.float32).astype(np.float64) / HD)
    pt = np.outer(pos, thetas)
    pt = np.concatenate([pt, pt], axis=-1)
    cosg = np.cos(pt).astype(np.float32).T.copy()
    sinf = np.sin(pt).astype(np.float32).T
    sing = sinf.copy()
    sing[:64, :] *= -1.0
    sing = sing.copy()
    tri = np.zeros((128, 4 * 512), dtype=np.float32)
    for d in range(4):
        kk = np.arange(128)[:, None]
        uu = np.arange(512)[None, :]
        tri[:, d * 512:(d + 1) * 512] = np.where(d * 128 + kk <= uu, 0.0, -1e30)
    ident = np.eye(128, dtype=np.float32)
    _ROPE_CACHE["t"] = (cosg, sing, tri, ident)
    return _ROPE_CACHE["t"]


_NC_CACHE = {}
_LAST_RES = None


def kernel(L0_attn_a_x, L0_attn_a_wq, L0_attn_a_wk, L0_attn_a_wv, L0_attn_a_wo,
           L0_attn_b_weight_qkv, L0_attn_b_weight_proj,
           L0_ffn_A_w, L0_ffn_B_w, L0_ffn_Wq_w, L0_ffn_keys):
    x = np.asarray(L0_attn_a_x, dtype=np.float32)
    cosg, sing, tri, ident = _host_tables()
    wqT = np.ascontiguousarray(np.asarray(L0_attn_a_wq, np.float32).T)
    wkT = np.ascontiguousarray(np.asarray(L0_attn_a_wk, np.float32).T)
    wvT = np.ascontiguousarray(np.asarray(L0_attn_a_wv, np.float32).T)
    woT = np.ascontiguousarray(np.asarray(L0_attn_a_wo, np.float32).T)
    wqkvT = np.ascontiguousarray(np.asarray(L0_attn_b_weight_qkv, np.float32).T)
    wprojT = np.ascontiguousarray(np.asarray(L0_attn_b_weight_proj, np.float32).T)
    AT = np.ascontiguousarray(np.asarray(L0_ffn_A_w, np.float32).T)
    BT = np.ascontiguousarray(np.asarray(L0_ffn_B_w, np.float32).T)
    Wq_raw = np.ascontiguousarray(np.asarray(L0_ffn_Wq_w, np.float32))
    keysT = np.ascontiguousarray(np.asarray(L0_ffn_keys, np.float32).T)

    if "nc" not in _NC_CACHE:
        _NC_CACHE["nc"] = build_kernel()
    nc = _NC_CACHE["nc"]

    in_maps = []
    for r in range(NCORES):
        x_core = np.ascontiguousarray(x[:, 512 * r:512 * (r + 1), :].reshape(TL, C))
        gidx = np.empty((128, 128), dtype=np.uint32)
        for op in range(NCORES):
            for tt in range(16):
                gidx[:, op * 16 + tt] = op * (B * T) + 2048 * r + tt * 128 + np.arange(128)
        in_maps.append(dict(
            x_core=x_core,
            wq_sl=np.ascontiguousarray(wqT[:, 512 * r:512 * (r + 1)]),
            wk_sl=np.ascontiguousarray(wkT[:, 512 * r:512 * (r + 1)]),
            wv_sl=np.ascontiguousarray(wvT[:, 512 * r:512 * (r + 1)]),
            woT=woT, wqkvT=wqkvT, wprojT=wprojT,
            AT=AT, BT=BT, Wq_raw=Wq_raw, keysT=keysT,
            cosg=cosg, sing=sing, ident_in=ident, tri_in=tri, gidx_in=gidx,
        ))

    res = run_bass_kernel_spmd(nc, in_maps, core_ids=list(range(NCORES)))
    global _LAST_RES
    _LAST_RES = res
    out = np.empty((B, T, C), dtype=np.float32)
    for r in range(NCORES):
        out[:, 512 * r:512 * (r + 1), :] = res.results[r]["out_y"].reshape(B, 512, C)
    return out


# version 18
# speedup vs baseline: 1.0074x; 1.0074x over previous
import numpy as np
import ml_dtypes
import concourse.bass as bass
import concourse.mybir as mybir
from concourse.bass import IndirectOffsetOnAxis
from concourse.bass_utils import run_bass_kernel_spmd
from contextlib import ExitStack

F32 = mybir.dt.float32
BF16 = mybir.dt.bfloat16
U32 = mybir.dt.uint32
AF = mybir.ActivationFunctionType
ALU = mybir.AluOpType
AX = mybir.AxisListType
AP = bass.AP

NCORES = 8
B, T, C = 4, 4096, 4096
TL = 2048
HD = 128
HPC = 4
NGT = 32
SCALE_A = float(1.0 / np.sqrt(HD))
SCALE_B = float(1.0 / np.sqrt(HD))
PB = 6144
NE, RANK, RD = 64, 8, 16
BOT = NE * RANK
EPS = 1e-5

_EXPOSE = False


class PSem:
    __slots__ = ("h", "n")

    def __init__(self, h):
        self.h = h
        self.n = 0


class Stream:
    def __init__(self, kc, parity=1, step=1, per_unit=1, cls="c"):
        self.ps = [kc.take_sem(cls) for _ in range(parity)]
        if cls != "p":
            kc.reg_stream(self)
        self.base = [p.n for p in self.ps]
        self.parity = parity
        self.step = step
        self.per_unit = per_unit
        self.issued = 0

    def inc(self, instr):
        g = self.issued // self.per_unit
        p = self.ps[g % self.parity]
        instr.then_inc(p.h, self.step)
        p.n += self.step
        self.issued += 1

    def wait(self, eng, g):
        if g < 0:
            return
        p = self.ps[g % self.parity]
        cnt = self.base[g % self.parity] + self.step * self.per_unit * (g // self.parity + 1)
        eng.wait_ge(p.h, cnt)

    def wait_done(self, eng):
        for p in self.ps:
            if p.n > 0:
                eng.wait_ge(p.h, p.n)


class KC:

    def __init__(self, nc):
        self.nc = nc
        self.stack = ExitStack()
        allsems = [PSem(self.stack.enter_context(nc.semaphore(f"s{i}"))) for i in range(92)]
        self.phase_sem = allsems[0]
        self.dummy = self.stack.enter_context(nc.sbuf_tensor("rcy_dummy", [1, 4], mybir.dt.float32))
        self.pin_pool = allsems[1:7]
        self.free = {"c": allsems[7:40], "h": allsems[40:65], "s": allsems[65:]}
        self.used = {"c": [], "h": []}
        self.live = []
        self.phase_n = 0
        self.io = {}

    def take_sem(self, cls="c"):
        if cls == "p":
            assert self.pin_pool, "out of pinned sems"
            return self.pin_pool.pop()
        if cls == "s":
            assert self.free["s"], "out of s sems"
            return self.free["s"].pop()
        assert self.free[cls], f"out of {cls} sems in phase"
        p = self.free[cls].pop()
        self.used[cls].append(p)
        return p

    def reg_stream(self, st):
        self.live.append(st)

    def recycle(self, extra_waits=()):
        for cls in ("c", "h"):
            self.free[cls] = self.used[cls] + self.free[cls]
        self.used = {"c": [], "h": []}
        self.live = []

    def sb(self, name, shape, dtype, stack=None):
        return (stack or self.stack).enter_context(self.nc.sbuf_tensor(name, shape, dtype))

    def ps_(self, name, shape, dtype, stack=None):
        return (stack or self.stack).enter_context(self.nc.psum_tensor(name, shape, dtype))

    def wait_barrier(self, eng):
        pass


def emit_ln(kc, x_src, lnT_sb, sh, tag, stk, sync_tail=None, gp_sec=None):
    nc = kc.nc
    lstk = ExitStack()
    sh.xin = kc.sb(f"{tag}_xin", [128, 2 * C], F32, lstk)
    sh.lntok = kc.sb(f"{tag}_lntok", [128, 2 * C], BF16, lstk)
    sh.stats = kc.sb(f"{tag}_stats", [128, 48], F32, lstk)
    sh.mv = kc.sb(f"{tag}_mv", [128, 2], F32, lstk)
    sh.veps = kc.sb(f"{tag}_veps", [128, 1], F32, lstk)
    sh.rstd = kc.sb(f"{tag}_rstd", [128, 1], F32, lstk)
    xld = Stream(kc, parity=2, step=16, cls="h")
    agg = Stream(kc)
    vep = Stream(kc)
    acl = Stream(kc)
    acr = Stream(kc)
    nrmv = Stream(kc)
    trp = Stream(kc)
    evc = Stream(kc)
    ps_tr0 = kc.ps_(f"{tag}_pstr0", [128, 512], BF16, lstk)
    ps_tr1 = kc.ps_(f"{tag}_pstr1", [128, 512], BF16, lstk)
    ps_trs = (ps_tr0, ps_tr1)

    with nc.Block() as blk:
        @blk.sync
        def _(sy):
            kc.wait_barrier(sy)
            for tt in range(16):
                nrmv.wait(sy, tt - 2)
                xld.inc(sy.dma_start(sh.xin[:, (tt % 2) * C:(tt % 2 + 1) * C],
                                     x_src[tt * 128:(tt + 1) * 128, :]))
            if sync_tail is not None:
                sync_tail(sy, evc)

        @blk.gpsimd
        def _(gp):
            if gp_sec is not None:
                gp_sec(gp)

        @blk.vector
        def _(v):
            kc.wait_barrier(v)
            for tt in range(16):
                xld.wait(v, tt)
                xt = sh.xin[:, (tt % 2) * C:(tt % 2 + 1) * C]
                for cch in range(8):
                    v.bn_stats(sh.stats[:, 6 * cch:6 * cch + 6], xt[:, 512 * cch:512 * (cch + 1)])
                agg.inc(v.bn_aggr(sh.mv[:], sh.stats[:]))
                agg.wait(v, tt)
                vep.inc(v.tensor_scalar(sh.veps[:], sh.mv[:, 1:2], EPS, None, ALU.add))
                acr.wait(v, tt)
                trp.wait(v, (tt - 2) * 8 + 7)
                nrmv.inc(v.tensor_scalar(sh.lntok[:, (tt % 2) * C:(tt % 2 + 1) * C], xt,
                                         sh.mv[:, 0:1], sh.rstd[:], ALU.subtract, ALU.mult))

        @blk.scalar
        def _(s):
            kc.wait_barrier(s)
            for tt in range(16):
                vep.wait(s, tt)
                acl.inc(s.activation(sh.rstd[:], sh.veps[:], AF.Ln))
                acl.wait(s, tt)
                acr.inc(s.activation(sh.rstd[:], sh.rstd[:], AF.Exp, scale=-0.5))
                for g in range(8):
                    gg = tt * 8 + g
                    trp.wait(s, gg)
                    dst = lnT_sb.ap().rearrange("p (c t) -> p c t", c=32)[:, g * 4:(g + 1) * 4,
                                                                          tt * 128:(tt + 1) * 128]
                    src = ps_trs[gg % 2].ap().rearrange("p (k f) -> p k f", k=4)
                    evc.inc(s.copy(dst, src))

        @blk.tensor
        def _(t):
            kc.wait_barrier(t)
            for tt in range(16):
                nrmv.wait(t, tt)
                lt = sh.lntok[:, (tt % 2) * C:(tt % 2 + 1) * C]
                for g in range(8):
                    gg = tt * 8 + g
                    evc.wait(t, gg - 2)
                    for k4 in range(4):
                        ii = t.transpose(
                            ps_trs[gg % 2][:, k4 * 128:(k4 + 1) * 128],
                            lt[:, (g * 4 + k4) * 128:(g * 4 + k4 + 1) * 128], sh.ident[:])
                    trp.inc(ii)

    lstk.close()
    return [evc, nrmv, xld]


def emit_out_proj(kc, tag, sh, lhsT_dram, w_dram, resid_dram, out_dram, stk):
    nc = kc.nc
    attnT = kc.sb(f"{tag}_attnT", [128, 32 * 1024], BF16, stk)
    w_sb = kc.sb(f"{tag}_wsb", [128, 2 * 32 * 512], BF16, stk)
    xres = kc.sb(f"{tag}_xres", [128, 2 * 512], F32, stk)
    ostage = kc.sb(f"{tag}_ost", [128, 2 * 512], F32, stk)
    ps_y = kc.ps_(f"{tag}_psy", [128, 1024], F32, stk)

    hld = Stream(kc, parity=1, step=16, cls="h")
    wld = Stream(kc, parity=2, step=16, cls="s")
    xrl = Stream(kc, parity=2, step=16, cls="h")
    pe = Stream(kc)
    ev = Stream(kc)
    st = Stream(kc, parity=2, step=16, cls="h")

    def IT(th, dc, tt):
        return (th * 8 + dc) * 8 + tt

    with nc.Block() as blk:
        @blk.sync
        def _(sy):
            kc.wait_barrier(sy)
            for th in range(2):
                if th == 1:
                    pe.wait(sy, IT(0, 7, 7))
                hld.inc(sy.dma_start(
                    attnT.ap().rearrange("p (c t) -> p c t", c=32),
                    lhsT_dram.rearrange("(c p) t -> p c t", p=128)[:, :, th * 1024:(th + 1) * 1024]))
                for dc in range(8):
                    for tt in range(8):
                        it = IT(th, dc, tt)
                        ev.wait(sy, it - 2)
                        xrl.inc(sy.dma_start(
                            xres[:, (it % 2) * 512:(it % 2 + 1) * 512],
                            resid_dram[th * 1024 + tt * 128:th * 1024 + tt * 128 + 128,
                                       dc * 512:(dc + 1) * 512]))

        @blk.gpsimd
        def _(gp):
            kc.wait_barrier(gp)
            for th in range(2):
                for dc in range(8):
                    i = th * 8 + dc
                    if i >= 2:
                        pe.wait(gp, (i - 1) * 8 - 1)
                    wld.inc(gp.dma_start(
                        w_sb.ap().rearrange("p (b c d) -> p b c d", b=2, c=32)[:, i % 2, :, :],
                        w_dram.rearrange("(c p) d -> p c d", p=128)[:, :, dc * 512:(dc + 1) * 512]))

        @blk.tensor
        def _(t):
            kc.wait_barrier(t)
            for th in range(2):
                hld.wait(t, th)
                for dc in range(8):
                    i = th * 8 + dc
                    wld.wait(t, i)
                    for tt in range(8):
                        it = IT(th, dc, tt)
                        ev.wait(t, it - 2)
                        for cc in range(32):
                            ii = t.matmul(
                                ps_y[:, (it % 2) * 512:(it % 2 + 1) * 512],
                                attnT.ap().rearrange("p (c t) -> p c t", c=32)[:, cc,
                                                                               tt * 128:(tt + 1) * 128],
                                w_sb.ap().rearrange("p (b c d) -> p b c d", b=2, c=32)[:, i % 2, cc, :],
                                start=(cc == 0), stop=(cc == 31))
                        pe.inc(ii)

        @blk.vector
        def _(v):
            kc.wait_barrier(v)
            for th in range(2):
                for dc in range(8):
                    for tt in range(8):
                        it = IT(th, dc, tt)
                        pe.wait(v, it)
                        xrl.wait(v, it)
                        st.wait(v, it - 2)
                        ev.inc(v.tensor_tensor(ostage[:, (it % 2) * 512:(it % 2 + 1) * 512],
                                               ps_y[:, (it % 2) * 512:(it % 2 + 1) * 512],
                                               xres[:, (it % 2) * 512:(it % 2 + 1) * 512], ALU.add))

        @blk.scalar
        def _(s):
            kc.wait_barrier(s)
            for th in range(2):
                for dc in range(8):
                    for tt in range(8):
                        it = IT(th, dc, tt)
                        ev.wait(s, it)
                        row0 = th * 1024 + tt * 128
                        st.inc(s.dma_start(out_dram[row0:row0 + 128, dc * 512:(dc + 1) * 512],
                                           ostage[:, (it % 2) * 512:(it % 2 + 1) * 512]))
            st.wait_done(s)

    return [st, ev, xrl, hld]


def emit_ffn(kc, tag, sh, lnT_sb, resid_dram, out_dram, first_ffn, stk, final=False):
    nc = kc.nc
    wcl = Stream(kc, parity=1, step=16, per_unit=2, cls="s")
    wcp = Stream(kc)
    wce = Stream(kc)
    if first_ffn:
        wstk = ExitStack()
        wq_bf = kc.sb(f"{tag}_wqbf", [RD, C], BF16, wstk)
        keys_bf = kc.sb(f"{tag}_keysbf", [RD, NE], BF16, wstk)
        ps_wc = kc.ps_(f"{tag}_pswc", [128, 64], F32, wstk)
        with nc.Block() as blk:
            @blk.gpsimd
            def _(gp):
                wcl.inc(gp.dma_start(wq_bf[:], kc.io["Wq_raw"]))
                wcl.inc(gp.dma_start(keys_bf[:], kc.io["keysT"]))

            @blk.tensor
            def _(t):
                wcl.wait(t, 0)
                for cc in range(32):
                    wce.wait(t, cc - 1)
                    wcp.inc(t.matmul(ps_wc[:, 0:64], wq_bf[:, cc * 128:(cc + 1) * 128],
                                     keys_bf[:], start=True, stop=True))

            @blk.scalar
            def _(s):
                for cc in range(32):
                    wcp.wait(s, cc)
                    wce.inc(s.copy(sh.wcomb[:, cc * 64:(cc + 1) * 64], ps_wc[:, 0:64]))
        wstk.close()
    AT_sb = kc.sb(f"{tag}_AT", [128, 32 * 512], BF16, stk)
    BT_sb = kc.sb(f"{tag}_BT", [128, 4 * 4096], BF16, stk)
    zg_bf = kc.sb(f"{tag}_zg", [128, 512], BF16, stk)
    zgT_sb = kc.sb(f"{tag}_zgT", [128, 512], BF16, stk)
    gate = kc.sb(f"{tag}_gate", [128, 64], F32, stk)
    m1 = kc.sb(f"{tag}_m1", [128, 8], F32, stk)
    m2 = kc.sb(f"{tag}_m2", [128, 8], F32, stk)
    th1 = kc.sb(f"{tag}_th1", [128, 1], F32, stk)
    th2 = kc.sb(f"{tag}_th2", [128, 1], F32, stk)
    nmax = kc.sb(f"{tag}_nmax", [128, 1], F32, stk)
    s2b = kc.sb(f"{tag}_s2", [128, 64], F32, stk)
    esum = kc.sb(f"{tag}_es", [128, 1], F32, stk)
    erec = kc.sb(f"{tag}_er", [128, 1], F32, stk)
    sc_f = kc.sb(f"{tag}_scf", [128, 64], F32, stk)
    xres = kc.sb(f"{tag}_xres", [128, 2 * 512], F32, stk)
    ostage = kc.sb(f"{tag}_ost", [128, 2 * 512], F32, stk)
    ps_z = kc.ps_(f"{tag}_psz", [128, 1024], F32, stk)
    ps_sc0 = kc.ps_(f"{tag}_pssc0", [128, 64], F32, stk)
    ps_sc1 = kc.ps_(f"{tag}_pssc1", [128, 64], F32, stk)
    ps_scs = (ps_sc0, ps_sc1)
    ps_zt = kc.ps_(f"{tag}_pszt", [128, 512], BF16, stk)
    ps_y = kc.ps_(f"{tag}_psy", [128, 1024], F32, stk)

    wla = Stream(kc, parity=1, step=16, per_unit=2, cls="s")
    pez = Stream(kc)
    scv = Stream(kc)
    vch = Stream(kc)
    ace = Stream(kc)
    zgv = Stream(kc)
    ztr = Stream(kc)
    zev = Stream(kc)
    xrl = Stream(kc, parity=2, step=16, cls="h")
    pey = Stream(kc)
    evy = Stream(kc)
    sty = Stream(kc, parity=2, step=16, cls="h")

    def IT(tt, dc):
        return tt * 8 + dc

    with nc.Block() as blk:
        @blk.gpsimd
        def _(gp):
            wla.inc(gp.dma_start(AT_sb.ap().rearrange("p (c n) -> p c n", c=32),
                                 kc.io["AT"].rearrange("(c p) n -> p c n", p=128)))
            wla.inc(gp.dma_start(BT_sb.ap().rearrange("p (c d) -> p c d", c=4),
                                 kc.io["BT"].rearrange("(c p) d -> p c d", p=128)))

        @blk.tensor
        def _(t):
            wla.wait(t, 0)

            def bpart(pt):
                zev.wait(t, pt)
                for dc in range(8):
                    it = IT(pt, dc)
                    evy.wait(t, it - 2)
                    for nch in range(4):
                        ii = t.matmul(ps_y[:, (it % 2) * 512:(it % 2 + 1) * 512],
                                      zgT_sb[:, nch * 128:(nch + 1) * 128],
                                      BT_sb.ap().rearrange("p (c d) -> p c d", c=4)[:, nch,
                                                                                    dc * 512:(dc + 1) * 512],
                                      start=(nch == 0), stop=(nch == 3))
                    pey.inc(ii)

            for tt in range(16):
                zgv.wait(t, tt - 2)
                for cc in range(32):
                    t.matmul(ps_z[:, (tt % 2) * 512:(tt % 2 + 1) * 512],
                             lnT_sb.ap().rearrange("p (c t) -> p c t", c=32)[:, cc,
                                                                             tt * 128:(tt + 1) * 128],
                             AT_sb.ap().rearrange("p (c n) -> p c n", c=32)[:, cc, :],
                             start=(cc == 0), stop=(cc == 31))
                scv.wait(t, tt - 2)
                for cc in range(32):
                    ii = t.matmul(ps_scs[tt % 2][:],
                                  lnT_sb.ap().rearrange("p (c t) -> p c t", c=32)[:, cc,
                                                                                  tt * 128:(tt + 1) * 128],
                                  sh.wcomb[:, cc * 64:(cc + 1) * 64],
                                  start=(cc == 0), stop=(cc == 31))
                pez.inc(ii)
                if tt >= 1:
                    pt = tt - 1
                    zgv.wait(t, pt)
                    zev.wait(t, pt - 1)
                    for k4 in range(4):
                        ii = t.transpose(ps_zt[:, k4 * 128:(k4 + 1) * 128],
                                         zg_bf[:, k4 * 128:(k4 + 1) * 128],
                                         sh.ident[:])
                    ztr.inc(ii)
                if tt >= 2:
                    bpart(tt - 2)
            pt = 15
            zgv.wait(t, pt)
            zev.wait(t, pt - 1)
            for k4 in range(4):
                ii = t.transpose(ps_zt[:, k4 * 128:(k4 + 1) * 128],
                                 zg_bf[:, k4 * 128:(k4 + 1) * 128],
                                 sh.ident[:])
            ztr.inc(ii)
            bpart(14)
            bpart(15)

        @blk.scalar
        def _(s):
            for tt in range(16):
                vch.wait(s, tt * 9 + 3)
                ace.inc(s.activation(s2b[:], sc_f[:], AF.Exp, bias=nmax[:]))
                if tt >= 2:
                    for dc in range(8):
                        it = IT(tt - 2, dc)
                        evy.wait(s, it)
                        sty.inc(s.dma_start(out_dram[(tt - 2) * 128:(tt - 1) * 128,
                                                     dc * 512:(dc + 1) * 512],
                                            ostage[:, (it % 2) * 512:(it % 2 + 1) * 512]))
            for ft in (14, 15):
                for dc in range(8):
                    it = IT(ft, dc)
                    evy.wait(s, it)
                    sty.inc(s.dma_start(out_dram[ft * 128:(ft + 1) * 128, dc * 512:(dc + 1) * 512],
                                        ostage[:, (it % 2) * 512:(it % 2 + 1) * 512]))
            if final:
                sty.wait_done(s)

        @blk.vector
        def _(v):
            for tt in range(16):
                vb = tt * 9
                pez.wait(v, tt)
                ace.wait(v, tt - 1)
                scv.inc(v.tensor_copy(sc_f[:], ps_scs[tt % 2][:]))
                scf = sc_f[:]
                scv.wait(v, tt)
                vch.inc(v.max(m1[:], scf))
                vch.wait(v, vb + 0)
                vch.inc(v.tensor_reduce(th1[:], m1[:], AX.X, ALU.min))
                vch.inc(v.tensor_reduce(nmax[:], m1[:, 0:1], AX.X, ALU.max, negate=True))
                vch.wait(v, vb + 1)
                v.tensor_scalar(s2b[:], scf, th1[:], None, ALU.is_ge)
                v.scalar_tensor_tensor(s2b[:], s2b[:], -1e9, scf, ALU.mult, ALU.add)
                vch.inc(v.max(m2[:], s2b[:]))
                vch.wait(v, vb + 3)
                vch.inc(v.tensor_reduce(th2[:], m2[:], AX.X, ALU.min))
                vch.wait(v, vb + 4)
                ace.wait(v, tt)
                vch.inc(v.tensor_scalar(sc_f[:], scf, th2[:], None, ALU.is_ge))
                v.tensor_tensor(s2b[:], s2b[:], sc_f[:], ALU.mult)
                vch.inc(v.tensor_reduce(esum[:], s2b[:], AX.X, ALU.add))
                vch.wait(v, vb + 6)
                vch.inc(v.reciprocal(erec[:], esum[:]))
                vch.wait(v, vb + 7)
                vch.inc(v.tensor_scalar(gate[:], s2b[:], erec[:], None, ALU.mult))
                vch.wait(v, vb + 8)
                ztr.wait(v, tt - 1)
                gb = AP(gate.ap().tensor, gate.ap().offset, [[64, 128], [1, 64], [0, 8]])
                zin = ps_z.ap().rearrange("p (b e r) -> p b e r", b=2, r=8)[:, tt % 2, :, :]
                if tt >= 1:
                    ztr.wait(v, tt - 1)
                zout = zg_bf.ap().rearrange("p (e r) -> p e r", r=8)
                zgv.inc(v.tensor_tensor(zout, zin, gb, ALU.mult))
                if tt >= 2:
                    for dc in range(8):
                        it = IT(tt - 2, dc)
                        pey.wait(v, it)
                        xrl.wait(v, it)
                        sty.wait(v, it - 2)
                        evy.inc(v.scalar_tensor_tensor(ostage[:, (it % 2) * 512:(it % 2 + 1) * 512],
                                                       ps_y[:, (it % 2) * 512:(it % 2 + 1) * 512], 2.0,
                                                       xres[:, (it % 2) * 512:(it % 2 + 1) * 512],
                                                       ALU.mult, ALU.add))
                if tt >= 1:
                    pt = tt - 1
                    ztr.wait(v, pt)
                    pey.wait(v, IT(pt - 1, 7))
                    zev.inc(v.tensor_copy(zgT_sb[:], ps_zt[:]))
            for dc in range(8):
                it = IT(14, dc)
                pey.wait(v, it)
                xrl.wait(v, it)
                sty.wait(v, it - 2)
                evy.inc(v.scalar_tensor_tensor(ostage[:, (it % 2) * 512:(it % 2 + 1) * 512],
                                               ps_y[:, (it % 2) * 512:(it % 2 + 1) * 512], 2.0,
                                               xres[:, (it % 2) * 512:(it % 2 + 1) * 512],
                                               ALU.mult, ALU.add))
            ztr.wait(v, 15)
            pey.wait(v, IT(14, 7))
            zev.inc(v.tensor_copy(zgT_sb[:], ps_zt[:]))
            for dc in range(8):
                it = IT(15, dc)
                pey.wait(v, it)
                xrl.wait(v, it)
                sty.wait(v, it - 2)
                evy.inc(v.scalar_tensor_tensor(ostage[:, (it % 2) * 512:(it % 2 + 1) * 512],
                                               ps_y[:, (it % 2) * 512:(it % 2 + 1) * 512], 2.0,
                                               xres[:, (it % 2) * 512:(it % 2 + 1) * 512],
                                               ALU.mult, ALU.add))

        @blk.sync
        def _(sy):
            for tt in range(16):
                for dc in range(8):
                    it = IT(tt, dc)
                    evy.wait(sy, it - 2)
                    xrl.inc(sy.dma_start(xres[:, (it % 2) * 512:(it % 2 + 1) * 512],
                                         resid_dram[tt * 128:(tt + 1) * 128, dc * 512:(dc + 1) * 512]))

    return [sty, evy, xrl]


def build_kernel(detect_races=True):
    nc = bass.Bass("TRN2", target_bir_lowering=False, detect_race_conditions=detect_races)
    kc = KC(nc)

    def din(name, shape, dtype=F32):
        apx = nc.dram_tensor(name, shape, dtype, kind="ExternalInput").ap()
        kc.io[name] = apx
        return apx

    x_core = din("x_core", [TL, C])
    wq_sl = din("wq_sl", [C, HPC * HD])
    wk_sl = din("wk_sl", [C, HPC * HD])
    wv_sl = din("wv_sl", [C, HPC * HD])
    woT = din("woT", [C, C])
    wqkvT = din("wqkvT", [C, PB])
    wprojT = din("wprojT", [C, C])
    din("AT", [C, BOT])
    din("BT", [BOT, C])
    din("Wq_raw", [RD, C])
    din("keysT", [RD, NE])
    cosg = din("cosg", [HD, B * T])
    sing = din("sing", [HD, B * T])
    ident_in = din("ident_in", [128, 128])
    tri_in = din("tri_in", [128, 4 * 512])
    gidx_in = din("gidx_in", [128, 128], U32)

    sk = "ExternalOutput" if _EXPOSE else "Internal"

    def dsc(name, shape, dtype):
        return nc.dram_tensor(name, shape, dtype, kind=sk).ap()

    ag_ln_in_a = nc.dram_tensor("ag_ln_in_a", [C, TL // 2], BF16, kind="Internal").ap()
    ag_ln_in_b = nc.dram_tensor("ag_ln_in_b", [C, TL // 2], BF16, kind="Internal").ap()
    ag_ln_out_a = nc.dram_tensor("ag_ln_out_a", [NCORES * C, TL // 2], BF16, kind="Internal", addr_space="Shared").ap()
    ag_ln_out_b = nc.dram_tensor("ag_ln_out_b", [NCORES * C, TL // 2], BF16, kind="Internal", addr_space="Shared").ap()
    nrm_in = nc.dram_tensor("nrm_in", [1, 2], F32, kind="Internal").ap()
    nrm_out = nc.dram_tensor("nrm_out", [NCORES, 2], F32, kind="Internal").ap()
    attn_ag_in_a = nc.dram_tensor("attn_ag_in_a", [B * T, HPC * HD // 2], BF16, kind="Internal").ap()
    attn_ag_in_b = nc.dram_tensor("attn_ag_in_b", [B * T, HPC * HD // 2], BF16, kind="Internal").ap()
    attn_ag_out_a = nc.dram_tensor("attn_ag_out_a", [NCORES * B * T, HPC * HD // 2], BF16, kind="Internal", addr_space="Shared").ap()
    attn_ag_out_b = nc.dram_tensor("attn_ag_out_b", [NCORES * B * T, HPC * HD // 2], BF16, kind="Internal", addr_space="Shared").ap()
    qT_sc = dsc("qT_sc", [HPC * HD, B * T], BF16)
    kT_sc = dsc("kT_sc", [HPC * HD, B * T], BF16)
    v_sc = dsc("v_sc", [B * T, HPC * HD], BF16)
    attnT_loc = dsc("attnT_loc", [C, TL], BF16)
    x1_sc = dsc("x1_sc", [TL, C], F32)
    x2_sc = dsc("x2_sc", [TL, C], F32)
    x3_sc = dsc("x3_sc", [TL, C], F32)
    qkvb_sc = dsc("qkvb_sc", [TL, PB], BF16)
    attnbT_loc = dsc("attnbT_loc", [C, TL], BF16)
    out_y = nc.dram_tensor("out_y", [TL, C], F32, kind="ExternalOutput").ap()

    class SH:
        pass

    sh = SH()
    sh.ident = kc.sb("ident", [128, 128], BF16)
    sh.ones_c = kc.sb("ones_c", [128, 1], BF16)
    sh.ones_rf = kc.sb("ones_rf", [1, 128], F32)
    sh.negM = kc.sb("negM", [128, 1], F32)
    sh.wcomb = kc.sb("wcomb", [128, 32 * NE], BF16)

    boot = Stream(kc, step=16, cls="h")
    bootg = Stream(kc, step=16, cls="s")
    bootm = Stream(kc)
    with nc.Block() as blk:
        @blk.sync
        def _(sy):
            pass

        @blk.gpsimd
        def _(gp):
            bootg.inc(gp.dma_start(sh.ident[:], ident_in))
            gp.memset(sh.ones_c[:], 1.0)
            bootm.inc(gp.memset(sh.ones_rf[:], 1.0))

    kc.recycle()

    ln1_stk = ExitStack()
    lnT_sb = kc.sb("lnT1", [128, 32 * TL], BF16, ln1_stk)
    shp = Stream(kc, parity=2, step=16, cls="h")
    ag0 = Stream(kc, cls="p")

    def _ln1_sync_tail(sy, evc):
        evc.wait(sy, 63)
        shp.inc(sy.dma_start(ag_ln_in_a.rearrange("(c p) t -> p c t", p=128),
                             lnT_sb.ap().rearrange("p (c t) -> p c t", c=32)[:, :, 0:1024]))
        evc.wait(sy, 127)
        shp.inc(sy.dma_start(ag_ln_in_b.rearrange("(c p) t -> p c t", p=128),
                             lnT_sb.ap().rearrange("p (c t) -> p c t", c=32)[:, :, 1024:2048]))

    def _ln1_gp(gp):
        kc.wait_barrier(gp)
        shp.wait(gp, 0)
        ag0.inc(gp.collective_compute("AllGather", ALU.bypass,
                                      replica_groups=[list(range(NCORES))],
                                      ins=[ag_ln_in_a.opt()], outs=[ag_ln_out_a.opt()]))
        shp.wait(gp, 1)
        ag0.inc(gp.collective_compute("AllGather", ALU.bypass,
                                      replica_groups=[list(range(NCORES))],
                                      ins=[ag_ln_in_b.opt()], outs=[ag_ln_out_b.opt()]))

    emit_ln(kc, x_core, lnT_sb, sh, "ln1", ln1_stk, sync_tail=_ln1_sync_tail, gp_sec=_ln1_gp)

    ln1_stk.close()
    kc.recycle()

    s13 = ExitStack()
    w_qkv = kc.sb("w_qkv", [128, 32 * (3 * HPC * HD)], BF16, s13)
    rhs_sb = kc.sb("rhs_sb", [128, 2 * 32 * 512], BF16, s13)
    cs_sb = kc.sb("cs_sb", [128, 2 * 2 * 512], F32, s13)
    rot_sb = kc.sb("rot_sb", [128, 512], F32, s13)
    t1_sb = kc.sb("t1_sb", [128, 512], F32, s13)
    t2_sb = kc.sb("t2_sb", [128, 512], F32, s13)
    qk_bf = kc.sb("qk_bf", [128, 2 * 512], BF16, s13)
    sq_bf = kc.sb("sq_bf", [128, 2 * 512], BF16, s13)
    ntmp = kc.sb("ntmp", [1, 2], F32, s13)
    runmax = kc.sb("runmax", [1, 2], F32, s13)
    v_bf = kc.sb("v_bf", [128, 2 * 512], BF16, s13)
    v_stage = kc.sb("v_stage", [128, 2 * 512], BF16, s13)
    ps_mm = kc.ps_("ps_mm", [128, 1024], F32, s13)
    ps_n = kc.ps_("ps_n", [1, 1024], F32, s13)
    ps_vt0 = kc.ps_("ps_vt0", [128, 512], BF16, s13)
    ps_vt1 = kc.ps_("ps_vt1", [128, 512], BF16, s13)

    wl3 = Stream(kc, step=16, per_unit=3, cls="s")
    rl3 = Stream(kc, parity=2, step=16, per_unit=3, cls="h")
    pe3 = Stream(kc)
    rpq = Stream(kc)
    rpv = Stream(kc)
    sqr = Stream(kc)
    pnm = Stream(kc)
    nrd = Stream(kc)
    spl = Stream(kc, parity=2, step=16, cls="s")
    vtp = Stream(kc)
    vev = Stream(kc)
    vsp = Stream(kc, parity=2, step=16, cls="h")

    NDJ = 12
    TC_ORDER = [o * 4 + tkl for tkl in (0, 1) for o in range(NCORES)] + \
               [o * 4 + tkl for tkl in (2, 3) for o in range(NCORES)]

    def QI(ti, dj):
        return ti * 8 + dj

    def WI(ti, dj):
        return ti * 4 + (dj - 8)

    with nc.Block() as blk:
        @blk.gpsimd
        def _(gp):
            kc.wait_barrier(gp)
            for wi, wd in enumerate((wq_sl, wk_sl, wv_sl)):
                wl3.inc(gp.dma_start(
                    w_qkv.ap().rearrange("p (c d) -> p c d", c=32)[:, :, wi * 512:(wi + 1) * 512],
                    wd.rearrange("(c p) d -> p c d", p=128)))
            for ti, tc in enumerate(TC_ORDER):
                for dj in range(8):
                    q = QI(ti, dj)
                    rpq.wait(gp, q)
                    dst = qT_sc if dj < 4 else kT_sc
                    jj = dj % 4
                    spl.inc(gp.dma_start(dst[jj * 128:(jj + 1) * 128, tc * 512:(tc + 1) * 512],
                                         qk_bf[:, (q % 2) * 512:(q % 2 + 1) * 512]))

        @blk.sync
        def _(sy):
            kc.wait_barrier(sy)
            ag0.wait(sy, 0)

            def vspill(ti2, dj):
                tc2 = TC_ORDER[ti2]
                w = WI(ti2, dj)
                vev.wait(sy, w)
                vsp.inc(sy.dma_start(
                    v_sc.rearrange("(tc s p) f -> p tc s f", p=128, s=4)[:, tc2, :,
                                                                         (dj - 8) * 128:(dj - 7) * 128],
                    v_stage.ap().rearrange("p (b s f) -> p b s f", b=2, s=4)[:, w % 2, :, :]))

            for ti, tc in enumerate(TC_ORDER):
                o, tkl = tc // 4, tc % 4
                if tkl >= 2:
                    ag0.wait(sy, 1)
                pe3.wait(sy, (ti - 1) * NDJ - 1)
                agt = ag_ln_out_a if tkl < 2 else ag_ln_out_b
                rl3.inc(sy.dma_start(
                    rhs_sb.ap().rearrange("p (b c t) -> p b c t", b=2, c=32)[:, ti % 2, :, :],
                    agt.rearrange("(o c p) t -> p o c t", o=NCORES, p=128)[:, o, :,
                                                                           (tkl % 2) * 512:(tkl % 2 + 1) * 512]))
                rl3.inc(sy.dma_start(
                    cs_sb.ap().rearrange("p (b x t) -> p b x t", b=2, x=2)[:, ti % 2, 0, :],
                    cosg[:, tc * 512:(tc + 1) * 512]))
                rl3.inc(sy.dma_start(
                    cs_sb.ap().rearrange("p (b x t) -> p b x t", b=2, x=2)[:, ti % 2, 1, :],
                    sing[:, tc * 512:(tc + 1) * 512]))
                if ti >= 1:
                    for dj in range(8, NDJ):
                        vspill(ti - 1, dj)
            for dj in range(8, NDJ):
                vspill(NGT - 1, dj)

        @blk.tensor
        def _(t):
            kc.wait_barrier(t)
            wl3.wait(t, 0)
            for ti, tc in enumerate(TC_ORDER):
                rl3.wait(t, ti)
                for dj in range(NDJ):
                    g = ti * NDJ + dj
                    g2 = g - 2
                    if g2 >= 0:
                        dj2 = g2 % NDJ
                        if dj2 < 8:
                            rpq.wait(t, QI(g2 // NDJ, dj2))
                        else:
                            rpv.wait(t, WI(g2 // NDJ, dj2))
                    for cc in range(32):
                        ii = t.matmul(ps_mm[:, (g % 2) * 512:(g % 2 + 1) * 512],
                                      w_qkv.ap().rearrange("p (c d) -> p c d", c=32)[:, cc,
                                                                                     dj * 128:(dj + 1) * 128],
                                      rhs_sb.ap().rearrange("p (b c t) -> p b c t", b=2, c=32)[:, ti % 2, cc, :],
                                      start=(cc == 0), stop=(cc == 31))
                    pe3.inc(ii)
                    if dj < 8:
                        q = QI(ti, dj)
                        if q >= 1:
                            qq = q - 1
                            sqr.wait(t, qq)
                            nrd.wait(t, 2 * (qq - 2) + 1)
                            pnm.inc(t.matmul(ps_n[0:1, (qq % 2) * 512:(qq % 2 + 1) * 512],
                                             sh.ones_c[:], sq_bf[:, (qq % 2) * 512:(qq % 2 + 1) * 512],
                                             start=True, stop=True))
                    else:
                        w = WI(ti, dj)
                        if w >= 1:
                            ww = w - 1
                            rpv.wait(t, ww)
                            vev.wait(t, ww - 2)
                            pvt = ps_vt0 if ww % 2 == 0 else ps_vt1
                            for sub in range(4):
                                ii = t.transpose(pvt[:, sub * 128:(sub + 1) * 128],
                                                 v_bf[:, (ww % 2) * 512 + sub * 128:(ww % 2) * 512 + (sub + 1) * 128],
                                                 sh.ident[:])
                            vtp.inc(ii)
            qq = QI(NGT - 1, 7)
            sqr.wait(t, qq)
            nrd.wait(t, 2 * (qq - 2) + 1)
            pnm.inc(t.matmul(ps_n[0:1, (qq % 2) * 512:(qq % 2 + 1) * 512],
                             sh.ones_c[:], sq_bf[:, (qq % 2) * 512:(qq % 2 + 1) * 512],
                             start=True, stop=True))
            ww = WI(NGT - 1, 11)
            rpv.wait(t, ww)
            vev.wait(t, ww - 2)
            pvt = ps_vt0 if ww % 2 == 0 else ps_vt1
            for sub in range(4):
                ii = t.transpose(pvt[:, sub * 128:(sub + 1) * 128],
                                 v_bf[:, (ww % 2) * 512 + sub * 128:(ww % 2) * 512 + (sub + 1) * 128],
                                 sh.ident[:])
            vtp.inc(ii)

        @blk.vector
        def _(v):
            kc.wait_barrier(v)
            for ti, tc in enumerate(TC_ORDER):
                for dj in range(8):
                    g = ti * NDJ + dj
                    q = QI(ti, dj)
                    pe3.wait(v, g)
                    ps = ps_mm[:, (g % 2) * 512:(g % 2 + 1) * 512]
                    cos = cs_sb.ap().rearrange("p (b x t) -> p b x t", b=2, x=2)[:, ti % 2, 0, :]
                    sin = cs_sb.ap().rearrange("p (b x t) -> p b x t", b=2, x=2)[:, ti % 2, 1, :]
                    v.tensor_copy(rot_sb[0:64, :], ps[64:128, :])
                    v.tensor_copy(rot_sb[64:128, :], ps[0:64, :])
                    v.tensor_tensor(t1_sb[:], ps, cos, ALU.mult)
                    v.tensor_tensor(t2_sb[:], rot_sb[:], sin, ALU.mult)
                    spl.wait(v, q - 2)
                    sqr.wait(v, q - 2)
                    rpq.inc(v.tensor_tensor(qk_bf[:, (q % 2) * 512:(q % 2 + 1) * 512],
                                            t1_sb[:], t2_sb[:], ALU.add))
                    if q >= 1:
                        qq = q - 1
                        col = 0 if (qq % 8) < 4 else 1
                        pnm.wait(v, qq)
                        nrd.inc(v.tensor_reduce(ntmp[0:1, col:col + 1],
                                                ps_n[0:1, (qq % 2) * 512:(qq % 2 + 1) * 512],
                                                AX.X, ALU.max))
                        nrd.wait(v, 2 * qq)
                        if qq == 0:
                            ii = v.tensor_copy(runmax[0:1, 0:1], ntmp[0:1, 0:1])
                        elif qq == 4:
                            ii = v.tensor_copy(runmax[0:1, 1:2], ntmp[0:1, 1:2])
                        else:
                            ii = v.tensor_tensor(runmax[0:1, col:col + 1], runmax[0:1, col:col + 1],
                                                 ntmp[0:1, col:col + 1], ALU.max)
                        nrd.inc(ii)
                for dj in range(8, NDJ):
                    w = WI(ti, dj)
                    if w >= 1:
                        ww = w - 1
                        vtp.wait(v, ww)
                        vsp.wait(v, ww - 2)
                        pvt = ps_vt0 if ww % 2 == 0 else ps_vt1
                        vev.inc(v.tensor_copy(
                            v_stage.ap().rearrange("p (b f) -> p b f", b=2)[:, ww % 2, :], pvt[:]))
            qq = QI(NGT - 1, 7)
            pnm.wait(v, qq)
            nrd.inc(v.tensor_reduce(ntmp[0:1, 1:2], ps_n[0:1, (qq % 2) * 512:(qq % 2 + 1) * 512],
                                    AX.X, ALU.max))
            nrd.wait(v, 2 * qq)
            nrd.inc(v.tensor_tensor(runmax[0:1, 1:2], runmax[0:1, 1:2], ntmp[0:1, 1:2], ALU.max))
            ww = WI(NGT - 1, 11)
            vtp.wait(v, ww)
            vsp.wait(v, ww - 2)
            pvt = ps_vt0 if ww % 2 == 0 else ps_vt1
            vev.inc(v.tensor_copy(v_stage.ap().rearrange("p (b f) -> p b f", b=2)[:, ww % 2, :], pvt[:]))

        @blk.scalar
        def _(s):
            kc.wait_barrier(s)
            for ti, tc in enumerate(TC_ORDER):
                for dj in range(NDJ):
                    g = ti * NDJ + dj
                    if dj < 8:
                        q = QI(ti, dj)
                        rpq.wait(s, q)
                        pnm.wait(s, q - 2)
                        sqr.inc(s.activation(sq_bf[:, (q % 2) * 512:(q % 2 + 1) * 512],
                                             qk_bf[:, (q % 2) * 512:(q % 2 + 1) * 512], AF.Square))
                    else:
                        w = WI(ti, dj)
                        pe3.wait(s, g)
                        vtp.wait(s, w - 2)
                        rpv.inc(s.copy(v_bf[:, (w % 2) * 512:(w % 2 + 1) * 512],
                                       ps_mm[:, (g % 2) * 512:(g % 2 + 1) * 512]))

    nD = Stream(kc, cls="p")
    nA1 = Stream(kc, step=16, cls="s")
    nA2 = Stream(kc, step=16, cls="s")
    nCl = Stream(kc, cls="s")
    nrm8 = kc.sb("nrm8", [8, 2], F32, s13)
    gmax = kc.sb("gmax", [1, 2], F32, s13)
    lnm = kc.sb("lnm", [1, 2], F32, s13)
    msum = kc.sb("msum", [1, 1], F32, s13)
    negs = kc.sb("negs", [1, 1], F32, s13)
    ps_nm = kc.ps_("ps_nm", [128, 64], F32, s13)
    with nc.Block() as blk:
        @blk.gpsimd
        def _(gp):
            nrd.wait_done(gp)
            nA1.inc(gp.dma_start(nrm_in, runmax[0:1, :]))
            nA1.wait(gp, 0)
            nCl.inc(gp.collective_compute("AllGather", ALU.bypass,
                                          replica_groups=[list(range(NCORES))],
                                          ins=[nrm_in.opt()], outs=[nrm_out.opt()]))
            nCl.wait(gp, 0)
            nA2.inc(gp.dma_start(nrm8[:], nrm_out))
            nA2.wait(gp, 0)
            nD.inc(gp.tensor_reduce(gmax[0:1, :], nrm8[:], AX.C, ALU.max))

        @blk.scalar
        def _(s):
            nD.wait(s, 0)
            nD.inc(s.activation(lnm[:], gmax[:], AF.Ln))

        @blk.vector
        def _(v):
            nD.wait(v, 1)
            nD.inc(v.tensor_reduce(msum[:], lnm[:], AX.X, ALU.add))

        @blk.scalar
        def _(s):
            nD.wait(s, 2)
            nD.inc(s.activation(msum[:], msum[:], AF.Exp, scale=0.5))

        @blk.vector
        def _(v):
            nD.wait(v, 3)
            nD.inc(v.tensor_scalar(negs[:], msum[:], -SCALE_A, -1.0, ALU.mult, ALU.add))

        @blk.tensor
        def _(t):
            nD.wait(t, 4)
            nD.inc(t.matmul(ps_nm[:, 0:1], sh.ones_rf[:], negs[:], start=True, stop=True))

        @blk.vector
        def _(v):
            nD.wait(v, 5)
            nD.inc(v.tensor_copy(sh.negM[:], ps_nm[:, 0:1]))

    s13.close()
    kc.recycle()

    p4 = ExitStack()
    tri_sb = kc.sb("tri4", [128, 4 * 512], F32, p4)
    q_sb = kc.sb("q_sb", [128, 2 * 512], BF16, p4)
    k_sb = kc.sb("k_sb", [128, 2 * 4096], BF16, p4)
    v_sb4 = kc.sb("v_sb4", [128, 2 * 4096], BF16, p4)
    p_bf = kc.sb("p_bf", [128, 2 * 1024], BF16, p4)
    lnl = kc.sb("lnl", [1, 512], F32, p4)
    linv = kc.sb("linv", [1, 512], F32, p4)
    lcols = kc.sb("lcols", [128, 4], F32, p4)
    ou_bf = kc.sb("ou_bf", [128, 2 * 512], BF16, p4)
    o_st = kc.sb("o_st", [128, 2 * 512], BF16, p4)
    ps_s4 = kc.ps_("ps_s4", [128, 2048], F32, p4)
    ps_o4 = kc.ps_("ps_o4", [128, 512], F32, p4)
    ps_l4 = kc.ps_("ps_l4", [1, 512], F32, p4)
    ps_lt = kc.ps_("ps_lt", [128, 4], F32, p4)
    ps_t4 = kc.ps_("ps_t4", [128, 512], BF16, p4)

    g_sb = kc.sb("g_sb", [128, 2 * 2048], BF16, p4)
    g_st = kc.sb("g_st", [128, 2 * 2048], BF16, p4)
    sh.gidx = kc.sb("gidx", [128, 128], U32, p4)
    ag2 = Stream(kc, cls="s")
    trl = Stream(kc, step=16, per_unit=2, cls="h")
    jld = Stream(kc, parity=2, step=16, per_unit=5, cls="h")
    qld = Stream(kc, parity=2, step=16, cls="h")
    pes = Stream(kc)
    msk = Stream(kc)
    exq = Stream(kc)
    pvl = Stream(kc)
    lnl1 = Stream(kc)
    fle = Stream(kc)
    oe1 = Stream(kc)
    ltr = Stream(kc)
    lcp = Stream(kc)
    otr = Stream(kc)
    nrm4 = Stream(kc)
    osp = Stream(kc, parity=2, step=16, per_unit=2, cls="h")

    NKT = {}
    NFIRST = {}
    DFIRST = {}
    PFIRST = {}
    n_ctr = 0
    d_ctr = 0
    p_ctr = 0
    for jb in range(16):
        for qb in range(8):
            u = jb * 8 + qb
            NFIRST[u] = n_ctr
            NKT[u] = 4 * (qb + 1)
            DFIRST[u] = d_ctr
            PFIRST[u] = p_ctr
            n_ctr += NKT[u]
            d_ctr += 4
            p_ctr += NKT[u] // 2

    def emit_otr(t, u):
        fle.wait(t, u)
        lcp.wait(t, u - 1)
        for sub in range(4):
            ii = t.transpose(ps_lt[:, sub:sub + 1], linv[0:1, sub * 128:(sub + 1) * 128], sh.ones_rf[0:1, 0:1])
        ltr.inc(ii)
        oe1.wait(t, u)
        nrm4.wait(t, u - 1)
        for sub in range(4):
            ii = t.transpose(ps_t4[:, sub * 128:(sub + 1) * 128],
                             ou_bf[:, (u % 2) * 512 + sub * 128:(u % 2) * 512 + (sub + 1) * 128],
                             sh.ident[:])
        otr.inc(ii)

    def ospill(sy, u):
        jb2, qb2 = u // 8, u % 8
        j2, b2 = jb2 // 4, jb2 % 4
        nrm4.wait(sy, u)
        tgt = attn_ag_in_a if j2 < 2 else attn_ag_in_b
        jj2 = j2 % 2
        for s_lo in range(2):
            osp.inc(sy.dma_start(
                AP(tgt.tensor, qb2 * 256 * 2048 + s_lo * 128 * 2048 + b2 * 512 + jj2 * 128,
                   [[2048, 128], [256, 2], [1, 128]]),
                AP(o_st.ap().tensor, (u % 2) * 512 + s_lo * 128, [[1024, 128], [256, 2], [1, 128]])))

    with nc.Block() as blk:
        @blk.gpsimd
        def _(gp):
            osp.wait(gp, 63)
            ag2.inc(gp.collective_compute("AllGather", ALU.bypass,
                                          replica_groups=[list(range(NCORES))],
                                          ins=[attn_ag_in_a.opt()], outs=[attn_ag_out_a.opt()]))
            osp.wait(gp, 126)
            osp.wait(gp, 127)
            ag2.inc(gp.collective_compute("AllGather", ALU.bypass,
                                          replica_groups=[list(range(NCORES))],
                                          ins=[attn_ag_in_b.opt()], outs=[attn_ag_out_b.opt()]))

        @blk.sync
        def _(sy):
            kc.wait_barrier(sy)
            nD.wait(sy, 6)
            trl.inc(sy.dma_start(tri_sb[:], tri_in))
            trl.inc(sy.dma_start(sh.gidx[:], gidx_in))
            for jb in range(16):
                j, b = jb // 4, jb % 4
                if jb >= 2:
                    u2 = (jb - 2) * 8 + 7
                    pvl.wait(sy, NFIRST[u2] + NKT[u2] - 1)
                jld.inc(sy.dma_start(
                    AP(k_sb.ap().tensor, (jb % 2) * 4096, [[2 * 4096, 128], [512, 8], [1, 512]]),
                    AP(kT_sc.tensor, (j * 128) * (B * T) + b * 512, [[B * T, 128], [2048, 8], [1, 512]])))
                for sub in range(4):
                    jld.inc(sy.dma_start(
                        AP(v_sb4.ap().tensor, (jb % 2) * 4096 + sub * 128,
                           [[2 * 4096, 128], [512, 8], [1, 128]]),
                        AP(v_sc.tensor, (b * 512 + sub * 128) * (HPC * HD) + j * 128,
                           [[HPC * HD, 128], [2048 * HPC * HD, 8], [1, 128]])))
                for qb in range(8):
                    u = jb * 8 + qb
                    if u >= 2:
                        u2 = u - 2
                        pes.wait(sy, NFIRST[u2] + NKT[u2] - 1)
                    qld.inc(sy.dma_start(
                        q_sb[:, (u % 2) * 512:(u % 2 + 1) * 512],
                        qT_sc[j * 128:(j + 1) * 128, qb * 2048 + b * 512:qb * 2048 + b * 512 + 512]))
                    if u >= 2:
                        ospill(sy, u - 2)
            ospill(sy, 126)
            ospill(sy, 127)

        @blk.tensor
        def _(t):
            kc.wait_barrier(t)
            nD.wait(t, 6)
            for jb in range(16):
                jld.wait(t, jb)
                for qb in range(8):
                    u = jb * 8 + qb
                    qld.wait(t, u)
                    L = NKT[u]
                    n0 = NFIRST[u]
                    M = L // 2
                    m0 = PFIRST[u]
                    for m in range(M + 1):
                        if m == 1:
                            oe1.wait(t, u - 1)
                            lnl1.wait(t, u - 1)
                        if m < M:
                            exq.wait(t, m0 + m - 2)
                            for t2 in range(2):
                                n = n0 + 2 * m + t2
                                kt = 2 * m + t2
                                pes.inc(t.matmul(ps_s4[:, ((m % 2) * 2 + t2) * 512:((m % 2) * 2 + t2 + 1) * 512],
                                                 k_sb[:, (jb % 2) * 4096 + kt * 128:(jb % 2) * 4096 + (kt + 1) * 128],
                                                 q_sb[:, (u % 2) * 512:(u % 2 + 1) * 512],
                                                 start=True, stop=True))
                        if m >= 1:
                            exq.wait(t, m0 + m - 1)
                            for t2 in range(2):
                                kt = 2 * (m - 1) + t2
                                n = n0 + kt
                                t.matmul(ps_o4[:],
                                         v_sb4[:, (jb % 2) * 4096 + kt * 128:(jb % 2) * 4096 + (kt + 1) * 128],
                                         p_bf[:, ((m - 1) % 2) * 1024 + t2 * 512:((m - 1) % 2) * 1024 + (t2 + 1) * 512],
                                         start=(kt == 0), stop=(kt == L - 1))
                                pvl.inc(t.matmul(ps_l4[0:1, 0:512],
                                                 sh.ones_c[:],
                                                 p_bf[:, ((m - 1) % 2) * 1024 + t2 * 512:((m - 1) % 2) * 1024 + (t2 + 1) * 512],
                                                 start=(kt == 0), stop=(kt == L - 1)))
                    if u >= 1:
                        emit_otr(t, u - 1)
            emit_otr(t, 127)

        @blk.vector
        def _(v):
            kc.wait_barrier(v)
            trl.wait(v, 0)

            def vfin(u):
                ltr.wait(v, u)
                lcp.inc(v.tensor_copy(lcols[:], ps_lt[:]))
                otr.wait(v, u)
                lcp.wait(v, u)
                osp.wait(v, u - 2)
                for sub in range(4):
                    ii = v.tensor_scalar(o_st[:, (u % 2) * 512 + sub * 128:(u % 2) * 512 + (sub + 1) * 128],
                                         ps_t4[:, sub * 128:(sub + 1) * 128],
                                         lcols[:, sub:sub + 1], None, ALU.mult)
                nrm4.inc(ii)

            def oevac(u):
                pvl.wait(v, NFIRST[u] + NKT[u] - 1)
                otr.wait(v, u - 2)
                oe1.inc(v.tensor_copy(ou_bf[:, (u % 2) * 512:(u % 2 + 1) * 512], ps_o4[:]))

            for jb in range(16):
                for qb in range(8):
                    u = jb * 8 + qb
                    L = NKT[u]
                    n0 = NFIRST[u]
                    if u >= 1:
                        oevac(u - 1)
                    for kt in range(4 * qb, L):
                        n = n0 + kt
                        pes.wait(v, n)
                        dd = kt - 4 * qb
                        mloc = (kt // 2) % 2
                        t2 = kt % 2
                        sl = (mloc * 2 + t2) * 512
                        msk.inc(v.tensor_tensor(ps_s4[:, sl:sl + 512],
                                                ps_s4[:, sl:sl + 512],
                                                tri_sb[:, dd * 512:(dd + 1) * 512], ALU.add))
                    if u >= 1:
                        vfin(u - 1)
            oevac(127)
            vfin(127)

        @blk.scalar
        def _(s):
            kc.wait_barrier(s)
            nD.wait(s, 6)
            for jb in range(16):
                for qb in range(8):
                    u = jb * 8 + qb
                    L = NKT[u]
                    n0 = NFIRST[u]
                    M = L // 2
                    m0 = PFIRST[u]
                    for m in range(M):
                        hi = 2 * m + 1
                        if hi >= 4 * qb:
                            msk.wait(s, DFIRST[u] + (hi - 4 * qb))
                        else:
                            pes.wait(s, n0 + hi)
                        pvl.wait(s, n0 + 2 * (m - 2) + 1)
                        exq.inc(s.activation(p_bf[:, (m % 2) * 1024:(m % 2 + 1) * 1024],
                                             ps_s4[:, (m % 2) * 1024:(m % 2 + 1) * 1024],
                                             AF.Exp, bias=sh.negM[:], scale=SCALE_A))
                    pvl.wait(s, n0 + L - 1)
                    lnl1.inc(s.activation(lnl[0:1, :], ps_l4[0:1, 0:512], AF.Ln))
                    lnl1.wait(s, u)
                    fle.inc(s.activation(linv[0:1, :], lnl[0:1, :], AF.Exp, scale=-1.0))



    p6streams = []
    prev_gev = None
    for half, (agt, agwait) in enumerate(((attn_ag_out_a, 0), (attn_ag_out_b, 1))):
        gix = Stream(kc, parity=2, step=16, cls="s")
        gtp = Stream(kc)
        gev = Stream(kc)
        gsp = Stream(kc, parity=2, step=16, per_unit=2, cls="h")
        agt_w = agt.rearrange("(a b) f -> a (b f)", b=8)
        pg = prev_gev
        with nc.Block() as blk:
            @blk.gpsimd
            def _(gp):
                osp.wait_done(gp)
                trl.wait(gp, 0)
                ag2.wait(gp, agwait)
                for g in range(16):
                    gtp.wait(gp, (g - 2) * 4 + 3)
                    gix.inc(gp.indirect_dma_start(
                        g_sb[:, (g % 2) * 2048:(g % 2) * 2048 + 2048], None,
                        agt_w, IndirectOffsetOnAxis(sh.gidx[:, g:g + 1], 0)))

            @blk.tensor
            def _(t):
                nrm4.wait(t, 127)
                if pg is not None:
                    pg.wait(t, 15 * 4 + 3)
                for g in range(16):
                    gix.wait(t, g)
                    for q in range(4):
                        bidx = g * 4 + q
                        gev.wait(t, bidx - 1)
                        for kk in range(4 * q, 4 * q + 4):
                            ii = t.transpose(
                                ps_t4[:, (kk % 4) * 128:(kk % 4 + 1) * 128],
                                g_sb[:, (g % 2) * 2048 + kk * 128:(g % 2) * 2048 + (kk + 1) * 128],
                                sh.ident[:])
                        gtp.inc(ii)

            @blk.scalar
            def _(s):
                for g in range(16):
                    gsp.wait(s, g - 2)
                    for q in range(4):
                        bidx = g * 4 + q
                        gtp.wait(s, bidx)
                        gev.inc(s.copy(g_st[:, (g % 2) * 2048 + q * 512:(g % 2) * 2048 + (q + 1) * 512],
                                       ps_t4[:, 0:512]))

            @blk.sync
            def _(sy):
                for g in range(16):
                    op, v = g // 2, g % 2
                    gev.wait(sy, g * 4 + 3)
                    for sub in range(2):
                        gsp.inc(sy.dma_start(
                            AP(attnT_loc.tensor, (4 * op + 2 * half + sub) * 128 * TL + v * 128,
                               [[TL, 128], [256, 8], [1, 128]]),
                            AP(g_st.ap().tensor, (g % 2) * 2048 + sub * 128,
                               [[4096, 128], [256, 8], [1, 128]])))
        p6streams += [gsp, gev]
        prev_gev = gev

    p4.close()
    kc.recycle()

    p7 = ExitStack()
    emit_out_proj(kc, "op_a", sh, attnT_loc, woT, x_core, x1_sc, p7)
    p7.close()
    kc.recycle()

    s89 = ExitStack()
    lnT2 = kc.sb("lnT2", [128, 32 * TL], BF16, s89)
    emit_ln(kc, x1_sc, lnT2, sh, "ln2", s89)
    emit_ffn(kc, "ffn1", sh, lnT2, x1_sc, x2_sc, True, s89)
    s89.close()
    kc.recycle()

    s1011 = ExitStack()
    lnT3 = kc.sb("lnT3", [128, 32 * TL], BF16, s1011)
    emit_ln(kc, x2_sc, lnT3, sh, "ln3", s1011)

    wb_sb = kc.sb("wb_sb", [128, 2 * 32 * 512], BF16, s1011)
    qb_stage = kc.sb("qb_stage", [128, 2 * 512], BF16, s1011)
    ps_qb = kc.ps_("ps_qb", [128, 1024], F32, s1011)
    wlb11 = Stream(kc, parity=2, step=16, cls="s")
    pe11 = Stream(kc)
    ev11 = Stream(kc)
    st11 = Stream(kc, parity=2, step=16, cls="h")
    with nc.Block() as blk:
        @blk.gpsimd
        def _(gp):
            kc.wait_barrier(gp)
            for pc in range(12):
                pe11.wait(gp, (pc - 1) * 16 - 1)
                wlb11.inc(gp.dma_start(
                    wb_sb.ap().rearrange("p (b c d) -> p b c d", b=2, c=32)[:, pc % 2, :, :],
                    wqkvT.rearrange("(c p) d -> p c d", p=128)[:, :, pc * 512:(pc + 1) * 512]))

        @blk.tensor
        def _(t):
            kc.wait_barrier(t)
            for pc in range(12):
                wlb11.wait(t, pc)
                for tt in range(16):
                    it = pc * 16 + tt
                    ev11.wait(t, it - 2)
                    for cc in range(32):
                        ii = t.matmul(ps_qb[:, (it % 2) * 512:(it % 2 + 1) * 512],
                                      lnT3.ap().rearrange("p (c t) -> p c t", c=32)[:, cc,
                                                                                    tt * 128:(tt + 1) * 128],
                                      wb_sb.ap().rearrange("p (b c d) -> p b c d", b=2, c=32)[:, pc % 2, cc, :],
                                      start=(cc == 0), stop=(cc == 31))
                    pe11.inc(ii)

        @blk.scalar
        def _(s):
            kc.wait_barrier(s)
            for pc in range(12):
                for tt in range(16):
                    it = pc * 16 + tt
                    pe11.wait(s, it)
                    st11.wait(s, it - 2)
                    ev11.inc(s.copy(qb_stage[:, (it % 2) * 512:(it % 2 + 1) * 512],
                                    ps_qb[:, (it % 2) * 512:(it % 2 + 1) * 512]))

        @blk.sync
        def _(sy):
            kc.wait_barrier(sy)
            for pc in range(12):
                for tt in range(16):
                    it = pc * 16 + tt
                    ev11.wait(sy, it)
                    st11.inc(sy.dma_start(qkvb_sc[tt * 128:(tt + 1) * 128, pc * 512:(pc + 1) * 512],
                                          qb_stage[:, (it % 2) * 512:(it % 2 + 1) * 512]))

    s1011.close()
    kc.recycle()

    p12 = ExitStack()
    qb_sb = kc.sb("qb_sb", [128, 4 * PB], BF16, p12)
    tmp12 = kc.sb("tmp12", [128, 1024], F32, p12)
    sA = kc.sb("sA", [128, 10 * 32], F32, p12)
    mA = kc.sb("mA", [128, 32], F32, p12)
    eA = kc.sb("eA", [128, 10 * 32], F32, p12)
    pA = kc.sb("pA", [128, 10 * 32], F32, p12)
    sumA = kc.sb("sumA", [128, 32], F32, p12)
    recA = kc.sb("recA", [128, 4 * 32], F32, p12)
    oacc = kc.sb("oacc", [128, C], F32, p12)
    ob_bf = kc.sb("ob_bf", [128, 2 * C], BF16, p12)
    ob_st = kc.sb("ob_st", [128, 2 * 512], BF16, p12)
    ps_tb = kc.ps_("ps_tb", [128, 512], BF16, p12)

    PAIRS = [(i, jb) for i in range(4) for jb in range(i + 1)]
    PIDX = {p: ix for ix, p in enumerate(PAIRS)}

    qbl = Stream(kc, parity=1, step=16, per_unit=4, cls="h")
    v12 = Stream(kc)
    vck = Stream(kc)
    a12 = Stream(kc)
    obc = Stream(kc)
    tbt = Stream(kc)
    ebt = Stream(kc)
    sbb = Stream(kc, parity=2, step=16, cls="h")

    def CK(ut, k):
        return ut * 9 + k

    with nc.Block() as blk:
        @blk.sync
        def _(sy):
            kc.wait_barrier(sy)

            def bspill(w):
                ut2, i2 = w // 4, w % 4
                for grp in range(8):
                    ebt.wait(sy, w * 8 + grp)
                    sbb.inc(sy.dma_start(
                        attnbT_loc.rearrange("(c p) t -> p c t", p=128)[:, grp * 4:(grp + 1) * 4,
                                                                        i2 * 512 + ut2 * 128:i2 * 512 + ut2 * 128 + 128],
                        ob_st.ap().rearrange("p (b s f) -> p b s f", b=2, s=4)[:, (w * 8 + grp) % 2, :, :]))

            for ut in range(4):
                if ut >= 1:
                    for w in range((ut - 1) * 4, ut * 4):
                        bspill(w)
                v12.wait(sy, CK(ut - 1, 8))
                for b in range(4):
                    qbl.inc(sy.dma_start(
                        AP(qb_sb.ap().tensor, b * PB, [[4 * PB, 128], [1, PB]]),
                        qkvb_sc[b * 512 + ut * 128:b * 512 + ut * 128 + 128, :]))
            for w in range(12, 16):
                bspill(w)

        @blk.vector
        def _(v):
            kc.wait_barrier(v)

            def edge(ii):
                vck.inc(ii)
                vck.wait(v, vck.issued - 1)
                return ii

            for ut in range(4):
                qbl.wait(v, ut)
                for (i, jb) in PAIRS:
                    px = PIDX[(i, jb)]
                    for gq in range(4):
                        v.tensor_tensor(
                            tmp12[:],
                            AP(qb_sb.ap().tensor, i * PB + gq * 2 * 768,
                               [[4 * PB, 128], [768, 2], [128, 4], [1, 128]]),
                            AP(qb_sb.ap().tensor, jb * PB + gq * 2 * 768 + 512,
                               [[4 * PB, 128], [768, 2], [0, 4], [1, 128]]),
                            ALU.mult)
                        ii = v.tensor_reduce(
                            AP(sA.ap().tensor, px * 32 + gq * 8,
                               [[10 * 32, 128], [4, 2], [1, 4]]),
                            tmp12.ap().rearrange("p (g q f) -> p g q f", g=2, q=4),
                            AX.X, ALU.add)
                v12.inc(ii)
                v12.wait(v, CK(ut, 0))
                for i in range(4):
                    msl = mA[:]
                    edge(v.tensor_copy(msl, sA[:, PIDX[(i, 0)] * 32:PIDX[(i, 0)] * 32 + 32]))
                    for jb in range(1, i + 1):
                        edge(v.tensor_tensor(msl, msl, sA[:, PIDX[(i, jb)] * 32:PIDX[(i, jb)] * 32 + 32],
                                             ALU.max))
                    for jb in range(i + 1):
                        px = PIDX[(i, jb)]
                        if i == 3 and jb == 3:
                            ii = v.tensor_tensor(eA[:, px * 32:px * 32 + 32],
                                                 sA[:, px * 32:px * 32 + 32], msl, ALU.subtract)
                        else:
                            edge(v.tensor_tensor(eA[:, px * 32:px * 32 + 32],
                                                 sA[:, px * 32:px * 32 + 32], msl, ALU.subtract))
                v12.inc(ii)
                a12.wait(v, ut)
                for i in range(4):
                    ssl = sumA[:]
                    edge(v.tensor_copy(ssl, pA[:, PIDX[(i, 0)] * 32:PIDX[(i, 0)] * 32 + 32]))
                    for jb in range(1, i + 1):
                        edge(v.tensor_tensor(ssl, ssl, pA[:, PIDX[(i, jb)] * 32:PIDX[(i, jb)] * 32 + 32],
                                             ALU.add))
                    if i == 3:
                        ii = v.tensor_copy(recA[:, i * 32:(i + 1) * 32], ssl)
                    else:
                        edge(v.tensor_copy(recA[:, i * 32:(i + 1) * 32], ssl))
                v12.inc(ii)
                v12.wait(v, CK(ut, 2))
                ii = v.reciprocal(recA[:], recA[:])
                v12.inc(ii)
                v12.wait(v, CK(ut, 3))
                for ix, (i, jb) in enumerate(PAIRS):
                    px = PIDX[(i, jb)]
                    if ix == len(PAIRS) - 1:
                        ii = v.tensor_tensor(pA[:, px * 32:px * 32 + 32], pA[:, px * 32:px * 32 + 32],
                                             recA[:, i * 32:(i + 1) * 32], ALU.mult)
                    else:
                        edge(v.tensor_tensor(pA[:, px * 32:px * 32 + 32], pA[:, px * 32:px * 32 + 32],
                                             recA[:, i * 32:(i + 1) * 32], ALU.mult))
                v12.inc(ii)
                v12.wait(v, CK(ut, 4))
                for i in range(4):
                    w = ut * 4 + i
                    obc.wait(v, w - 1)
                    for jb in range(i + 1):
                        px = PIDX[(i, jb)]
                        for gq in range(4):
                            pap = AP(pA.ap().tensor, px * 32 + gq * 8,
                                     [[10 * 32, 128], [4, 2], [1, 4], [0, 128]])
                            vap = AP(qb_sb.ap().tensor, jb * PB + gq * 2 * 768 + 640,
                                     [[4 * PB, 128], [768, 2], [0, 4], [1, 128]])
                            oap = AP(oacc.ap().tensor, gq * 1024,
                                     [[C, 128], [512, 2], [128, 4], [1, 128]])
                            if jb == 0:
                                ii = v.tensor_tensor(oap, pap, vap, ALU.mult)
                            else:
                                v.tensor_tensor(tmp12[:], pap, vap, ALU.mult)
                                ii = v.tensor_tensor(
                                    oap, oap,
                                    tmp12.ap().rearrange("p (g q f) -> p g q f", g=2, q=4),
                                    ALU.add)
                    v12.inc(ii)

        @blk.scalar
        def _(s):
            kc.wait_barrier(s)
            for ut in range(4):
                v12.wait(s, CK(ut, 1))
                a12.inc(s.activation(pA[:], eA[:], AF.Exp, scale=SCALE_B))
                for i in range(4):
                    w = ut * 4 + i
                    v12.wait(s, CK(ut, 5 + i))
                    tbt.wait(s, (w - 2) * 8 + 7)
                    obc.inc(s.copy(ob_bf[:, (w % 2) * C:(w % 2 + 1) * C], oacc[:]))
                    for grp in range(8):
                        tbt.wait(s, w * 8 + grp)
                        sbb.wait(s, w * 8 + grp - 2)
                        ebt.inc(s.copy(ob_st[:, ((w * 8 + grp) % 2) * 512:((w * 8 + grp) % 2 + 1) * 512],
                                       ps_tb[:]))

        @blk.tensor
        def _(t):
            kc.wait_barrier(t)
            for w in range(16):
                obc.wait(t, w)
                for grp in range(8):
                    ebt.wait(t, w * 8 + grp - 1)
                    for k4 in range(4):
                        cc = grp * 4 + k4
                        ii = t.transpose(ps_tb[:, k4 * 128:(k4 + 1) * 128],
                                         ob_bf[:, (w % 2) * C + cc * 128:(w % 2) * C + (cc + 1) * 128],
                                         sh.ident[:])
                    tbt.inc(ii)

    p12.close()
    kc.recycle()

    p13 = ExitStack()
    emit_out_proj(kc, "op_b", sh, attnbT_loc, wprojT, x2_sc, x3_sc, p13)
    p13.close()
    kc.recycle()

    s1415 = ExitStack()
    lnT4 = kc.sb("lnT4", [128, 32 * TL], BF16, s1415)
    emit_ln(kc, x3_sc, lnT4, sh, "ln4", s1415)
    emit_ffn(kc, "ffn2", sh, lnT4, x3_sc, out_y, False, s1415, final=True)
    s1415.close()

    kc.stack.close()
    return nc


_ROPE_CACHE = {}


def _host_tables():
    if "t" in _ROPE_CACHE:
        return _ROPE_CACHE["t"]
    pos = np.empty(B * T, dtype=np.float64)
    for o in range(NCORES):
        for tl in range(TL):
            pos[o * TL + tl] = 512 * o + (tl % 512)
    thetas = 1.0 / 10000.0 ** (np.arange(0, HD, 2, dtype=np.float32).astype(np.float64) / HD)
    pt = np.outer(pos, thetas)
    pt = np.concatenate([pt, pt], axis=-1)
    cosg = np.cos(pt).astype(np.float32).T.copy()
    sinf = np.sin(pt).astype(np.float32).T
    sing = sinf.copy()
    sing[:64, :] *= -1.0
    sing = sing.copy()
    tri = np.zeros((128, 4 * 512), dtype=np.float32)
    for d in range(4):
        kk = np.arange(128)[:, None]
        uu = np.arange(512)[None, :]
        tri[:, d * 512:(d + 1) * 512] = np.where(d * 128 + kk <= uu, 0.0, -1e30)
    ident = np.eye(128, dtype=np.float32)
    _ROPE_CACHE["t"] = (cosg, sing, tri, ident)
    return _ROPE_CACHE["t"]


_NC_CACHE = {}
_LAST_RES = None


def kernel(L0_attn_a_x, L0_attn_a_wq, L0_attn_a_wk, L0_attn_a_wv, L0_attn_a_wo,
           L0_attn_b_weight_qkv, L0_attn_b_weight_proj,
           L0_ffn_A_w, L0_ffn_B_w, L0_ffn_Wq_w, L0_ffn_keys):
    x = np.asarray(L0_attn_a_x, dtype=np.float32)
    cosg, sing, tri, ident = _host_tables()
    wqT = np.ascontiguousarray(np.asarray(L0_attn_a_wq, np.float32).T)
    wkT = np.ascontiguousarray(np.asarray(L0_attn_a_wk, np.float32).T)
    wvT = np.ascontiguousarray(np.asarray(L0_attn_a_wv, np.float32).T)
    woT = np.ascontiguousarray(np.asarray(L0_attn_a_wo, np.float32).T)
    wqkvT = np.ascontiguousarray(np.asarray(L0_attn_b_weight_qkv, np.float32).T)
    wprojT = np.ascontiguousarray(np.asarray(L0_attn_b_weight_proj, np.float32).T)
    AT = np.ascontiguousarray(np.asarray(L0_ffn_A_w, np.float32).T)
    BT = np.ascontiguousarray(np.asarray(L0_ffn_B_w, np.float32).T)
    Wq_raw = np.ascontiguousarray(np.asarray(L0_ffn_Wq_w, np.float32))
    keysT = np.ascontiguousarray(np.asarray(L0_ffn_keys, np.float32).T)

    if "nc" not in _NC_CACHE:
        _NC_CACHE["nc"] = build_kernel()
    nc = _NC_CACHE["nc"]

    in_maps = []
    for r in range(NCORES):
        x_core = np.ascontiguousarray(x[:, 512 * r:512 * (r + 1), :].reshape(TL, C))
        gidx = np.zeros((128, 128), dtype=np.uint32)
        for op in range(NCORES):
            for v in range(2):
                gidx[:, op * 2 + v] = op * 2048 + 256 * r + v * 128 + np.arange(128)
        in_maps.append(dict(
            x_core=x_core,
            wq_sl=np.ascontiguousarray(wqT[:, 512 * r:512 * (r + 1)]),
            wk_sl=np.ascontiguousarray(wkT[:, 512 * r:512 * (r + 1)]),
            wv_sl=np.ascontiguousarray(wvT[:, 512 * r:512 * (r + 1)]),
            woT=woT, wqkvT=wqkvT, wprojT=wprojT,
            AT=AT, BT=BT, Wq_raw=Wq_raw, keysT=keysT,
            cosg=cosg, sing=sing, ident_in=ident, tri_in=tri, gidx_in=gidx,
        ))

    res = run_bass_kernel_spmd(nc, in_maps, core_ids=list(range(NCORES)))
    global _LAST_RES
    _LAST_RES = res
    out = np.empty((B, T, C), dtype=np.float32)
    for r in range(NCORES):
        out[:, 512 * r:512 * (r + 1), :] = res.results[r]["out_y"].reshape(B, 512, C)
    return out


# version 19 (final)
# speedup vs baseline: 1.0437x; 1.0360x over previous
import numpy as np
import ml_dtypes
import concourse.bass as bass
import concourse.mybir as mybir
from concourse.bass import IndirectOffsetOnAxis
from concourse.bass_utils import run_bass_kernel_spmd
from contextlib import ExitStack

F32 = mybir.dt.float32
BF16 = mybir.dt.bfloat16
U32 = mybir.dt.uint32
AF = mybir.ActivationFunctionType
ALU = mybir.AluOpType
AX = mybir.AxisListType
AP = bass.AP

NCORES = 8
B, T, C = 4, 4096, 4096
TL = 2048
HD = 128
HPC = 4
NGT = 32
SCALE_A = float(1.0 / np.sqrt(HD))
SCALE_B = float(1.0 / np.sqrt(HD))
PB = 6144
NE, RANK, RD = 64, 8, 16
BOT = NE * RANK
EPS = 1e-5

_EXPOSE = False


class PSem:
    __slots__ = ("h", "n")

    def __init__(self, h):
        self.h = h
        self.n = 0


class Stream:
    def __init__(self, kc, parity=1, step=1, per_unit=1, cls="c"):
        self.ps = [kc.take_sem(cls) for _ in range(parity)]
        if cls != "p":
            kc.reg_stream(self)
        self.base = [p.n for p in self.ps]
        self.parity = parity
        self.step = step
        self.per_unit = per_unit
        self.issued = 0

    def inc(self, instr):
        g = self.issued // self.per_unit
        p = self.ps[g % self.parity]
        instr.then_inc(p.h, self.step)
        p.n += self.step
        self.issued += 1

    def wait(self, eng, g):
        if g < 0:
            return
        p = self.ps[g % self.parity]
        cnt = self.base[g % self.parity] + self.step * self.per_unit * (g // self.parity + 1)
        eng.wait_ge(p.h, cnt)

    def wait_done(self, eng):
        for p in self.ps:
            if p.n > 0:
                eng.wait_ge(p.h, p.n)


class KC:

    def __init__(self, nc):
        self.nc = nc
        self.stack = ExitStack()
        allsems = [PSem(self.stack.enter_context(nc.semaphore(f"s{i}"))) for i in range(92)]
        self.phase_sem = allsems[0]
        self.dummy = self.stack.enter_context(nc.sbuf_tensor("rcy_dummy", [1, 4], mybir.dt.float32))
        self.pin_pool = allsems[1:7]
        self.free = {"c": allsems[7:40], "h": allsems[40:65], "s": allsems[65:]}
        self.used = {"c": [], "h": []}
        self.live = []
        self.phase_n = 0
        self.io = {}

    def take_sem(self, cls="c"):
        if cls == "p":
            assert self.pin_pool, "out of pinned sems"
            return self.pin_pool.pop()
        if cls == "s":
            assert self.free["s"], "out of s sems"
            return self.free["s"].pop()
        assert self.free[cls], f"out of {cls} sems in phase"
        p = self.free[cls].pop()
        self.used[cls].append(p)
        return p

    def reg_stream(self, st):
        self.live.append(st)

    def recycle(self, extra_waits=()):
        for cls in ("c", "h"):
            self.free[cls] = self.used[cls] + self.free[cls]
        self.used = {"c": [], "h": []}
        self.live = []

    def sb(self, name, shape, dtype, stack=None):
        return (stack or self.stack).enter_context(self.nc.sbuf_tensor(name, shape, dtype))

    def ps_(self, name, shape, dtype, stack=None):
        return (stack or self.stack).enter_context(self.nc.psum_tensor(name, shape, dtype))

    def wait_barrier(self, eng):
        pass


def emit_ln(kc, x_src, lnT_sb, sh, tag, stk, sync_tail=None, gp_sec=None):
    nc = kc.nc
    lstk = ExitStack()
    sh.xin = kc.sb(f"{tag}_xin", [128, 2 * C], F32, lstk)
    sh.lntok = kc.sb(f"{tag}_lntok", [128, 2 * C], BF16, lstk)
    sh.stats = kc.sb(f"{tag}_stats", [128, 48], F32, lstk)
    sh.mv = kc.sb(f"{tag}_mv", [128, 2], F32, lstk)
    sh.veps = kc.sb(f"{tag}_veps", [128, 1], F32, lstk)
    sh.rstd = kc.sb(f"{tag}_rstd", [128, 1], F32, lstk)
    xld = Stream(kc, parity=2, step=16, cls="h")
    agg = Stream(kc)
    vep = Stream(kc)
    acl = Stream(kc)
    acr = Stream(kc)
    nrmv = Stream(kc)
    trp = Stream(kc)
    evc = Stream(kc)
    ps_tr0 = kc.ps_(f"{tag}_pstr0", [128, 512], BF16, lstk)
    ps_tr1 = kc.ps_(f"{tag}_pstr1", [128, 512], BF16, lstk)
    ps_trs = (ps_tr0, ps_tr1)

    with nc.Block() as blk:
        @blk.sync
        def _(sy):
            kc.wait_barrier(sy)
            for tt in range(16):
                nrmv.wait(sy, tt - 2)
                xld.inc(sy.dma_start(sh.xin[:, (tt % 2) * C:(tt % 2 + 1) * C],
                                     x_src[tt * 128:(tt + 1) * 128, :]))
            if sync_tail is not None:
                sync_tail(sy, evc)

        @blk.gpsimd
        def _(gp):
            if gp_sec is not None:
                gp_sec(gp)

        @blk.vector
        def _(v):
            kc.wait_barrier(v)
            for tt in range(16):
                xld.wait(v, tt)
                xt = sh.xin[:, (tt % 2) * C:(tt % 2 + 1) * C]
                for cch in range(8):
                    v.bn_stats(sh.stats[:, 6 * cch:6 * cch + 6], xt[:, 512 * cch:512 * (cch + 1)])
                agg.inc(v.bn_aggr(sh.mv[:], sh.stats[:]))
                agg.wait(v, tt)
                vep.inc(v.tensor_scalar(sh.veps[:], sh.mv[:, 1:2], EPS, None, ALU.add))
                acr.wait(v, tt)
                trp.wait(v, (tt - 2) * 8 + 7)
                nrmv.inc(v.tensor_scalar(sh.lntok[:, (tt % 2) * C:(tt % 2 + 1) * C], xt,
                                         sh.mv[:, 0:1], sh.rstd[:], ALU.subtract, ALU.mult))

        @blk.scalar
        def _(s):
            kc.wait_barrier(s)
            for tt in range(16):
                vep.wait(s, tt)
                acl.inc(s.activation(sh.rstd[:], sh.veps[:], AF.Ln))
                acl.wait(s, tt)
                acr.inc(s.activation(sh.rstd[:], sh.rstd[:], AF.Exp, scale=-0.5))
                for g in range(8):
                    gg = tt * 8 + g
                    trp.wait(s, gg)
                    dst = lnT_sb.ap().rearrange("p (c t) -> p c t", c=32)[:, g * 4:(g + 1) * 4,
                                                                          tt * 128:(tt + 1) * 128]
                    src = ps_trs[gg % 2].ap().rearrange("p (k f) -> p k f", k=4)
                    evc.inc(s.copy(dst, src))

        @blk.tensor
        def _(t):
            kc.wait_barrier(t)
            for tt in range(16):
                nrmv.wait(t, tt)
                lt = sh.lntok[:, (tt % 2) * C:(tt % 2 + 1) * C]
                for g in range(8):
                    gg = tt * 8 + g
                    evc.wait(t, gg - 2)
                    for k4 in range(4):
                        ii = t.transpose(
                            ps_trs[gg % 2][:, k4 * 128:(k4 + 1) * 128],
                            lt[:, (g * 4 + k4) * 128:(g * 4 + k4 + 1) * 128], sh.ident[:])
                    trp.inc(ii)

    lstk.close()
    return [evc, nrmv, xld]


def emit_out_proj(kc, tag, sh, lhsT_dram, w_dram, resid_dram, out_dram, stk):
    nc = kc.nc
    attnT = kc.sb(f"{tag}_attnT", [128, 32 * 1024], BF16, stk)
    w_sb = kc.sb(f"{tag}_wsb", [128, 2 * 32 * 512], BF16, stk)
    xres = kc.sb(f"{tag}_xres", [128, 2 * 512], F32, stk)
    ostage = kc.sb(f"{tag}_ost", [128, 2 * 512], F32, stk)
    ps_y = kc.ps_(f"{tag}_psy", [128, 1024], F32, stk)

    hld = Stream(kc, parity=1, step=16, cls="h")
    wld = Stream(kc, parity=2, step=16, cls="s")
    xrl = Stream(kc, parity=2, step=16, cls="h")
    pe = Stream(kc)
    ev = Stream(kc)
    st = Stream(kc, parity=2, step=16, cls="h")

    def IT(th, dc, tt):
        return (th * 8 + dc) * 8 + tt

    with nc.Block() as blk:
        @blk.sync
        def _(sy):
            kc.wait_barrier(sy)
            for th in range(2):
                if th == 1:
                    pe.wait(sy, IT(0, 7, 7))
                hld.inc(sy.dma_start(
                    attnT.ap().rearrange("p (c t) -> p c t", c=32),
                    lhsT_dram.rearrange("(c p) t -> p c t", p=128)[:, :, th * 1024:(th + 1) * 1024]))
                for dc in range(8):
                    for tt in range(8):
                        it = IT(th, dc, tt)
                        ev.wait(sy, it - 2)
                        xrl.inc(sy.dma_start(
                            xres[:, (it % 2) * 512:(it % 2 + 1) * 512],
                            resid_dram[th * 1024 + tt * 128:th * 1024 + tt * 128 + 128,
                                       dc * 512:(dc + 1) * 512]))

        @blk.gpsimd
        def _(gp):
            kc.wait_barrier(gp)
            for th in range(2):
                for dc in range(8):
                    i = th * 8 + dc
                    if i >= 2:
                        pe.wait(gp, (i - 1) * 8 - 1)
                    wld.inc(gp.dma_start(
                        w_sb.ap().rearrange("p (b c d) -> p b c d", b=2, c=32)[:, i % 2, :, :],
                        w_dram.rearrange("(c p) d -> p c d", p=128)[:, :, dc * 512:(dc + 1) * 512]))

        @blk.tensor
        def _(t):
            kc.wait_barrier(t)
            for th in range(2):
                hld.wait(t, th)
                for dc in range(8):
                    i = th * 8 + dc
                    wld.wait(t, i)
                    for tt in range(8):
                        it = IT(th, dc, tt)
                        ev.wait(t, it - 2)
                        for cc in range(32):
                            ii = t.matmul(
                                ps_y[:, (it % 2) * 512:(it % 2 + 1) * 512],
                                attnT.ap().rearrange("p (c t) -> p c t", c=32)[:, cc,
                                                                               tt * 128:(tt + 1) * 128],
                                w_sb.ap().rearrange("p (b c d) -> p b c d", b=2, c=32)[:, i % 2, cc, :],
                                start=(cc == 0), stop=(cc == 31))
                        pe.inc(ii)

        @blk.vector
        def _(v):
            kc.wait_barrier(v)
            for th in range(2):
                for dc in range(8):
                    for tt in range(8):
                        it = IT(th, dc, tt)
                        pe.wait(v, it)
                        xrl.wait(v, it)
                        st.wait(v, it - 2)
                        ev.inc(v.tensor_tensor(ostage[:, (it % 2) * 512:(it % 2 + 1) * 512],
                                               ps_y[:, (it % 2) * 512:(it % 2 + 1) * 512],
                                               xres[:, (it % 2) * 512:(it % 2 + 1) * 512], ALU.add))

        @blk.scalar
        def _(s):
            kc.wait_barrier(s)
            for th in range(2):
                for dc in range(8):
                    for tt in range(8):
                        it = IT(th, dc, tt)
                        ev.wait(s, it)
                        row0 = th * 1024 + tt * 128
                        st.inc(s.dma_start(out_dram[row0:row0 + 128, dc * 512:(dc + 1) * 512],
                                           ostage[:, (it % 2) * 512:(it % 2 + 1) * 512]))
            st.wait_done(s)

    return [st, ev, xrl, hld]


def emit_ffn(kc, tag, sh, lnT_sb, resid_dram, out_dram, first_ffn, stk, final=False):
    nc = kc.nc
    wcl = Stream(kc, parity=1, step=16, per_unit=2, cls="s")
    wcp = Stream(kc)
    wce = Stream(kc)
    if first_ffn:
        wstk = ExitStack()
        wq_bf = kc.sb(f"{tag}_wqbf", [RD, C], BF16, wstk)
        keys_bf = kc.sb(f"{tag}_keysbf", [RD, NE], BF16, wstk)
        ps_wc = kc.ps_(f"{tag}_pswc", [128, 64], F32, wstk)
        with nc.Block() as blk:
            @blk.gpsimd
            def _(gp):
                wcl.inc(gp.dma_start(wq_bf[:], kc.io["Wq_raw"]))
                wcl.inc(gp.dma_start(keys_bf[:], kc.io["keysT"]))

            @blk.tensor
            def _(t):
                wcl.wait(t, 0)
                for cc in range(32):
                    wce.wait(t, cc - 1)
                    wcp.inc(t.matmul(ps_wc[:, 0:64], wq_bf[:, cc * 128:(cc + 1) * 128],
                                     keys_bf[:], start=True, stop=True))

            @blk.scalar
            def _(s):
                for cc in range(32):
                    wcp.wait(s, cc)
                    wce.inc(s.copy(sh.wcomb[:, cc * 64:(cc + 1) * 64], ps_wc[:, 0:64]))
        wstk.close()
    AT_sb = kc.sb(f"{tag}_AT", [128, 32 * 512], BF16, stk)
    BT_sb = kc.sb(f"{tag}_BT", [128, 4 * 4096], BF16, stk)
    zg_bf = kc.sb(f"{tag}_zg", [128, 512], BF16, stk)
    zgT_sb = kc.sb(f"{tag}_zgT", [128, 512], BF16, stk)
    gate = kc.sb(f"{tag}_gate", [128, 64], F32, stk)
    m1 = kc.sb(f"{tag}_m1", [128, 8], F32, stk)
    m2 = kc.sb(f"{tag}_m2", [128, 8], F32, stk)
    th1 = kc.sb(f"{tag}_th1", [128, 1], F32, stk)
    th2 = kc.sb(f"{tag}_th2", [128, 1], F32, stk)
    nmax = kc.sb(f"{tag}_nmax", [128, 1], F32, stk)
    s2b = kc.sb(f"{tag}_s2", [128, 64], F32, stk)
    esum = kc.sb(f"{tag}_es", [128, 1], F32, stk)
    erec = kc.sb(f"{tag}_er", [128, 1], F32, stk)
    sc_f = kc.sb(f"{tag}_scf", [128, 64], F32, stk)
    xres = kc.sb(f"{tag}_xres", [128, 2 * 512], F32, stk)
    ostage = kc.sb(f"{tag}_ost", [128, 2 * 512], F32, stk)
    ps_z = kc.ps_(f"{tag}_psz", [128, 1024], F32, stk)
    ps_sc0 = kc.ps_(f"{tag}_pssc0", [128, 64], F32, stk)
    ps_sc1 = kc.ps_(f"{tag}_pssc1", [128, 64], F32, stk)
    ps_scs = (ps_sc0, ps_sc1)
    ps_zt = kc.ps_(f"{tag}_pszt", [128, 512], BF16, stk)
    ps_y = kc.ps_(f"{tag}_psy", [128, 1024], F32, stk)

    wla = Stream(kc, parity=1, step=16, per_unit=2, cls="s")
    pez = Stream(kc)
    scv = Stream(kc)
    vch = Stream(kc)
    ace = Stream(kc)
    zgv = Stream(kc)
    ztr = Stream(kc)
    zev = Stream(kc)
    xrl = Stream(kc, parity=2, step=16, cls="h")
    pey = Stream(kc)
    evy = Stream(kc)
    sty = Stream(kc, parity=2, step=16, cls="h")

    def IT(tt, dc):
        return tt * 8 + dc

    with nc.Block() as blk:
        @blk.gpsimd
        def _(gp):
            wla.inc(gp.dma_start(AT_sb.ap().rearrange("p (c n) -> p c n", c=32),
                                 kc.io["AT"].rearrange("(c p) n -> p c n", p=128)))
            wla.inc(gp.dma_start(BT_sb.ap().rearrange("p (c d) -> p c d", c=4),
                                 kc.io["BT"].rearrange("(c p) d -> p c d", p=128)))

        @blk.tensor
        def _(t):
            wla.wait(t, 0)

            def bpart(pt):
                zev.wait(t, pt)
                for dc in range(8):
                    it = IT(pt, dc)
                    evy.wait(t, it - 2)
                    for nch in range(4):
                        ii = t.matmul(ps_y[:, (it % 2) * 512:(it % 2 + 1) * 512],
                                      zgT_sb[:, nch * 128:(nch + 1) * 128],
                                      BT_sb.ap().rearrange("p (c d) -> p c d", c=4)[:, nch,
                                                                                    dc * 512:(dc + 1) * 512],
                                      start=(nch == 0), stop=(nch == 3))
                    pey.inc(ii)

            for tt in range(16):
                zgv.wait(t, tt - 2)
                for cc in range(32):
                    t.matmul(ps_z[:, (tt % 2) * 512:(tt % 2 + 1) * 512],
                             lnT_sb.ap().rearrange("p (c t) -> p c t", c=32)[:, cc,
                                                                             tt * 128:(tt + 1) * 128],
                             AT_sb.ap().rearrange("p (c n) -> p c n", c=32)[:, cc, :],
                             start=(cc == 0), stop=(cc == 31))
                scv.wait(t, tt - 2)
                for cc in range(32):
                    ii = t.matmul(ps_scs[tt % 2][:],
                                  lnT_sb.ap().rearrange("p (c t) -> p c t", c=32)[:, cc,
                                                                                  tt * 128:(tt + 1) * 128],
                                  sh.wcomb[:, cc * 64:(cc + 1) * 64],
                                  start=(cc == 0), stop=(cc == 31))
                pez.inc(ii)
                if tt >= 1:
                    pt = tt - 1
                    zgv.wait(t, pt)
                    zev.wait(t, pt - 1)
                    for k4 in range(4):
                        ii = t.transpose(ps_zt[:, k4 * 128:(k4 + 1) * 128],
                                         zg_bf[:, k4 * 128:(k4 + 1) * 128],
                                         sh.ident[:])
                    ztr.inc(ii)
                if tt >= 2:
                    bpart(tt - 2)
            pt = 15
            zgv.wait(t, pt)
            zev.wait(t, pt - 1)
            for k4 in range(4):
                ii = t.transpose(ps_zt[:, k4 * 128:(k4 + 1) * 128],
                                 zg_bf[:, k4 * 128:(k4 + 1) * 128],
                                 sh.ident[:])
            ztr.inc(ii)
            bpart(14)
            bpart(15)

        @blk.scalar
        def _(s):
            for tt in range(16):
                vch.wait(s, tt * 9 + 3)
                ace.inc(s.activation(s2b[:], sc_f[:], AF.Exp, bias=nmax[:]))
                if tt >= 2:
                    for dc in range(8):
                        it = IT(tt - 2, dc)
                        evy.wait(s, it)
                        sty.inc(s.dma_start(out_dram[(tt - 2) * 128:(tt - 1) * 128,
                                                     dc * 512:(dc + 1) * 512],
                                            ostage[:, (it % 2) * 512:(it % 2 + 1) * 512]))
            for ft in (14, 15):
                for dc in range(8):
                    it = IT(ft, dc)
                    evy.wait(s, it)
                    sty.inc(s.dma_start(out_dram[ft * 128:(ft + 1) * 128, dc * 512:(dc + 1) * 512],
                                        ostage[:, (it % 2) * 512:(it % 2 + 1) * 512]))
            if final:
                sty.wait_done(s)

        @blk.vector
        def _(v):
            for tt in range(16):
                vb = tt * 9
                pez.wait(v, tt)
                ace.wait(v, tt - 1)
                scv.inc(v.tensor_copy(sc_f[:], ps_scs[tt % 2][:]))
                scf = sc_f[:]
                scv.wait(v, tt)
                vch.inc(v.max(m1[:], scf))
                vch.wait(v, vb + 0)
                vch.inc(v.tensor_reduce(th1[:], m1[:], AX.X, ALU.min))
                vch.inc(v.tensor_reduce(nmax[:], m1[:, 0:1], AX.X, ALU.max, negate=True))
                vch.wait(v, vb + 1)
                v.tensor_scalar(s2b[:], scf, th1[:], None, ALU.is_ge)
                v.scalar_tensor_tensor(s2b[:], s2b[:], -1e9, scf, ALU.mult, ALU.add)
                vch.inc(v.max(m2[:], s2b[:]))
                vch.wait(v, vb + 3)
                vch.inc(v.tensor_reduce(th2[:], m2[:], AX.X, ALU.min))
                vch.wait(v, vb + 4)
                ace.wait(v, tt)
                vch.inc(v.tensor_scalar(sc_f[:], scf, th2[:], None, ALU.is_ge))
                v.tensor_tensor(s2b[:], s2b[:], sc_f[:], ALU.mult)
                vch.inc(v.tensor_reduce(esum[:], s2b[:], AX.X, ALU.add))
                vch.wait(v, vb + 6)
                vch.inc(v.reciprocal(erec[:], esum[:]))
                vch.wait(v, vb + 7)
                vch.inc(v.tensor_scalar(gate[:], s2b[:], erec[:], None, ALU.mult))
                vch.wait(v, vb + 8)
                ztr.wait(v, tt - 1)
                gb = AP(gate.ap().tensor, gate.ap().offset, [[64, 128], [1, 64], [0, 8]])
                zin = ps_z.ap().rearrange("p (b e r) -> p b e r", b=2, r=8)[:, tt % 2, :, :]
                if tt >= 1:
                    ztr.wait(v, tt - 1)
                zout = zg_bf.ap().rearrange("p (e r) -> p e r", r=8)
                zgv.inc(v.tensor_tensor(zout, zin, gb, ALU.mult))
                if tt >= 2:
                    for dc in range(8):
                        it = IT(tt - 2, dc)
                        pey.wait(v, it)
                        xrl.wait(v, it)
                        sty.wait(v, it - 2)
                        evy.inc(v.scalar_tensor_tensor(ostage[:, (it % 2) * 512:(it % 2 + 1) * 512],
                                                       ps_y[:, (it % 2) * 512:(it % 2 + 1) * 512], 2.0,
                                                       xres[:, (it % 2) * 512:(it % 2 + 1) * 512],
                                                       ALU.mult, ALU.add))
                if tt >= 1:
                    pt = tt - 1
                    ztr.wait(v, pt)
                    pey.wait(v, IT(pt - 1, 7))
                    zev.inc(v.tensor_copy(zgT_sb[:], ps_zt[:]))
            for dc in range(8):
                it = IT(14, dc)
                pey.wait(v, it)
                xrl.wait(v, it)
                sty.wait(v, it - 2)
                evy.inc(v.scalar_tensor_tensor(ostage[:, (it % 2) * 512:(it % 2 + 1) * 512],
                                               ps_y[:, (it % 2) * 512:(it % 2 + 1) * 512], 2.0,
                                               xres[:, (it % 2) * 512:(it % 2 + 1) * 512],
                                               ALU.mult, ALU.add))
            ztr.wait(v, 15)
            pey.wait(v, IT(14, 7))
            zev.inc(v.tensor_copy(zgT_sb[:], ps_zt[:]))
            for dc in range(8):
                it = IT(15, dc)
                pey.wait(v, it)
                xrl.wait(v, it)
                sty.wait(v, it - 2)
                evy.inc(v.scalar_tensor_tensor(ostage[:, (it % 2) * 512:(it % 2 + 1) * 512],
                                               ps_y[:, (it % 2) * 512:(it % 2 + 1) * 512], 2.0,
                                               xres[:, (it % 2) * 512:(it % 2 + 1) * 512],
                                               ALU.mult, ALU.add))

        @blk.sync
        def _(sy):
            for tt in range(16):
                for dc in range(8):
                    it = IT(tt, dc)
                    evy.wait(sy, it - 2)
                    xrl.inc(sy.dma_start(xres[:, (it % 2) * 512:(it % 2 + 1) * 512],
                                         resid_dram[tt * 128:(tt + 1) * 128, dc * 512:(dc + 1) * 512]))

    return [sty, evy, xrl]


def build_kernel(detect_races=True):
    nc = bass.Bass("TRN2", target_bir_lowering=False, detect_race_conditions=detect_races)
    kc = KC(nc)

    def din(name, shape, dtype=F32):
        apx = nc.dram_tensor(name, shape, dtype, kind="ExternalInput").ap()
        kc.io[name] = apx
        return apx

    x_core = din("x_core", [TL, C])
    wq_sl = din("wq_sl", [C, HPC * HD])
    wk_sl = din("wk_sl", [C, HPC * HD])
    wv_sl = din("wv_sl", [C, HPC * HD])
    woT = din("woT", [C, C])
    wqkvT = din("wqkvT", [C, PB])
    wprojT = din("wprojT", [C, C])
    din("AT", [C, BOT])
    din("BT", [BOT, C])
    din("Wq_raw", [RD, C])
    din("keysT", [RD, NE])
    cosg = din("cosg", [HD, B * T])
    sing = din("sing", [HD, B * T])
    ident_in = din("ident_in", [128, 128])
    tri_in = din("tri_in", [128, 4 * 512])
    gidx_in = din("gidx_in", [128, 128], U32)

    sk = "ExternalOutput" if _EXPOSE else "Internal"

    def dsc(name, shape, dtype):
        return nc.dram_tensor(name, shape, dtype, kind=sk).ap()

    ag_ln_in_a = nc.dram_tensor("ag_ln_in_a", [C, TL // 2], BF16, kind="Internal").ap()
    ag_ln_in_b = nc.dram_tensor("ag_ln_in_b", [C, TL // 2], BF16, kind="Internal").ap()
    ag_ln_out_a = nc.dram_tensor("ag_ln_out_a", [NCORES * C, TL // 2], BF16, kind="Internal", addr_space="Shared").ap()
    ag_ln_out_b = nc.dram_tensor("ag_ln_out_b", [NCORES * C, TL // 2], BF16, kind="Internal", addr_space="Shared").ap()
    nrm_in = nc.dram_tensor("nrm_in", [1, 2], F32, kind="Internal").ap()
    nrm_out = nc.dram_tensor("nrm_out", [NCORES, 2], F32, kind="Internal").ap()
    attn_ag_in_a = nc.dram_tensor("attn_ag_in_a", [B * T, HPC * HD // 2], BF16, kind="Internal").ap()
    attn_ag_in_b = nc.dram_tensor("attn_ag_in_b", [B * T, HPC * HD // 2], BF16, kind="Internal").ap()
    attn_ag_out_a = nc.dram_tensor("attn_ag_out_a", [NCORES * B * T, HPC * HD // 2], BF16, kind="Internal", addr_space="Shared").ap()
    attn_ag_out_b = nc.dram_tensor("attn_ag_out_b", [NCORES * B * T, HPC * HD // 2], BF16, kind="Internal", addr_space="Shared").ap()
    qT_sc = dsc("qT_sc", [HPC * HD, B * T], BF16)
    kT_sc = dsc("kT_sc", [HPC * HD, B * T], BF16)
    v_sc = dsc("v_sc", [B * T, HPC * HD], BF16)
    attnT_loc = dsc("attnT_loc", [C, TL], BF16)
    x1_sc = dsc("x1_sc", [TL, C], F32)
    x2_sc = dsc("x2_sc", [TL, C], F32)
    x3_sc = dsc("x3_sc", [TL, C], F32)
    qkvb_sc = dsc("qkvb_sc", [TL, PB], BF16)
    attnbT_loc = dsc("attnbT_loc", [C, TL], BF16)
    out_y = nc.dram_tensor("out_y", [TL, C], F32, kind="ExternalOutput").ap()

    class SH:
        pass

    sh = SH()
    sh.ident = kc.sb("ident", [128, 128], BF16)
    sh.ones_c = kc.sb("ones_c", [128, 1], BF16)
    sh.ones_rf = kc.sb("ones_rf", [1, 128], F32)
    sh.negM = kc.sb("negM", [128, 1], F32)
    sh.wcomb = kc.sb("wcomb", [128, 32 * NE], BF16)

    boot = Stream(kc, step=16, cls="h")
    bootg = Stream(kc, step=16, cls="s")
    bootm = Stream(kc)
    with nc.Block() as blk:
        @blk.sync
        def _(sy):
            pass

        @blk.gpsimd
        def _(gp):
            bootg.inc(gp.dma_start(sh.ident[:], ident_in))
            gp.memset(sh.ones_c[:], 1.0)
            bootm.inc(gp.memset(sh.ones_rf[:], 1.0))

    kc.recycle()

    ln1_stk = ExitStack()
    lnT_sb = kc.sb("lnT1", [128, 32 * TL], BF16, ln1_stk)
    shp = Stream(kc, parity=2, step=16, cls="h")
    ag0 = Stream(kc, cls="p")

    def _ln1_sync_tail(sy, evc):
        evc.wait(sy, 63)
        shp.inc(sy.dma_start(ag_ln_in_a.rearrange("(c p) t -> p c t", p=128),
                             lnT_sb.ap().rearrange("p (c t) -> p c t", c=32)[:, :, 0:1024]))
        evc.wait(sy, 127)
        shp.inc(sy.dma_start(ag_ln_in_b.rearrange("(c p) t -> p c t", p=128),
                             lnT_sb.ap().rearrange("p (c t) -> p c t", c=32)[:, :, 1024:2048]))

    def _ln1_gp(gp):
        kc.wait_barrier(gp)
        shp.wait(gp, 0)
        ag0.inc(gp.collective_compute("AllGather", ALU.bypass,
                                      replica_groups=[list(range(NCORES))],
                                      ins=[ag_ln_in_a.opt()], outs=[ag_ln_out_a.opt()]))
        shp.wait(gp, 1)
        ag0.inc(gp.collective_compute("AllGather", ALU.bypass,
                                      replica_groups=[list(range(NCORES))],
                                      ins=[ag_ln_in_b.opt()], outs=[ag_ln_out_b.opt()]))

    emit_ln(kc, x_core, lnT_sb, sh, "ln1", ln1_stk, sync_tail=_ln1_sync_tail, gp_sec=_ln1_gp)

    ln1_stk.close()
    kc.recycle()

    s13 = ExitStack()
    w_qkv = kc.sb("w_qkv", [128, 32 * (3 * HPC * HD)], BF16, s13)
    rhs_sb = kc.sb("rhs_sb", [128, 2 * 32 * 512], BF16, s13)
    cs_sb = kc.sb("cs_sb", [128, 2 * 2 * 512], F32, s13)
    rot_sb = kc.sb("rot_sb", [128, 512], F32, s13)
    t1_sb = kc.sb("t1_sb", [128, 512], F32, s13)
    t2_sb = kc.sb("t2_sb", [128, 512], F32, s13)
    qk_bf = kc.sb("qk_bf", [128, 2 * 512], BF16, s13)
    sq_bf = kc.sb("sq_bf", [128, 2 * 512], BF16, s13)
    ntmp = kc.sb("ntmp", [1, 2], F32, s13)
    runmax = kc.sb("runmax", [1, 2], F32, s13)
    v_bf = kc.sb("v_bf", [128, 2 * 512], BF16, s13)
    v_stage = kc.sb("v_stage", [128, 2 * 512], BF16, s13)
    ps_mm = kc.ps_("ps_mm", [128, 1024], F32, s13)
    ps_n = kc.ps_("ps_n", [1, 1024], F32, s13)
    ps_vt0 = kc.ps_("ps_vt0", [128, 512], BF16, s13)
    ps_vt1 = kc.ps_("ps_vt1", [128, 512], BF16, s13)

    wl3 = Stream(kc, step=16, per_unit=3, cls="s")
    rl3 = Stream(kc, parity=2, step=16, per_unit=3, cls="h")
    pe3 = Stream(kc)
    rpq = Stream(kc)
    rpv = Stream(kc)
    sqr = Stream(kc)
    pnm = Stream(kc)
    nrd = Stream(kc)
    spl = Stream(kc, parity=2, step=16, cls="s")
    vtp = Stream(kc)
    vev = Stream(kc)
    vsp = Stream(kc, parity=2, step=16, cls="h")

    NDJ = 12
    TC_ORDER = [o * 4 + tkl for tkl in (0, 1) for o in range(NCORES)] + \
               [o * 4 + tkl for tkl in (2, 3) for o in range(NCORES)]

    def QI(ti, dj):
        return ti * 8 + dj

    def WI(ti, dj):
        return ti * 4 + (dj - 8)

    with nc.Block() as blk:
        @blk.gpsimd
        def _(gp):
            kc.wait_barrier(gp)
            for wi, wd in enumerate((wq_sl, wk_sl, wv_sl)):
                wl3.inc(gp.dma_start(
                    w_qkv.ap().rearrange("p (c d) -> p c d", c=32)[:, :, wi * 512:(wi + 1) * 512],
                    wd.rearrange("(c p) d -> p c d", p=128)))
            for ti, tc in enumerate(TC_ORDER):
                for dj in range(8):
                    q = QI(ti, dj)
                    rpq.wait(gp, q)
                    dst = qT_sc if dj < 4 else kT_sc
                    jj = dj % 4
                    spl.inc(gp.dma_start(dst[jj * 128:(jj + 1) * 128, tc * 512:(tc + 1) * 512],
                                         qk_bf[:, (q % 2) * 512:(q % 2 + 1) * 512]))

        @blk.sync
        def _(sy):
            kc.wait_barrier(sy)
            ag0.wait(sy, 0)

            def vspill(ti2, dj):
                tc2 = TC_ORDER[ti2]
                w = WI(ti2, dj)
                vev.wait(sy, w)
                vsp.inc(sy.dma_start(
                    v_sc.rearrange("(tc s p) f -> p tc s f", p=128, s=4)[:, tc2, :,
                                                                         (dj - 8) * 128:(dj - 7) * 128],
                    v_stage.ap().rearrange("p (b s f) -> p b s f", b=2, s=4)[:, w % 2, :, :]))

            for ti, tc in enumerate(TC_ORDER):
                o, tkl = tc // 4, tc % 4
                if tkl >= 2:
                    ag0.wait(sy, 1)
                pe3.wait(sy, (ti - 1) * NDJ - 1)
                agt = ag_ln_out_a if tkl < 2 else ag_ln_out_b
                rl3.inc(sy.dma_start(
                    rhs_sb.ap().rearrange("p (b c t) -> p b c t", b=2, c=32)[:, ti % 2, :, :],
                    agt.rearrange("(o c p) t -> p o c t", o=NCORES, p=128)[:, o, :,
                                                                           (tkl % 2) * 512:(tkl % 2 + 1) * 512]))
                rl3.inc(sy.dma_start(
                    cs_sb.ap().rearrange("p (b x t) -> p b x t", b=2, x=2)[:, ti % 2, 0, :],
                    cosg[:, tc * 512:(tc + 1) * 512]))
                rl3.inc(sy.dma_start(
                    cs_sb.ap().rearrange("p (b x t) -> p b x t", b=2, x=2)[:, ti % 2, 1, :],
                    sing[:, tc * 512:(tc + 1) * 512]))
                if ti >= 1:
                    for dj in range(8, NDJ):
                        vspill(ti - 1, dj)
            for dj in range(8, NDJ):
                vspill(NGT - 1, dj)

        @blk.tensor
        def _(t):
            kc.wait_barrier(t)
            wl3.wait(t, 0)
            for ti, tc in enumerate(TC_ORDER):
                rl3.wait(t, ti)
                for dj in range(NDJ):
                    g = ti * NDJ + dj
                    g2 = g - 2
                    if g2 >= 0:
                        dj2 = g2 % NDJ
                        if dj2 < 8:
                            rpq.wait(t, QI(g2 // NDJ, dj2))
                        else:
                            rpv.wait(t, WI(g2 // NDJ, dj2))
                    for cc in range(32):
                        ii = t.matmul(ps_mm[:, (g % 2) * 512:(g % 2 + 1) * 512],
                                      w_qkv.ap().rearrange("p (c d) -> p c d", c=32)[:, cc,
                                                                                     dj * 128:(dj + 1) * 128],
                                      rhs_sb.ap().rearrange("p (b c t) -> p b c t", b=2, c=32)[:, ti % 2, cc, :],
                                      start=(cc == 0), stop=(cc == 31))
                    pe3.inc(ii)
                    if dj < 8:
                        q = QI(ti, dj)
                        if q >= 1:
                            qq = q - 1
                            sqr.wait(t, qq)
                            nrd.wait(t, 2 * (qq - 2) + 1)
                            pnm.inc(t.matmul(ps_n[0:1, (qq % 2) * 512:(qq % 2 + 1) * 512],
                                             sh.ones_c[:], sq_bf[:, (qq % 2) * 512:(qq % 2 + 1) * 512],
                                             start=True, stop=True))
                    else:
                        w = WI(ti, dj)
                        if w >= 1:
                            ww = w - 1
                            rpv.wait(t, ww)
                            vev.wait(t, ww - 2)
                            pvt = ps_vt0 if ww % 2 == 0 else ps_vt1
                            for sub in range(4):
                                ii = t.transpose(pvt[:, sub * 128:(sub + 1) * 128],
                                                 v_bf[:, (ww % 2) * 512 + sub * 128:(ww % 2) * 512 + (sub + 1) * 128],
                                                 sh.ident[:])
                            vtp.inc(ii)
            qq = QI(NGT - 1, 7)
            sqr.wait(t, qq)
            nrd.wait(t, 2 * (qq - 2) + 1)
            pnm.inc(t.matmul(ps_n[0:1, (qq % 2) * 512:(qq % 2 + 1) * 512],
                             sh.ones_c[:], sq_bf[:, (qq % 2) * 512:(qq % 2 + 1) * 512],
                             start=True, stop=True))
            ww = WI(NGT - 1, 11)
            rpv.wait(t, ww)
            vev.wait(t, ww - 2)
            pvt = ps_vt0 if ww % 2 == 0 else ps_vt1
            for sub in range(4):
                ii = t.transpose(pvt[:, sub * 128:(sub + 1) * 128],
                                 v_bf[:, (ww % 2) * 512 + sub * 128:(ww % 2) * 512 + (sub + 1) * 128],
                                 sh.ident[:])
            vtp.inc(ii)

        @blk.vector
        def _(v):
            kc.wait_barrier(v)
            for ti, tc in enumerate(TC_ORDER):
                for dj in range(8):
                    g = ti * NDJ + dj
                    q = QI(ti, dj)
                    pe3.wait(v, g)
                    ps = ps_mm[:, (g % 2) * 512:(g % 2 + 1) * 512]
                    cos = cs_sb.ap().rearrange("p (b x t) -> p b x t", b=2, x=2)[:, ti % 2, 0, :]
                    sin = cs_sb.ap().rearrange("p (b x t) -> p b x t", b=2, x=2)[:, ti % 2, 1, :]
                    v.tensor_copy(rot_sb[0:64, :], ps[64:128, :])
                    v.tensor_copy(rot_sb[64:128, :], ps[0:64, :])
                    v.tensor_tensor(t1_sb[:], ps, cos, ALU.mult)
                    v.tensor_tensor(t2_sb[:], rot_sb[:], sin, ALU.mult)
                    spl.wait(v, q - 2)
                    sqr.wait(v, q - 2)
                    rpq.inc(v.tensor_tensor(qk_bf[:, (q % 2) * 512:(q % 2 + 1) * 512],
                                            t1_sb[:], t2_sb[:], ALU.add))
                    if q >= 1:
                        qq = q - 1
                        col = 0 if (qq % 8) < 4 else 1
                        pnm.wait(v, qq)
                        nrd.inc(v.tensor_reduce(ntmp[0:1, col:col + 1],
                                                ps_n[0:1, (qq % 2) * 512:(qq % 2 + 1) * 512],
                                                AX.X, ALU.max))
                        nrd.wait(v, 2 * qq)
                        if qq == 0:
                            ii = v.tensor_copy(runmax[0:1, 0:1], ntmp[0:1, 0:1])
                        elif qq == 4:
                            ii = v.tensor_copy(runmax[0:1, 1:2], ntmp[0:1, 1:2])
                        else:
                            ii = v.tensor_tensor(runmax[0:1, col:col + 1], runmax[0:1, col:col + 1],
                                                 ntmp[0:1, col:col + 1], ALU.max)
                        nrd.inc(ii)
                for dj in range(8, NDJ):
                    w = WI(ti, dj)
                    if w >= 1:
                        ww = w - 1
                        vtp.wait(v, ww)
                        vsp.wait(v, ww - 2)
                        pvt = ps_vt0 if ww % 2 == 0 else ps_vt1
                        vev.inc(v.tensor_copy(
                            v_stage.ap().rearrange("p (b f) -> p b f", b=2)[:, ww % 2, :], pvt[:]))
            qq = QI(NGT - 1, 7)
            pnm.wait(v, qq)
            nrd.inc(v.tensor_reduce(ntmp[0:1, 1:2], ps_n[0:1, (qq % 2) * 512:(qq % 2 + 1) * 512],
                                    AX.X, ALU.max))
            nrd.wait(v, 2 * qq)
            nrd.inc(v.tensor_tensor(runmax[0:1, 1:2], runmax[0:1, 1:2], ntmp[0:1, 1:2], ALU.max))
            ww = WI(NGT - 1, 11)
            vtp.wait(v, ww)
            vsp.wait(v, ww - 2)
            pvt = ps_vt0 if ww % 2 == 0 else ps_vt1
            vev.inc(v.tensor_copy(v_stage.ap().rearrange("p (b f) -> p b f", b=2)[:, ww % 2, :], pvt[:]))

        @blk.scalar
        def _(s):
            kc.wait_barrier(s)
            for ti, tc in enumerate(TC_ORDER):
                for dj in range(NDJ):
                    g = ti * NDJ + dj
                    if dj < 8:
                        q = QI(ti, dj)
                        rpq.wait(s, q)
                        pnm.wait(s, q - 2)
                        sqr.inc(s.activation(sq_bf[:, (q % 2) * 512:(q % 2 + 1) * 512],
                                             qk_bf[:, (q % 2) * 512:(q % 2 + 1) * 512], AF.Square))
                    else:
                        w = WI(ti, dj)
                        pe3.wait(s, g)
                        vtp.wait(s, w - 2)
                        rpv.inc(s.copy(v_bf[:, (w % 2) * 512:(w % 2 + 1) * 512],
                                       ps_mm[:, (g % 2) * 512:(g % 2 + 1) * 512]))

    nD = Stream(kc, cls="p")
    nA1 = Stream(kc, step=16, cls="s")
    nA2 = Stream(kc, step=16, cls="s")
    nCl = Stream(kc, cls="s")
    nrm8 = kc.sb("nrm8", [8, 2], F32, s13)
    gmax = kc.sb("gmax", [1, 2], F32, s13)
    lnm = kc.sb("lnm", [1, 2], F32, s13)
    msum = kc.sb("msum", [1, 1], F32, s13)
    negs = kc.sb("negs", [1, 1], F32, s13)
    ps_nm = kc.ps_("ps_nm", [128, 64], F32, s13)
    with nc.Block() as blk:
        @blk.gpsimd
        def _(gp):
            nrd.wait_done(gp)
            nA1.inc(gp.dma_start(nrm_in, runmax[0:1, :]))
            nA1.wait(gp, 0)
            nCl.inc(gp.collective_compute("AllGather", ALU.bypass,
                                          replica_groups=[list(range(NCORES))],
                                          ins=[nrm_in.opt()], outs=[nrm_out.opt()]))
            nCl.wait(gp, 0)
            nA2.inc(gp.dma_start(nrm8[:], nrm_out))
            nA2.wait(gp, 0)
            nD.inc(gp.tensor_reduce(gmax[0:1, :], nrm8[:], AX.C, ALU.max))

        @blk.scalar
        def _(s):
            nD.wait(s, 0)
            nD.inc(s.activation(lnm[:], gmax[:], AF.Ln))

        @blk.vector
        def _(v):
            nD.wait(v, 1)
            nD.inc(v.tensor_reduce(msum[:], lnm[:], AX.X, ALU.add))

        @blk.scalar
        def _(s):
            nD.wait(s, 2)
            nD.inc(s.activation(msum[:], msum[:], AF.Exp, scale=0.5))

        @blk.vector
        def _(v):
            nD.wait(v, 3)
            nD.inc(v.tensor_scalar(negs[:], msum[:], -SCALE_A, -1.0, ALU.mult, ALU.add))

        @blk.tensor
        def _(t):
            nD.wait(t, 4)
            nD.inc(t.matmul(ps_nm[:, 0:1], sh.ones_rf[:], negs[:], start=True, stop=True))

        @blk.vector
        def _(v):
            nD.wait(v, 5)
            nD.inc(v.tensor_copy(sh.negM[:], ps_nm[:, 0:1]))

    s13.close()
    kc.recycle()

    p4 = ExitStack()
    tri_sb = kc.sb("tri4", [128, 4 * 512], F32, p4)
    q_sb = kc.sb("q_sb", [128, 2 * 512], BF16, p4)
    k_sb = kc.sb("k_sb", [128, 2 * 4096], BF16, p4)
    v_sb4 = kc.sb("v_sb4", [128, 2 * 4096], BF16, p4)
    p_bf = kc.sb("p_bf", [128, 2 * 1024], BF16, p4)
    lnl = kc.sb("lnl", [1, 512], F32, p4)
    linv = kc.sb("linv", [1, 512], F32, p4)
    lcols = kc.sb("lcols", [128, 4], F32, p4)
    ou_bf = kc.sb("ou_bf", [128, 2 * 512], BF16, p4)
    o_st = kc.sb("o_st", [128, 2 * 512], BF16, p4)
    ps_s4 = kc.ps_("ps_s4", [128, 2048], F32, p4)
    ps_o4 = kc.ps_("ps_o4", [128, 512], F32, p4)
    ps_l4 = kc.ps_("ps_l4", [1, 512], F32, p4)
    ps_lt = kc.ps_("ps_lt", [128, 4], F32, p4)
    ps_t4 = kc.ps_("ps_t4", [128, 512], BF16, p4)

    g_sb = kc.sb("g_sb", [128, 2 * 2048], BF16, p4)
    g_st = kc.sb("g_st", [128, 2 * 2048], BF16, p4)
    sh.gidx = kc.sb("gidx", [128, 128], U32, p4)
    ag2 = Stream(kc, cls="s")
    trl = Stream(kc, step=16, per_unit=2, cls="h")
    jld = Stream(kc, parity=2, step=16, per_unit=5, cls="h")
    qld = Stream(kc, parity=2, step=16, cls="h")
    pes = Stream(kc)
    msk = Stream(kc)
    exq = Stream(kc)
    pvl = Stream(kc)
    lnl1 = Stream(kc)
    fle = Stream(kc)
    oe1 = Stream(kc)
    ltr = Stream(kc)
    lcp = Stream(kc)
    otr = Stream(kc)
    nrm4 = Stream(kc)
    osp = Stream(kc, parity=2, step=16, per_unit=2, cls="h")

    NKT = {}
    NFIRST = {}
    DFIRST = {}
    PFIRST = {}
    n_ctr = 0
    d_ctr = 0
    p_ctr = 0
    for jb in range(16):
        for qb in range(8):
            u = jb * 8 + qb
            NFIRST[u] = n_ctr
            NKT[u] = 4 * (qb + 1)
            DFIRST[u] = d_ctr
            PFIRST[u] = p_ctr
            n_ctr += NKT[u]
            d_ctr += 4
            p_ctr += NKT[u] // 2

    def emit_otr(t, u):
        fle.wait(t, u)
        lcp.wait(t, u - 1)
        for sub in range(4):
            ii = t.transpose(ps_lt[:, sub:sub + 1], linv[0:1, sub * 128:(sub + 1) * 128], sh.ones_rf[0:1, 0:1])
        ltr.inc(ii)
        oe1.wait(t, u)
        nrm4.wait(t, u - 1)
        for sub in range(4):
            ii = t.transpose(ps_t4[:, sub * 128:(sub + 1) * 128],
                             ou_bf[:, (u % 2) * 512 + sub * 128:(u % 2) * 512 + (sub + 1) * 128],
                             sh.ident[:])
        otr.inc(ii)

    def ospill(sy, u):
        jb2, qb2 = u // 8, u % 8
        j2, b2 = jb2 // 4, jb2 % 4
        nrm4.wait(sy, u)
        tgt = attn_ag_in_a if j2 < 2 else attn_ag_in_b
        jj2 = j2 % 2
        for s_lo in range(2):
            osp.inc(sy.dma_start(
                AP(tgt.tensor, qb2 * 256 * 2048 + s_lo * 128 * 2048 + b2 * 512 + jj2 * 128,
                   [[2048, 128], [256, 2], [1, 128]]),
                AP(o_st.ap().tensor, (u % 2) * 512 + s_lo * 128, [[1024, 128], [256, 2], [1, 128]])))

    with nc.Block() as blk:
        @blk.gpsimd
        def _(gp):
            osp.wait(gp, 63)
            ag2.inc(gp.collective_compute("AllGather", ALU.bypass,
                                          replica_groups=[list(range(NCORES))],
                                          ins=[attn_ag_in_a.opt()], outs=[attn_ag_out_a.opt()]))
            osp.wait(gp, 126)
            osp.wait(gp, 127)
            ag2.inc(gp.collective_compute("AllGather", ALU.bypass,
                                          replica_groups=[list(range(NCORES))],
                                          ins=[attn_ag_in_b.opt()], outs=[attn_ag_out_b.opt()]))

        @blk.sync
        def _(sy):
            kc.wait_barrier(sy)
            nD.wait(sy, 6)
            trl.inc(sy.dma_start(tri_sb[:], tri_in))
            trl.inc(sy.dma_start(sh.gidx[:], gidx_in))
            for jb in range(16):
                j, b = jb // 4, jb % 4
                if jb >= 2:
                    u2 = (jb - 2) * 8 + 7
                    pvl.wait(sy, NFIRST[u2] + NKT[u2] - 1)
                jld.inc(sy.dma_start(
                    AP(k_sb.ap().tensor, (jb % 2) * 4096, [[2 * 4096, 128], [512, 8], [1, 512]]),
                    AP(kT_sc.tensor, (j * 128) * (B * T) + b * 512, [[B * T, 128], [2048, 8], [1, 512]])))
                for sub in range(4):
                    jld.inc(sy.dma_start(
                        AP(v_sb4.ap().tensor, (jb % 2) * 4096 + sub * 128,
                           [[2 * 4096, 128], [512, 8], [1, 128]]),
                        AP(v_sc.tensor, (b * 512 + sub * 128) * (HPC * HD) + j * 128,
                           [[HPC * HD, 128], [2048 * HPC * HD, 8], [1, 128]])))
                for qb in range(8):
                    u = jb * 8 + qb
                    if u >= 2:
                        u2 = u - 2
                        pes.wait(sy, NFIRST[u2] + NKT[u2] - 1)
                    qld.inc(sy.dma_start(
                        q_sb[:, (u % 2) * 512:(u % 2 + 1) * 512],
                        qT_sc[j * 128:(j + 1) * 128, qb * 2048 + b * 512:qb * 2048 + b * 512 + 512]))
                    if u >= 2:
                        ospill(sy, u - 2)
            ospill(sy, 126)
            ospill(sy, 127)

        @blk.tensor
        def _(t):
            kc.wait_barrier(t)
            nD.wait(t, 6)
            for jb in range(16):
                jld.wait(t, jb)
                for qb in range(8):
                    u = jb * 8 + qb
                    qld.wait(t, u)
                    L = NKT[u]
                    n0 = NFIRST[u]
                    M = L // 2
                    m0 = PFIRST[u]
                    for m in range(M + 1):
                        if m == 1:
                            oe1.wait(t, u - 1)
                            lnl1.wait(t, u - 1)
                        if m < M:
                            exq.wait(t, 2 * (m0 + m - 2) + 1)
                            for t2 in range(2):
                                n = n0 + 2 * m + t2
                                kt = 2 * m + t2
                                pes.inc(t.matmul(ps_s4[:, ((m % 2) * 2 + t2) * 512:((m % 2) * 2 + t2 + 1) * 512],
                                                 k_sb[:, (jb % 2) * 4096 + kt * 128:(jb % 2) * 4096 + (kt + 1) * 128],
                                                 q_sb[:, (u % 2) * 512:(u % 2 + 1) * 512],
                                                 start=True, stop=True))
                        if m >= 1:
                            for t2 in range(2):
                                exq.wait(t, 2 * (m0 + m - 1) + t2)
                                kt = 2 * (m - 1) + t2
                                n = n0 + kt
                                t.matmul(ps_o4[:],
                                         v_sb4[:, (jb % 2) * 4096 + kt * 128:(jb % 2) * 4096 + (kt + 1) * 128],
                                         p_bf[:, ((m - 1) % 2) * 1024 + t2 * 512:((m - 1) % 2) * 1024 + (t2 + 1) * 512],
                                         start=(kt == 0), stop=(kt == L - 1))
                                pvl.inc(t.matmul(ps_l4[0:1, 0:512],
                                                 sh.ones_c[:],
                                                 p_bf[:, ((m - 1) % 2) * 1024 + t2 * 512:((m - 1) % 2) * 1024 + (t2 + 1) * 512],
                                                 start=(kt == 0), stop=(kt == L - 1)))
                    if u >= 1:
                        emit_otr(t, u - 1)
            emit_otr(t, 127)

        @blk.vector
        def _(v):
            kc.wait_barrier(v)
            trl.wait(v, 0)

            def vfin(u):
                ltr.wait(v, u)
                lcp.inc(v.tensor_copy(lcols[:], ps_lt[:]))
                otr.wait(v, u)
                lcp.wait(v, u)
                osp.wait(v, u - 2)
                for sub in range(4):
                    ii = v.tensor_scalar(o_st[:, (u % 2) * 512 + sub * 128:(u % 2) * 512 + (sub + 1) * 128],
                                         ps_t4[:, sub * 128:(sub + 1) * 128],
                                         lcols[:, sub:sub + 1], None, ALU.mult)
                nrm4.inc(ii)

            def oevac(u):
                pvl.wait(v, NFIRST[u] + NKT[u] - 1)
                otr.wait(v, u - 2)
                oe1.inc(v.tensor_copy(ou_bf[:, (u % 2) * 512:(u % 2 + 1) * 512], ps_o4[:]))

            for jb in range(16):
                for qb in range(8):
                    u = jb * 8 + qb
                    L = NKT[u]
                    n0 = NFIRST[u]
                    if u >= 1:
                        oevac(u - 1)
                    for kt in range(4 * qb, L):
                        n = n0 + kt
                        pes.wait(v, n)
                        dd = kt - 4 * qb
                        mloc = (kt // 2) % 2
                        t2 = kt % 2
                        sl = (mloc * 2 + t2) * 512
                        msk.inc(v.tensor_tensor(ps_s4[:, sl:sl + 512],
                                                ps_s4[:, sl:sl + 512],
                                                tri_sb[:, dd * 512:(dd + 1) * 512], ALU.add))
                    if u >= 1:
                        vfin(u - 1)
            oevac(127)
            vfin(127)

        @blk.scalar
        def _(s):
            kc.wait_barrier(s)
            nD.wait(s, 6)
            for jb in range(16):
                for qb in range(8):
                    u = jb * 8 + qb
                    L = NKT[u]
                    n0 = NFIRST[u]
                    M = L // 2
                    m0 = PFIRST[u]
                    for m in range(M):
                        for t2 in range(2):
                            kt = 2 * m + t2
                            if kt >= 4 * qb:
                                msk.wait(s, DFIRST[u] + (kt - 4 * qb))
                            else:
                                pes.wait(s, n0 + kt)
                            pvl.wait(s, n0 + 2 * (m - 2) + t2)
                            exq.inc(s.activation(
                                p_bf[:, (m % 2) * 1024 + t2 * 512:(m % 2) * 1024 + (t2 + 1) * 512],
                                ps_s4[:, (m % 2) * 1024 + t2 * 512:(m % 2) * 1024 + (t2 + 1) * 512],
                                AF.Exp, bias=sh.negM[:], scale=SCALE_A))
                    pvl.wait(s, n0 + L - 1)
                    lnl1.inc(s.activation(lnl[0:1, :], ps_l4[0:1, 0:512], AF.Ln))
                    lnl1.wait(s, u)
                    fle.inc(s.activation(linv[0:1, :], lnl[0:1, :], AF.Exp, scale=-1.0))



    p6streams = []
    prev_gev = None
    for half, (agt, agwait) in enumerate(((attn_ag_out_a, 0), (attn_ag_out_b, 1))):
        gix = Stream(kc, parity=2, step=16, cls="s")
        gtp = Stream(kc)
        gev = Stream(kc)
        gsp = Stream(kc, parity=2, step=16, per_unit=2, cls="h")
        agt_w = agt.rearrange("(a b) f -> a (b f)", b=8)
        pg = prev_gev
        with nc.Block() as blk:
            @blk.gpsimd
            def _(gp):
                osp.wait_done(gp)
                trl.wait(gp, 0)
                ag2.wait(gp, agwait)
                for g in range(16):
                    gtp.wait(gp, (g - 2) * 4 + 3)
                    gix.inc(gp.indirect_dma_start(
                        g_sb[:, (g % 2) * 2048:(g % 2) * 2048 + 2048], None,
                        agt_w, IndirectOffsetOnAxis(sh.gidx[:, g:g + 1], 0)))

            @blk.tensor
            def _(t):
                nrm4.wait(t, 127)
                if pg is not None:
                    pg.wait(t, 15 * 4 + 3)
                for g in range(16):
                    gix.wait(t, g)
                    for q in range(4):
                        bidx = g * 4 + q
                        gev.wait(t, bidx - 1)
                        for kk in range(4 * q, 4 * q + 4):
                            ii = t.transpose(
                                ps_t4[:, (kk % 4) * 128:(kk % 4 + 1) * 128],
                                g_sb[:, (g % 2) * 2048 + kk * 128:(g % 2) * 2048 + (kk + 1) * 128],
                                sh.ident[:])
                        gtp.inc(ii)

            @blk.scalar
            def _(s):
                for g in range(16):
                    gsp.wait(s, g - 2)
                    for q in range(4):
                        bidx = g * 4 + q
                        gtp.wait(s, bidx)
                        gev.inc(s.copy(g_st[:, (g % 2) * 2048 + q * 512:(g % 2) * 2048 + (q + 1) * 512],
                                       ps_t4[:, 0:512]))

            @blk.sync
            def _(sy):
                for g in range(16):
                    op, v = g // 2, g % 2
                    gev.wait(sy, g * 4 + 3)
                    for sub in range(2):
                        gsp.inc(sy.dma_start(
                            AP(attnT_loc.tensor, (4 * op + 2 * half + sub) * 128 * TL + v * 128,
                               [[TL, 128], [256, 8], [1, 128]]),
                            AP(g_st.ap().tensor, (g % 2) * 2048 + sub * 128,
                               [[4096, 128], [256, 8], [1, 128]])))
        p6streams += [gsp, gev]
        prev_gev = gev

    p4.close()
    kc.recycle()

    p7 = ExitStack()
    emit_out_proj(kc, "op_a", sh, attnT_loc, woT, x_core, x1_sc, p7)
    p7.close()
    kc.recycle()

    s89 = ExitStack()
    lnT2 = kc.sb("lnT2", [128, 32 * TL], BF16, s89)
    emit_ln(kc, x1_sc, lnT2, sh, "ln2", s89)
    emit_ffn(kc, "ffn1", sh, lnT2, x1_sc, x2_sc, True, s89)
    s89.close()
    kc.recycle()

    s1011 = ExitStack()
    lnT3 = kc.sb("lnT3", [128, 32 * TL], BF16, s1011)
    emit_ln(kc, x2_sc, lnT3, sh, "ln3", s1011)

    wb_sb = kc.sb("wb_sb", [128, 2 * 32 * 512], BF16, s1011)
    qb_stage = kc.sb("qb_stage", [128, 2 * 512], BF16, s1011)
    ps_qb = kc.ps_("ps_qb", [128, 1024], F32, s1011)
    wlb11 = Stream(kc, parity=2, step=16, cls="s")
    pe11 = Stream(kc)
    ev11 = Stream(kc)
    st11 = Stream(kc, parity=2, step=16, cls="h")
    with nc.Block() as blk:
        @blk.gpsimd
        def _(gp):
            kc.wait_barrier(gp)
            for pc in range(12):
                pe11.wait(gp, (pc - 1) * 16 - 1)
                wlb11.inc(gp.dma_start(
                    wb_sb.ap().rearrange("p (b c d) -> p b c d", b=2, c=32)[:, pc % 2, :, :],
                    wqkvT.rearrange("(c p) d -> p c d", p=128)[:, :, pc * 512:(pc + 1) * 512]))

        @blk.tensor
        def _(t):
            kc.wait_barrier(t)
            for pc in range(12):
                wlb11.wait(t, pc)
                for tt in range(16):
                    it = pc * 16 + tt
                    ev11.wait(t, it - 2)
                    for cc in range(32):
                        ii = t.matmul(ps_qb[:, (it % 2) * 512:(it % 2 + 1) * 512],
                                      lnT3.ap().rearrange("p (c t) -> p c t", c=32)[:, cc,
                                                                                    tt * 128:(tt + 1) * 128],
                                      wb_sb.ap().rearrange("p (b c d) -> p b c d", b=2, c=32)[:, pc % 2, cc, :],
                                      start=(cc == 0), stop=(cc == 31))
                    pe11.inc(ii)

        @blk.scalar
        def _(s):
            kc.wait_barrier(s)
            for pc in range(12):
                for tt in range(16):
                    it = pc * 16 + tt
                    pe11.wait(s, it)
                    st11.wait(s, it - 2)
                    ev11.inc(s.copy(qb_stage[:, (it % 2) * 512:(it % 2 + 1) * 512],
                                    ps_qb[:, (it % 2) * 512:(it % 2 + 1) * 512]))

        @blk.sync
        def _(sy):
            kc.wait_barrier(sy)
            for pc in range(12):
                for tt in range(16):
                    it = pc * 16 + tt
                    ev11.wait(sy, it)
                    st11.inc(sy.dma_start(qkvb_sc[tt * 128:(tt + 1) * 128, pc * 512:(pc + 1) * 512],
                                          qb_stage[:, (it % 2) * 512:(it % 2 + 1) * 512]))

    s1011.close()
    kc.recycle()

    p12 = ExitStack()
    qb_sb = kc.sb("qb_sb", [128, 4 * PB], BF16, p12)
    tmp12 = kc.sb("tmp12", [128, 1024], F32, p12)
    sA = kc.sb("sA", [128, 10 * 32], F32, p12)
    mA = kc.sb("mA", [128, 32], F32, p12)
    eA = kc.sb("eA", [128, 10 * 32], F32, p12)
    pA = kc.sb("pA", [128, 10 * 32], F32, p12)
    sumA = kc.sb("sumA", [128, 32], F32, p12)
    recA = kc.sb("recA", [128, 4 * 32], F32, p12)
    oacc = kc.sb("oacc", [128, C], F32, p12)
    ob_bf = kc.sb("ob_bf", [128, 2 * C], BF16, p12)
    ob_st = kc.sb("ob_st", [128, 2 * 512], BF16, p12)
    ps_tb = kc.ps_("ps_tb", [128, 512], BF16, p12)

    PAIRS = [(i, jb) for i in range(4) for jb in range(i + 1)]
    PIDX = {p: ix for ix, p in enumerate(PAIRS)}

    qbl = Stream(kc, parity=1, step=16, per_unit=4, cls="h")
    v12 = Stream(kc)
    vck = Stream(kc)
    a12 = Stream(kc)
    obc = Stream(kc)
    tbt = Stream(kc)
    ebt = Stream(kc)
    sbb = Stream(kc, parity=2, step=16, cls="h")

    def CK(ut, k):
        return ut * 9 + k

    with nc.Block() as blk:
        @blk.sync
        def _(sy):
            kc.wait_barrier(sy)

            def bspill(w):
                ut2, i2 = w // 4, w % 4
                for grp in range(8):
                    ebt.wait(sy, w * 8 + grp)
                    sbb.inc(sy.dma_start(
                        attnbT_loc.rearrange("(c p) t -> p c t", p=128)[:, grp * 4:(grp + 1) * 4,
                                                                        i2 * 512 + ut2 * 128:i2 * 512 + ut2 * 128 + 128],
                        ob_st.ap().rearrange("p (b s f) -> p b s f", b=2, s=4)[:, (w * 8 + grp) % 2, :, :]))

            for ut in range(4):
                if ut >= 1:
                    for w in range((ut - 1) * 4, ut * 4):
                        bspill(w)
                v12.wait(sy, CK(ut - 1, 8))
                for b in range(4):
                    qbl.inc(sy.dma_start(
                        AP(qb_sb.ap().tensor, b * PB, [[4 * PB, 128], [1, PB]]),
                        qkvb_sc[b * 512 + ut * 128:b * 512 + ut * 128 + 128, :]))
            for w in range(12, 16):
                bspill(w)

        @blk.vector
        def _(v):
            kc.wait_barrier(v)

            def edge(ii):
                vck.inc(ii)
                vck.wait(v, vck.issued - 1)
                return ii

            for ut in range(4):
                qbl.wait(v, ut)
                for (i, jb) in PAIRS:
                    px = PIDX[(i, jb)]
                    for gq in range(4):
                        v.tensor_tensor(
                            tmp12[:],
                            AP(qb_sb.ap().tensor, i * PB + gq * 2 * 768,
                               [[4 * PB, 128], [768, 2], [128, 4], [1, 128]]),
                            AP(qb_sb.ap().tensor, jb * PB + gq * 2 * 768 + 512,
                               [[4 * PB, 128], [768, 2], [0, 4], [1, 128]]),
                            ALU.mult)
                        ii = v.tensor_reduce(
                            AP(sA.ap().tensor, px * 32 + gq * 8,
                               [[10 * 32, 128], [4, 2], [1, 4]]),
                            tmp12.ap().rearrange("p (g q f) -> p g q f", g=2, q=4),
                            AX.X, ALU.add)
                v12.inc(ii)
                v12.wait(v, CK(ut, 0))
                for i in range(4):
                    msl = mA[:]
                    edge(v.tensor_copy(msl, sA[:, PIDX[(i, 0)] * 32:PIDX[(i, 0)] * 32 + 32]))
                    for jb in range(1, i + 1):
                        edge(v.tensor_tensor(msl, msl, sA[:, PIDX[(i, jb)] * 32:PIDX[(i, jb)] * 32 + 32],
                                             ALU.max))
                    for jb in range(i + 1):
                        px = PIDX[(i, jb)]
                        if i == 3 and jb == 3:
                            ii = v.tensor_tensor(eA[:, px * 32:px * 32 + 32],
                                                 sA[:, px * 32:px * 32 + 32], msl, ALU.subtract)
                        else:
                            edge(v.tensor_tensor(eA[:, px * 32:px * 32 + 32],
                                                 sA[:, px * 32:px * 32 + 32], msl, ALU.subtract))
                v12.inc(ii)
                a12.wait(v, ut)
                for i in range(4):
                    ssl = sumA[:]
                    edge(v.tensor_copy(ssl, pA[:, PIDX[(i, 0)] * 32:PIDX[(i, 0)] * 32 + 32]))
                    for jb in range(1, i + 1):
                        edge(v.tensor_tensor(ssl, ssl, pA[:, PIDX[(i, jb)] * 32:PIDX[(i, jb)] * 32 + 32],
                                             ALU.add))
                    if i == 3:
                        ii = v.tensor_copy(recA[:, i * 32:(i + 1) * 32], ssl)
                    else:
                        edge(v.tensor_copy(recA[:, i * 32:(i + 1) * 32], ssl))
                v12.inc(ii)
                v12.wait(v, CK(ut, 2))
                ii = v.reciprocal(recA[:], recA[:])
                v12.inc(ii)
                v12.wait(v, CK(ut, 3))
                for ix, (i, jb) in enumerate(PAIRS):
                    px = PIDX[(i, jb)]
                    if ix == len(PAIRS) - 1:
                        ii = v.tensor_tensor(pA[:, px * 32:px * 32 + 32], pA[:, px * 32:px * 32 + 32],
                                             recA[:, i * 32:(i + 1) * 32], ALU.mult)
                    else:
                        edge(v.tensor_tensor(pA[:, px * 32:px * 32 + 32], pA[:, px * 32:px * 32 + 32],
                                             recA[:, i * 32:(i + 1) * 32], ALU.mult))
                v12.inc(ii)
                v12.wait(v, CK(ut, 4))
                for i in range(4):
                    w = ut * 4 + i
                    obc.wait(v, w - 1)
                    for jb in range(i + 1):
                        px = PIDX[(i, jb)]
                        for gq in range(4):
                            pap = AP(pA.ap().tensor, px * 32 + gq * 8,
                                     [[10 * 32, 128], [4, 2], [1, 4], [0, 128]])
                            vap = AP(qb_sb.ap().tensor, jb * PB + gq * 2 * 768 + 640,
                                     [[4 * PB, 128], [768, 2], [0, 4], [1, 128]])
                            oap = AP(oacc.ap().tensor, gq * 1024,
                                     [[C, 128], [512, 2], [128, 4], [1, 128]])
                            if jb == 0:
                                ii = v.tensor_tensor(oap, pap, vap, ALU.mult)
                            else:
                                v.tensor_tensor(tmp12[:], pap, vap, ALU.mult)
                                ii = v.tensor_tensor(
                                    oap, oap,
                                    tmp12.ap().rearrange("p (g q f) -> p g q f", g=2, q=4),
                                    ALU.add)
                    v12.inc(ii)

        @blk.scalar
        def _(s):
            kc.wait_barrier(s)
            for ut in range(4):
                v12.wait(s, CK(ut, 1))
                a12.inc(s.activation(pA[:], eA[:], AF.Exp, scale=SCALE_B))
                for i in range(4):
                    w = ut * 4 + i
                    v12.wait(s, CK(ut, 5 + i))
                    tbt.wait(s, (w - 2) * 8 + 7)
                    obc.inc(s.copy(ob_bf[:, (w % 2) * C:(w % 2 + 1) * C], oacc[:]))
                    for grp in range(8):
                        tbt.wait(s, w * 8 + grp)
                        sbb.wait(s, w * 8 + grp - 2)
                        ebt.inc(s.copy(ob_st[:, ((w * 8 + grp) % 2) * 512:((w * 8 + grp) % 2 + 1) * 512],
                                       ps_tb[:]))

        @blk.tensor
        def _(t):
            kc.wait_barrier(t)
            for w in range(16):
                obc.wait(t, w)
                for grp in range(8):
                    ebt.wait(t, w * 8 + grp - 1)
                    for k4 in range(4):
                        cc = grp * 4 + k4
                        ii = t.transpose(ps_tb[:, k4 * 128:(k4 + 1) * 128],
                                         ob_bf[:, (w % 2) * C + cc * 128:(w % 2) * C + (cc + 1) * 128],
                                         sh.ident[:])
                    tbt.inc(ii)

    p12.close()
    kc.recycle()

    p13 = ExitStack()
    emit_out_proj(kc, "op_b", sh, attnbT_loc, wprojT, x2_sc, x3_sc, p13)
    p13.close()
    kc.recycle()

    s1415 = ExitStack()
    lnT4 = kc.sb("lnT4", [128, 32 * TL], BF16, s1415)
    emit_ln(kc, x3_sc, lnT4, sh, "ln4", s1415)
    emit_ffn(kc, "ffn2", sh, lnT4, x3_sc, out_y, False, s1415, final=True)
    s1415.close()

    kc.stack.close()
    return nc


_ROPE_CACHE = {}


def _host_tables():
    if "t" in _ROPE_CACHE:
        return _ROPE_CACHE["t"]
    pos = np.empty(B * T, dtype=np.float64)
    for o in range(NCORES):
        for tl in range(TL):
            pos[o * TL + tl] = 512 * o + (tl % 512)
    thetas = 1.0 / 10000.0 ** (np.arange(0, HD, 2, dtype=np.float32).astype(np.float64) / HD)
    pt = np.outer(pos, thetas)
    pt = np.concatenate([pt, pt], axis=-1)
    cosg = np.cos(pt).astype(np.float32).T.copy()
    sinf = np.sin(pt).astype(np.float32).T
    sing = sinf.copy()
    sing[:64, :] *= -1.0
    sing = sing.copy()
    tri = np.zeros((128, 4 * 512), dtype=np.float32)
    for d in range(4):
        kk = np.arange(128)[:, None]
        uu = np.arange(512)[None, :]
        tri[:, d * 512:(d + 1) * 512] = np.where(d * 128 + kk <= uu, 0.0, -1e30)
    ident = np.eye(128, dtype=np.float32)
    _ROPE_CACHE["t"] = (cosg, sing, tri, ident)
    return _ROPE_CACHE["t"]


_NC_CACHE = {}
_LAST_RES = None


def kernel(L0_attn_a_x, L0_attn_a_wq, L0_attn_a_wk, L0_attn_a_wv, L0_attn_a_wo,
           L0_attn_b_weight_qkv, L0_attn_b_weight_proj,
           L0_ffn_A_w, L0_ffn_B_w, L0_ffn_Wq_w, L0_ffn_keys):
    x = np.asarray(L0_attn_a_x, dtype=np.float32)
    cosg, sing, tri, ident = _host_tables()
    wqT = np.ascontiguousarray(np.asarray(L0_attn_a_wq, np.float32).T)
    wkT = np.ascontiguousarray(np.asarray(L0_attn_a_wk, np.float32).T)
    wvT = np.ascontiguousarray(np.asarray(L0_attn_a_wv, np.float32).T)
    woT = np.ascontiguousarray(np.asarray(L0_attn_a_wo, np.float32).T)
    wqkvT = np.ascontiguousarray(np.asarray(L0_attn_b_weight_qkv, np.float32).T)
    wprojT = np.ascontiguousarray(np.asarray(L0_attn_b_weight_proj, np.float32).T)
    AT = np.ascontiguousarray(np.asarray(L0_ffn_A_w, np.float32).T)
    BT = np.ascontiguousarray(np.asarray(L0_ffn_B_w, np.float32).T)
    Wq_raw = np.ascontiguousarray(np.asarray(L0_ffn_Wq_w, np.float32))
    keysT = np.ascontiguousarray(np.asarray(L0_ffn_keys, np.float32).T)

    if "nc" not in _NC_CACHE:
        _NC_CACHE["nc"] = build_kernel()
    nc = _NC_CACHE["nc"]

    in_maps = []
    for r in range(NCORES):
        x_core = np.ascontiguousarray(x[:, 512 * r:512 * (r + 1), :].reshape(TL, C))
        gidx = np.zeros((128, 128), dtype=np.uint32)
        for op in range(NCORES):
            for v in range(2):
                gidx[:, op * 2 + v] = op * 2048 + 256 * r + v * 128 + np.arange(128)
        in_maps.append(dict(
            x_core=x_core,
            wq_sl=np.ascontiguousarray(wqT[:, 512 * r:512 * (r + 1)]),
            wk_sl=np.ascontiguousarray(wkT[:, 512 * r:512 * (r + 1)]),
            wv_sl=np.ascontiguousarray(wvT[:, 512 * r:512 * (r + 1)]),
            woT=woT, wqkvT=wqkvT, wprojT=wprojT,
            AT=AT, BT=BT, Wq_raw=Wq_raw, keysT=keysT,
            cosg=cosg, sing=sing, ident_in=ident, tri_in=tri, gidx_in=gidx,
        ))

    res = run_bass_kernel_spmd(nc, in_maps, core_ids=list(range(NCORES)))
    global _LAST_RES
    _LAST_RES = res
    out = np.empty((B, T, C), dtype=np.float32)
    for r in range(NCORES):
        out[:, 512 * r:512 * (r + 1), :] = res.results[r]["out_y"].reshape(B, 512, C)
    return out


# version 20
# speedup vs baseline: 1.0453x; 1.0015x over previous
import numpy as np
import ml_dtypes
import concourse.bass as bass
import concourse.mybir as mybir
from concourse.bass import IndirectOffsetOnAxis
from concourse.bass_utils import run_bass_kernel_spmd
from contextlib import ExitStack

F32 = mybir.dt.float32
BF16 = mybir.dt.bfloat16
U32 = mybir.dt.uint32
AF = mybir.ActivationFunctionType
ALU = mybir.AluOpType
AX = mybir.AxisListType
AP = bass.AP

NCORES = 8
B, T, C = 4, 4096, 4096
TL = 2048
HD = 128
HPC = 4
NGT = 32
SCALE_A = float(1.0 / np.sqrt(HD))
SCALE_B = float(1.0 / np.sqrt(HD))
PB = 6144
NE, RANK, RD = 64, 8, 16
BOT = NE * RANK
EPS = 1e-5

_EXPOSE = False


class PSem:
    __slots__ = ("h", "n")

    def __init__(self, h):
        self.h = h
        self.n = 0


class Stream:
    def __init__(self, kc, parity=1, step=1, per_unit=1, cls="c"):
        self.ps = [kc.take_sem(cls) for _ in range(parity)]
        if cls != "p":
            kc.reg_stream(self)
        self.base = [p.n for p in self.ps]
        self.parity = parity
        self.step = step
        self.per_unit = per_unit
        self.issued = 0

    def inc(self, instr):
        g = self.issued // self.per_unit
        p = self.ps[g % self.parity]
        instr.then_inc(p.h, self.step)
        p.n += self.step
        self.issued += 1

    def wait(self, eng, g):
        if g < 0:
            return
        p = self.ps[g % self.parity]
        cnt = self.base[g % self.parity] + self.step * self.per_unit * (g // self.parity + 1)
        eng.wait_ge(p.h, cnt)

    def wait_done(self, eng):
        for p in self.ps:
            if p.n > 0:
                eng.wait_ge(p.h, p.n)


class KC:

    def __init__(self, nc):
        self.nc = nc
        self.stack = ExitStack()
        allsems = [PSem(self.stack.enter_context(nc.semaphore(f"s{i}"))) for i in range(92)]
        self.phase_sem = allsems[0]
        self.dummy = self.stack.enter_context(nc.sbuf_tensor("rcy_dummy", [1, 4], mybir.dt.float32))
        self.pin_pool = allsems[1:7]
        self.free = {"c": allsems[7:40], "h": allsems[40:65], "s": allsems[65:]}
        self.used = {"c": [], "h": []}
        self.live = []
        self.phase_n = 0
        self.io = {}

    def take_sem(self, cls="c"):
        if cls == "p":
            assert self.pin_pool, "out of pinned sems"
            return self.pin_pool.pop()
        if cls == "s":
            assert self.free["s"], "out of s sems"
            return self.free["s"].pop()
        assert self.free[cls], f"out of {cls} sems in phase"
        p = self.free[cls].pop()
        self.used[cls].append(p)
        return p

    def reg_stream(self, st):
        self.live.append(st)

    def recycle(self, extra_waits=()):
        for cls in ("c", "h"):
            self.free[cls] = self.used[cls] + self.free[cls]
        self.used = {"c": [], "h": []}
        self.live = []

    def sb(self, name, shape, dtype, stack=None):
        return (stack or self.stack).enter_context(self.nc.sbuf_tensor(name, shape, dtype))

    def ps_(self, name, shape, dtype, stack=None):
        return (stack or self.stack).enter_context(self.nc.psum_tensor(name, shape, dtype))

    def wait_barrier(self, eng):
        pass


def emit_ln(kc, x_src, lnT_sb, sh, tag, stk, sync_tail=None, gp_sec=None):
    nc = kc.nc
    lstk = ExitStack()
    sh.xin = kc.sb(f"{tag}_xin", [128, 2 * C], F32, lstk)
    sh.lntok = kc.sb(f"{tag}_lntok", [128, 2 * C], BF16, lstk)
    sh.stats = kc.sb(f"{tag}_stats", [128, 48], F32, lstk)
    sh.mv = kc.sb(f"{tag}_mv", [128, 2], F32, lstk)
    sh.veps = kc.sb(f"{tag}_veps", [128, 1], F32, lstk)
    sh.rstd = kc.sb(f"{tag}_rstd", [128, 1], F32, lstk)
    xld = Stream(kc, parity=2, step=16, cls="h")
    agg = Stream(kc)
    vep = Stream(kc)
    acl = Stream(kc)
    acr = Stream(kc)
    nrmv = Stream(kc)
    trp = Stream(kc)
    evc = Stream(kc)
    ps_tr0 = kc.ps_(f"{tag}_pstr0", [128, 512], BF16, lstk)
    ps_tr1 = kc.ps_(f"{tag}_pstr1", [128, 512], BF16, lstk)
    ps_trs = (ps_tr0, ps_tr1)

    with nc.Block() as blk:
        @blk.sync
        def _(sy):
            kc.wait_barrier(sy)
            for tt in range(16):
                nrmv.wait(sy, tt - 2)
                xld.inc(sy.dma_start(sh.xin[:, (tt % 2) * C:(tt % 2 + 1) * C],
                                     x_src[tt * 128:(tt + 1) * 128, :]))
            if sync_tail is not None:
                sync_tail(sy, evc)

        @blk.gpsimd
        def _(gp):
            if gp_sec is not None:
                gp_sec(gp)

        @blk.vector
        def _(v):
            kc.wait_barrier(v)
            for tt in range(16):
                xld.wait(v, tt)
                xt = sh.xin[:, (tt % 2) * C:(tt % 2 + 1) * C]
                for cch in range(8):
                    v.bn_stats(sh.stats[:, 6 * cch:6 * cch + 6], xt[:, 512 * cch:512 * (cch + 1)])
                agg.inc(v.bn_aggr(sh.mv[:], sh.stats[:]))
                agg.wait(v, tt)
                vep.inc(v.tensor_scalar(sh.veps[:], sh.mv[:, 1:2], EPS, None, ALU.add))
                acr.wait(v, tt)
                trp.wait(v, (tt - 2) * 8 + 7)
                nrmv.inc(v.tensor_scalar(sh.lntok[:, (tt % 2) * C:(tt % 2 + 1) * C], xt,
                                         sh.mv[:, 0:1], sh.rstd[:], ALU.subtract, ALU.mult))

        @blk.scalar
        def _(s):
            kc.wait_barrier(s)
            for tt in range(16):
                vep.wait(s, tt)
                acl.inc(s.activation(sh.rstd[:], sh.veps[:], AF.Ln))
                acl.wait(s, tt)
                acr.inc(s.activation(sh.rstd[:], sh.rstd[:], AF.Exp, scale=-0.5))
                for g in range(8):
                    gg = tt * 8 + g
                    trp.wait(s, gg)
                    dst = lnT_sb.ap().rearrange("p (c t) -> p c t", c=32)[:, g * 4:(g + 1) * 4,
                                                                          tt * 128:(tt + 1) * 128]
                    src = ps_trs[gg % 2].ap().rearrange("p (k f) -> p k f", k=4)
                    evc.inc(s.copy(dst, src))

        @blk.tensor
        def _(t):
            kc.wait_barrier(t)
            for tt in range(16):
                nrmv.wait(t, tt)
                lt = sh.lntok[:, (tt % 2) * C:(tt % 2 + 1) * C]
                for g in range(8):
                    gg = tt * 8 + g
                    evc.wait(t, gg - 2)
                    for k4 in range(4):
                        ii = t.transpose(
                            ps_trs[gg % 2][:, k4 * 128:(k4 + 1) * 128],
                            lt[:, (g * 4 + k4) * 128:(g * 4 + k4 + 1) * 128], sh.ident[:])
                    trp.inc(ii)

    lstk.close()
    return [evc, nrmv, xld]


def emit_out_proj(kc, tag, sh, lhsT_dram, w_dram, resid_dram, out_dram, stk):
    nc = kc.nc
    attnT = kc.sb(f"{tag}_attnT", [128, 32 * 1024], BF16, stk)
    w_sb = kc.sb(f"{tag}_wsb", [128, 2 * 32 * 512], BF16, stk)
    xres = kc.sb(f"{tag}_xres", [128, 2 * 512], F32, stk)
    ostage = kc.sb(f"{tag}_ost", [128, 2 * 512], F32, stk)
    ps_y = kc.ps_(f"{tag}_psy", [128, 1024], F32, stk)

    hld = Stream(kc, parity=1, step=16, cls="h")
    wld = Stream(kc, parity=2, step=16, cls="s")
    xrl = Stream(kc, parity=2, step=16, cls="h")
    pe = Stream(kc)
    ev = Stream(kc)
    st = Stream(kc, parity=2, step=16, cls="h")

    def IT(th, dc, tt):
        return (th * 8 + dc) * 8 + tt

    with nc.Block() as blk:
        @blk.sync
        def _(sy):
            kc.wait_barrier(sy)
            for th in range(2):
                if th == 1:
                    pe.wait(sy, IT(0, 7, 7))
                hld.inc(sy.dma_start(
                    attnT.ap().rearrange("p (c t) -> p c t", c=32),
                    lhsT_dram.rearrange("(c p) t -> p c t", p=128)[:, :, th * 1024:(th + 1) * 1024]))
                for dc in range(8):
                    for tt in range(8):
                        it = IT(th, dc, tt)
                        ev.wait(sy, it - 2)
                        xrl.inc(sy.dma_start(
                            xres[:, (it % 2) * 512:(it % 2 + 1) * 512],
                            resid_dram[th * 1024 + tt * 128:th * 1024 + tt * 128 + 128,
                                       dc * 512:(dc + 1) * 512]))

        @blk.gpsimd
        def _(gp):
            kc.wait_barrier(gp)
            for th in range(2):
                for dc in range(8):
                    i = th * 8 + dc
                    if i >= 2:
                        pe.wait(gp, (i - 1) * 8 - 1)
                    wld.inc(gp.dma_start(
                        w_sb.ap().rearrange("p (b c d) -> p b c d", b=2, c=32)[:, i % 2, :, :],
                        w_dram.rearrange("(c p) d -> p c d", p=128)[:, :, dc * 512:(dc + 1) * 512]))

        @blk.tensor
        def _(t):
            kc.wait_barrier(t)
            for th in range(2):
                hld.wait(t, th)
                for dc in range(8):
                    i = th * 8 + dc
                    wld.wait(t, i)
                    for tt in range(8):
                        it = IT(th, dc, tt)
                        ev.wait(t, it - 2)
                        for cc in range(32):
                            ii = t.matmul(
                                ps_y[:, (it % 2) * 512:(it % 2 + 1) * 512],
                                attnT.ap().rearrange("p (c t) -> p c t", c=32)[:, cc,
                                                                               tt * 128:(tt + 1) * 128],
                                w_sb.ap().rearrange("p (b c d) -> p b c d", b=2, c=32)[:, i % 2, cc, :],
                                start=(cc == 0), stop=(cc == 31))
                        pe.inc(ii)

        @blk.vector
        def _(v):
            kc.wait_barrier(v)
            for th in range(2):
                for dc in range(8):
                    for tt in range(8):
                        it = IT(th, dc, tt)
                        pe.wait(v, it)
                        xrl.wait(v, it)
                        st.wait(v, it - 2)
                        ev.inc(v.tensor_tensor(ostage[:, (it % 2) * 512:(it % 2 + 1) * 512],
                                               ps_y[:, (it % 2) * 512:(it % 2 + 1) * 512],
                                               xres[:, (it % 2) * 512:(it % 2 + 1) * 512], ALU.add))

        @blk.scalar
        def _(s):
            kc.wait_barrier(s)
            for th in range(2):
                for dc in range(8):
                    for tt in range(8):
                        it = IT(th, dc, tt)
                        ev.wait(s, it)
                        row0 = th * 1024 + tt * 128
                        st.inc(s.dma_start(out_dram[row0:row0 + 128, dc * 512:(dc + 1) * 512],
                                           ostage[:, (it % 2) * 512:(it % 2 + 1) * 512]))
            st.wait_done(s)

    return [st, ev, xrl, hld]


def emit_ffn(kc, tag, sh, lnT_sb, resid_dram, out_dram, first_ffn, stk, final=False):
    nc = kc.nc
    wcl = Stream(kc, parity=1, step=16, per_unit=2, cls="s")
    wcp = Stream(kc)
    wce = Stream(kc)
    if first_ffn:
        wstk = ExitStack()
        wq_bf = kc.sb(f"{tag}_wqbf", [RD, C], BF16, wstk)
        keys_bf = kc.sb(f"{tag}_keysbf", [RD, NE], BF16, wstk)
        ps_wc = kc.ps_(f"{tag}_pswc", [128, 64], F32, wstk)
        with nc.Block() as blk:
            @blk.gpsimd
            def _(gp):
                wcl.inc(gp.dma_start(wq_bf[:], kc.io["Wq_raw"]))
                wcl.inc(gp.dma_start(keys_bf[:], kc.io["keysT"]))

            @blk.tensor
            def _(t):
                wcl.wait(t, 0)
                for cc in range(32):
                    wce.wait(t, cc - 1)
                    wcp.inc(t.matmul(ps_wc[:, 0:64], wq_bf[:, cc * 128:(cc + 1) * 128],
                                     keys_bf[:], start=True, stop=True))

            @blk.scalar
            def _(s):
                for cc in range(32):
                    wcp.wait(s, cc)
                    wce.inc(s.copy(sh.wcomb[:, cc * 64:(cc + 1) * 64], ps_wc[:, 0:64]))
        wstk.close()
    AT_sb = kc.sb(f"{tag}_AT", [128, 32 * 512], BF16, stk)
    BT_sb = kc.sb(f"{tag}_BT", [128, 4 * 4096], BF16, stk)
    zg_bf = kc.sb(f"{tag}_zg", [128, 512], BF16, stk)
    zgT_sb = kc.sb(f"{tag}_zgT", [128, 512], BF16, stk)
    gate = kc.sb(f"{tag}_gate", [128, 64], F32, stk)
    m1 = kc.sb(f"{tag}_m1", [128, 8], F32, stk)
    m2 = kc.sb(f"{tag}_m2", [128, 8], F32, stk)
    th1 = kc.sb(f"{tag}_th1", [128, 1], F32, stk)
    th2 = kc.sb(f"{tag}_th2", [128, 1], F32, stk)
    nmax = kc.sb(f"{tag}_nmax", [128, 1], F32, stk)
    s2b = kc.sb(f"{tag}_s2", [128, 64], F32, stk)
    esum = kc.sb(f"{tag}_es", [128, 1], F32, stk)
    erec = kc.sb(f"{tag}_er", [128, 1], F32, stk)
    sc_f = kc.sb(f"{tag}_scf", [128, 64], F32, stk)
    xres = kc.sb(f"{tag}_xres", [128, 2 * 512], F32, stk)
    ostage = kc.sb(f"{tag}_ost", [128, 2 * 512], F32, stk)
    ps_z = kc.ps_(f"{tag}_psz", [128, 1024], F32, stk)
    ps_sc0 = kc.ps_(f"{tag}_pssc0", [128, 64], F32, stk)
    ps_sc1 = kc.ps_(f"{tag}_pssc1", [128, 64], F32, stk)
    ps_scs = (ps_sc0, ps_sc1)
    ps_zt = kc.ps_(f"{tag}_pszt", [128, 512], BF16, stk)
    ps_y = kc.ps_(f"{tag}_psy", [128, 1024], F32, stk)

    wla = Stream(kc, parity=1, step=16, per_unit=2, cls="s")
    pez = Stream(kc)
    scv = Stream(kc)
    vch = Stream(kc)
    ace = Stream(kc)
    zgv = Stream(kc)
    ztr = Stream(kc)
    zev = Stream(kc)
    xrl = Stream(kc, parity=2, step=16, cls="h")
    pey = Stream(kc)
    evy = Stream(kc)
    sty = Stream(kc, parity=2, step=16, cls="h")

    def IT(tt, dc):
        return tt * 8 + dc

    with nc.Block() as blk:
        @blk.gpsimd
        def _(gp):
            wla.inc(gp.dma_start(AT_sb.ap().rearrange("p (c n) -> p c n", c=32),
                                 kc.io["AT"].rearrange("(c p) n -> p c n", p=128)))
            wla.inc(gp.dma_start(BT_sb.ap().rearrange("p (c d) -> p c d", c=4),
                                 kc.io["BT"].rearrange("(c p) d -> p c d", p=128)))

        @blk.tensor
        def _(t):
            wla.wait(t, 0)

            def bpart(pt):
                zev.wait(t, pt)
                for dc in range(8):
                    it = IT(pt, dc)
                    evy.wait(t, it - 2)
                    for nch in range(4):
                        ii = t.matmul(ps_y[:, (it % 2) * 512:(it % 2 + 1) * 512],
                                      zgT_sb[:, nch * 128:(nch + 1) * 128],
                                      BT_sb.ap().rearrange("p (c d) -> p c d", c=4)[:, nch,
                                                                                    dc * 512:(dc + 1) * 512],
                                      start=(nch == 0), stop=(nch == 3))
                    pey.inc(ii)

            for tt in range(16):
                zgv.wait(t, tt - 2)
                for cc in range(32):
                    t.matmul(ps_z[:, (tt % 2) * 512:(tt % 2 + 1) * 512],
                             lnT_sb.ap().rearrange("p (c t) -> p c t", c=32)[:, cc,
                                                                             tt * 128:(tt + 1) * 128],
                             AT_sb.ap().rearrange("p (c n) -> p c n", c=32)[:, cc, :],
                             start=(cc == 0), stop=(cc == 31))
                scv.wait(t, tt - 2)
                for cc in range(32):
                    ii = t.matmul(ps_scs[tt % 2][:],
                                  lnT_sb.ap().rearrange("p (c t) -> p c t", c=32)[:, cc,
                                                                                  tt * 128:(tt + 1) * 128],
                                  sh.wcomb[:, cc * 64:(cc + 1) * 64],
                                  start=(cc == 0), stop=(cc == 31))
                pez.inc(ii)
                if tt >= 1:
                    pt = tt - 1
                    zgv.wait(t, pt)
                    zev.wait(t, pt - 1)
                    for k4 in range(4):
                        ii = t.transpose(ps_zt[:, k4 * 128:(k4 + 1) * 128],
                                         zg_bf[:, k4 * 128:(k4 + 1) * 128],
                                         sh.ident[:])
                    ztr.inc(ii)
                if tt >= 2:
                    bpart(tt - 2)
            pt = 15
            zgv.wait(t, pt)
            zev.wait(t, pt - 1)
            for k4 in range(4):
                ii = t.transpose(ps_zt[:, k4 * 128:(k4 + 1) * 128],
                                 zg_bf[:, k4 * 128:(k4 + 1) * 128],
                                 sh.ident[:])
            ztr.inc(ii)
            bpart(14)
            bpart(15)

        @blk.scalar
        def _(s):
            for tt in range(16):
                vch.wait(s, tt * 9 + 3)
                ace.inc(s.activation(s2b[:], sc_f[:], AF.Exp, bias=nmax[:]))
                if tt >= 2:
                    for dc in range(8):
                        it = IT(tt - 2, dc)
                        evy.wait(s, it)
                        sty.inc(s.dma_start(out_dram[(tt - 2) * 128:(tt - 1) * 128,
                                                     dc * 512:(dc + 1) * 512],
                                            ostage[:, (it % 2) * 512:(it % 2 + 1) * 512]))
            for ft in (14, 15):
                for dc in range(8):
                    it = IT(ft, dc)
                    evy.wait(s, it)
                    sty.inc(s.dma_start(out_dram[ft * 128:(ft + 1) * 128, dc * 512:(dc + 1) * 512],
                                        ostage[:, (it % 2) * 512:(it % 2 + 1) * 512]))
            if final:
                sty.wait_done(s)

        @blk.vector
        def _(v):
            for tt in range(16):
                vb = tt * 9
                pez.wait(v, tt)
                ace.wait(v, tt - 1)
                scv.inc(v.tensor_copy(sc_f[:], ps_scs[tt % 2][:]))
                scf = sc_f[:]
                scv.wait(v, tt)
                vch.inc(v.max(m1[:], scf))
                vch.wait(v, vb + 0)
                vch.inc(v.tensor_reduce(th1[:], m1[:], AX.X, ALU.min))
                vch.inc(v.tensor_reduce(nmax[:], m1[:, 0:1], AX.X, ALU.max, negate=True))
                vch.wait(v, vb + 1)
                v.tensor_scalar(s2b[:], scf, th1[:], None, ALU.is_ge)
                v.scalar_tensor_tensor(s2b[:], s2b[:], -1e9, scf, ALU.mult, ALU.add)
                vch.inc(v.max(m2[:], s2b[:]))
                vch.wait(v, vb + 3)
                vch.inc(v.tensor_reduce(th2[:], m2[:], AX.X, ALU.min))
                vch.wait(v, vb + 4)
                ace.wait(v, tt)
                vch.inc(v.tensor_scalar(sc_f[:], scf, th2[:], None, ALU.is_ge))
                v.tensor_tensor(s2b[:], s2b[:], sc_f[:], ALU.mult)
                vch.inc(v.tensor_reduce(esum[:], s2b[:], AX.X, ALU.add))
                vch.wait(v, vb + 6)
                vch.inc(v.reciprocal(erec[:], esum[:]))
                vch.wait(v, vb + 7)
                vch.inc(v.tensor_scalar(gate[:], s2b[:], erec[:], None, ALU.mult))
                vch.wait(v, vb + 8)
                ztr.wait(v, tt - 1)
                gb = AP(gate.ap().tensor, gate.ap().offset, [[64, 128], [1, 64], [0, 8]])
                zin = ps_z.ap().rearrange("p (b e r) -> p b e r", b=2, r=8)[:, tt % 2, :, :]
                if tt >= 1:
                    ztr.wait(v, tt - 1)
                zout = zg_bf.ap().rearrange("p (e r) -> p e r", r=8)
                zgv.inc(v.tensor_tensor(zout, zin, gb, ALU.mult))
                if tt >= 2:
                    for dc in range(8):
                        it = IT(tt - 2, dc)
                        pey.wait(v, it)
                        xrl.wait(v, it)
                        sty.wait(v, it - 2)
                        evy.inc(v.scalar_tensor_tensor(ostage[:, (it % 2) * 512:(it % 2 + 1) * 512],
                                                       ps_y[:, (it % 2) * 512:(it % 2 + 1) * 512], 2.0,
                                                       xres[:, (it % 2) * 512:(it % 2 + 1) * 512],
                                                       ALU.mult, ALU.add))
                if tt >= 1:
                    pt = tt - 1
                    ztr.wait(v, pt)
                    pey.wait(v, IT(pt - 1, 7))
                    zev.inc(v.tensor_copy(zgT_sb[:], ps_zt[:]))
            for dc in range(8):
                it = IT(14, dc)
                pey.wait(v, it)
                xrl.wait(v, it)
                sty.wait(v, it - 2)
                evy.inc(v.scalar_tensor_tensor(ostage[:, (it % 2) * 512:(it % 2 + 1) * 512],
                                               ps_y[:, (it % 2) * 512:(it % 2 + 1) * 512], 2.0,
                                               xres[:, (it % 2) * 512:(it % 2 + 1) * 512],
                                               ALU.mult, ALU.add))
            ztr.wait(v, 15)
            pey.wait(v, IT(14, 7))
            zev.inc(v.tensor_copy(zgT_sb[:], ps_zt[:]))
            for dc in range(8):
                it = IT(15, dc)
                pey.wait(v, it)
                xrl.wait(v, it)
                sty.wait(v, it - 2)
                evy.inc(v.scalar_tensor_tensor(ostage[:, (it % 2) * 512:(it % 2 + 1) * 512],
                                               ps_y[:, (it % 2) * 512:(it % 2 + 1) * 512], 2.0,
                                               xres[:, (it % 2) * 512:(it % 2 + 1) * 512],
                                               ALU.mult, ALU.add))

        @blk.sync
        def _(sy):
            for tt in range(16):
                for dc in range(8):
                    it = IT(tt, dc)
                    evy.wait(sy, it - 2)
                    xrl.inc(sy.dma_start(xres[:, (it % 2) * 512:(it % 2 + 1) * 512],
                                         resid_dram[tt * 128:(tt + 1) * 128, dc * 512:(dc + 1) * 512]))

    return [sty, evy, xrl]


def build_kernel(detect_races=True):
    nc = bass.Bass("TRN2", target_bir_lowering=False, detect_race_conditions=detect_races)
    kc = KC(nc)

    def din(name, shape, dtype=F32):
        apx = nc.dram_tensor(name, shape, dtype, kind="ExternalInput").ap()
        kc.io[name] = apx
        return apx

    x_core = din("x_core", [TL, C])
    wq_sl = din("wq_sl", [C, HPC * HD])
    wk_sl = din("wk_sl", [C, HPC * HD])
    wv_sl = din("wv_sl", [C, HPC * HD])
    woT = din("woT", [C, C])
    wqkvT = din("wqkvT", [C, PB])
    wprojT = din("wprojT", [C, C])
    din("AT", [C, BOT])
    din("BT", [BOT, C])
    din("Wq_raw", [RD, C])
    din("keysT", [RD, NE])
    cosg = din("cosg", [HD, B * T])
    sing = din("sing", [HD, B * T])
    ident_in = din("ident_in", [128, 128])
    tri_in = din("tri_in", [128, 4 * 512])
    gidx_in = din("gidx_in", [128, 128], U32)

    sk = "ExternalOutput" if _EXPOSE else "Internal"

    def dsc(name, shape, dtype):
        return nc.dram_tensor(name, shape, dtype, kind=sk).ap()

    ag_ln_in_q = [nc.dram_tensor(f"ag_ln_in_q{k}", [C, TL // 4], BF16, kind="Internal").ap()
                  for k in range(4)]
    ag_ln_out_q = [nc.dram_tensor(f"ag_ln_out_q{k}", [NCORES * C, TL // 4], BF16,
                                  kind="Internal", addr_space="Shared").ap()
                   for k in range(4)]
    nrm_in = nc.dram_tensor("nrm_in", [1, 2], F32, kind="Internal").ap()
    nrm_out = nc.dram_tensor("nrm_out", [NCORES, 2], F32, kind="Internal").ap()
    attn_ag_in_a = nc.dram_tensor("attn_ag_in_a", [B * T, HPC * HD // 2], BF16, kind="Internal").ap()
    attn_ag_in_b = nc.dram_tensor("attn_ag_in_b", [B * T, HPC * HD // 2], BF16, kind="Internal").ap()
    attn_ag_out_a = nc.dram_tensor("attn_ag_out_a", [NCORES * B * T, HPC * HD // 2], BF16, kind="Internal", addr_space="Shared").ap()
    attn_ag_out_b = nc.dram_tensor("attn_ag_out_b", [NCORES * B * T, HPC * HD // 2], BF16, kind="Internal", addr_space="Shared").ap()
    qT_sc = dsc("qT_sc", [HPC * HD, B * T], BF16)
    kT_sc = dsc("kT_sc", [HPC * HD, B * T], BF16)
    v_sc = dsc("v_sc", [B * T, HPC * HD], BF16)
    attnT_loc = dsc("attnT_loc", [C, TL], BF16)
    x1_sc = dsc("x1_sc", [TL, C], F32)
    x2_sc = dsc("x2_sc", [TL, C], F32)
    x3_sc = dsc("x3_sc", [TL, C], F32)
    qkvb_sc = dsc("qkvb_sc", [TL, PB], BF16)
    attnbT_loc = dsc("attnbT_loc", [C, TL], BF16)
    out_y = nc.dram_tensor("out_y", [TL, C], F32, kind="ExternalOutput").ap()

    class SH:
        pass

    sh = SH()
    sh.ident = kc.sb("ident", [128, 128], BF16)
    sh.ones_c = kc.sb("ones_c", [128, 1], BF16)
    sh.ones_rf = kc.sb("ones_rf", [1, 128], F32)
    sh.negM = kc.sb("negM", [128, 1], F32)
    sh.wcomb = kc.sb("wcomb", [128, 32 * NE], BF16)

    boot = Stream(kc, step=16, cls="h")
    bootg = Stream(kc, step=16, cls="s")
    bootm = Stream(kc)
    with nc.Block() as blk:
        @blk.sync
        def _(sy):
            pass

        @blk.gpsimd
        def _(gp):
            bootg.inc(gp.dma_start(sh.ident[:], ident_in))
            gp.memset(sh.ones_c[:], 1.0)
            bootm.inc(gp.memset(sh.ones_rf[:], 1.0))

    kc.recycle()

    ln1_stk = ExitStack()
    lnT_sb = kc.sb("lnT1", [128, 32 * TL], BF16, ln1_stk)
    shp = Stream(kc, parity=4, step=16, cls="h")
    ag0 = Stream(kc, cls="p")

    def _ln1_sync_tail(sy, evc):
        for k in range(4):
            evc.wait(sy, 32 * k + 31)
            shp.inc(sy.dma_start(ag_ln_in_q[k].rearrange("(c p) t -> p c t", p=128),
                                 lnT_sb.ap().rearrange("p (c t) -> p c t", c=32)[:, :,
                                                                                 512 * k:512 * (k + 1)]))

    def _ln1_gp(gp):
        kc.wait_barrier(gp)
        for k in range(4):
            shp.wait(gp, k)
            ag0.inc(gp.collective_compute("AllGather", ALU.bypass,
                                          replica_groups=[list(range(NCORES))],
                                          ins=[ag_ln_in_q[k].opt()], outs=[ag_ln_out_q[k].opt()]))

    emit_ln(kc, x_core, lnT_sb, sh, "ln1", ln1_stk, sync_tail=_ln1_sync_tail, gp_sec=_ln1_gp)

    ln1_stk.close()
    kc.recycle()

    s13 = ExitStack()
    w_qkv = kc.sb("w_qkv", [128, 32 * (3 * HPC * HD)], BF16, s13)
    rhs_sb = kc.sb("rhs_sb", [128, 2 * 32 * 512], BF16, s13)
    cs_sb = kc.sb("cs_sb", [128, 2 * 2 * 512], F32, s13)
    rot_sb = kc.sb("rot_sb", [128, 512], F32, s13)
    t1_sb = kc.sb("t1_sb", [128, 512], F32, s13)
    t2_sb = kc.sb("t2_sb", [128, 512], F32, s13)
    qk_bf = kc.sb("qk_bf", [128, 2 * 512], BF16, s13)
    sq_bf = kc.sb("sq_bf", [128, 2 * 512], BF16, s13)
    ntmp = kc.sb("ntmp", [1, 2], F32, s13)
    runmax = kc.sb("runmax", [1, 2], F32, s13)
    v_bf = kc.sb("v_bf", [128, 2 * 512], BF16, s13)
    v_stage = kc.sb("v_stage", [128, 2 * 512], BF16, s13)
    ps_mm = kc.ps_("ps_mm", [128, 1024], F32, s13)
    ps_n = kc.ps_("ps_n", [1, 1024], F32, s13)
    ps_vt0 = kc.ps_("ps_vt0", [128, 512], BF16, s13)
    ps_vt1 = kc.ps_("ps_vt1", [128, 512], BF16, s13)

    wl3 = Stream(kc, step=16, per_unit=3, cls="s")
    rl3 = Stream(kc, parity=2, step=16, per_unit=3, cls="h")
    pe3 = Stream(kc)
    rpq = Stream(kc)
    rpv = Stream(kc)
    sqr = Stream(kc)
    pnm = Stream(kc)
    nrd = Stream(kc)
    spl = Stream(kc, parity=2, step=16, cls="s")
    vtp = Stream(kc)
    vev = Stream(kc)
    vsp = Stream(kc, parity=2, step=16, cls="h")

    NDJ = 12
    TC_ORDER = [o * 4 + tkl for tkl in range(4) for o in range(NCORES)]

    def QI(ti, dj):
        return ti * 8 + dj

    def WI(ti, dj):
        return ti * 4 + (dj - 8)

    with nc.Block() as blk:
        @blk.gpsimd
        def _(gp):
            kc.wait_barrier(gp)
            for wi, wd in enumerate((wq_sl, wk_sl, wv_sl)):
                wl3.inc(gp.dma_start(
                    w_qkv.ap().rearrange("p (c d) -> p c d", c=32)[:, :, wi * 512:(wi + 1) * 512],
                    wd.rearrange("(c p) d -> p c d", p=128)))
            for ti, tc in enumerate(TC_ORDER):
                for dj in range(8):
                    q = QI(ti, dj)
                    rpq.wait(gp, q)
                    dst = qT_sc if dj < 4 else kT_sc
                    jj = dj % 4
                    spl.inc(gp.dma_start(dst[jj * 128:(jj + 1) * 128, tc * 512:(tc + 1) * 512],
                                         qk_bf[:, (q % 2) * 512:(q % 2 + 1) * 512]))

        @blk.sync
        def _(sy):
            kc.wait_barrier(sy)
            ag0.wait(sy, 0)

            def vspill(ti2, dj):
                tc2 = TC_ORDER[ti2]
                w = WI(ti2, dj)
                vev.wait(sy, w)
                vsp.inc(sy.dma_start(
                    v_sc.rearrange("(tc s p) f -> p tc s f", p=128, s=4)[:, tc2, :,
                                                                         (dj - 8) * 128:(dj - 7) * 128],
                    v_stage.ap().rearrange("p (b s f) -> p b s f", b=2, s=4)[:, w % 2, :, :]))

            for ti, tc in enumerate(TC_ORDER):
                o, tkl = tc // 4, tc % 4
                if tkl >= 1:
                    ag0.wait(sy, tkl)
                pe3.wait(sy, (ti - 1) * NDJ - 1)
                agt = ag_ln_out_q[tkl]
                rl3.inc(sy.dma_start(
                    rhs_sb.ap().rearrange("p (b c t) -> p b c t", b=2, c=32)[:, ti % 2, :, :],
                    agt.rearrange("(o c p) t -> p o c t", o=NCORES, p=128)[:, o, :, :]))
                rl3.inc(sy.dma_start(
                    cs_sb.ap().rearrange("p (b x t) -> p b x t", b=2, x=2)[:, ti % 2, 0, :],
                    cosg[:, tc * 512:(tc + 1) * 512]))
                rl3.inc(sy.dma_start(
                    cs_sb.ap().rearrange("p (b x t) -> p b x t", b=2, x=2)[:, ti % 2, 1, :],
                    sing[:, tc * 512:(tc + 1) * 512]))
                if ti >= 1:
                    for dj in range(8, NDJ):
                        vspill(ti - 1, dj)
            for dj in range(8, NDJ):
                vspill(NGT - 1, dj)

        @blk.tensor
        def _(t):
            kc.wait_barrier(t)
            wl3.wait(t, 0)
            for ti, tc in enumerate(TC_ORDER):
                rl3.wait(t, ti)
                for dj in range(NDJ):
                    g = ti * NDJ + dj
                    g2 = g - 2
                    if g2 >= 0:
                        dj2 = g2 % NDJ
                        if dj2 < 8:
                            rpq.wait(t, QI(g2 // NDJ, dj2))
                        else:
                            rpv.wait(t, WI(g2 // NDJ, dj2))
                    for cc in range(32):
                        ii = t.matmul(ps_mm[:, (g % 2) * 512:(g % 2 + 1) * 512],
                                      w_qkv.ap().rearrange("p (c d) -> p c d", c=32)[:, cc,
                                                                                     dj * 128:(dj + 1) * 128],
                                      rhs_sb.ap().rearrange("p (b c t) -> p b c t", b=2, c=32)[:, ti % 2, cc, :],
                                      start=(cc == 0), stop=(cc == 31))
                    pe3.inc(ii)
                    if dj < 8:
                        q = QI(ti, dj)
                        if q >= 1:
                            qq = q - 1
                            sqr.wait(t, qq)
                            nrd.wait(t, 2 * (qq - 2) + 1)
                            pnm.inc(t.matmul(ps_n[0:1, (qq % 2) * 512:(qq % 2 + 1) * 512],
                                             sh.ones_c[:], sq_bf[:, (qq % 2) * 512:(qq % 2 + 1) * 512],
                                             start=True, stop=True))
                    else:
                        w = WI(ti, dj)
                        if w >= 1:
                            ww = w - 1
                            rpv.wait(t, ww)
                            vev.wait(t, ww - 2)
                            pvt = ps_vt0 if ww % 2 == 0 else ps_vt1
                            for sub in range(4):
                                ii = t.transpose(pvt[:, sub * 128:(sub + 1) * 128],
                                                 v_bf[:, (ww % 2) * 512 + sub * 128:(ww % 2) * 512 + (sub + 1) * 128],
                                                 sh.ident[:])
                            vtp.inc(ii)
            qq = QI(NGT - 1, 7)
            sqr.wait(t, qq)
            nrd.wait(t, 2 * (qq - 2) + 1)
            pnm.inc(t.matmul(ps_n[0:1, (qq % 2) * 512:(qq % 2 + 1) * 512],
                             sh.ones_c[:], sq_bf[:, (qq % 2) * 512:(qq % 2 + 1) * 512],
                             start=True, stop=True))
            ww = WI(NGT - 1, 11)
            rpv.wait(t, ww)
            vev.wait(t, ww - 2)
            pvt = ps_vt0 if ww % 2 == 0 else ps_vt1
            for sub in range(4):
                ii = t.transpose(pvt[:, sub * 128:(sub + 1) * 128],
                                 v_bf[:, (ww % 2) * 512 + sub * 128:(ww % 2) * 512 + (sub + 1) * 128],
                                 sh.ident[:])
            vtp.inc(ii)

        @blk.vector
        def _(v):
            kc.wait_barrier(v)
            for ti, tc in enumerate(TC_ORDER):
                for dj in range(8):
                    g = ti * NDJ + dj
                    q = QI(ti, dj)
                    pe3.wait(v, g)
                    ps = ps_mm[:, (g % 2) * 512:(g % 2 + 1) * 512]
                    cos = cs_sb.ap().rearrange("p (b x t) -> p b x t", b=2, x=2)[:, ti % 2, 0, :]
                    sin = cs_sb.ap().rearrange("p (b x t) -> p b x t", b=2, x=2)[:, ti % 2, 1, :]
                    v.tensor_copy(rot_sb[0:64, :], ps[64:128, :])
                    v.tensor_copy(rot_sb[64:128, :], ps[0:64, :])
                    v.tensor_tensor(t1_sb[:], ps, cos, ALU.mult)
                    v.tensor_tensor(t2_sb[:], rot_sb[:], sin, ALU.mult)
                    spl.wait(v, q - 2)
                    sqr.wait(v, q - 2)
                    rpq.inc(v.tensor_tensor(qk_bf[:, (q % 2) * 512:(q % 2 + 1) * 512],
                                            t1_sb[:], t2_sb[:], ALU.add))
                    if q >= 1:
                        qq = q - 1
                        col = 0 if (qq % 8) < 4 else 1
                        pnm.wait(v, qq)
                        nrd.inc(v.tensor_reduce(ntmp[0:1, col:col + 1],
                                                ps_n[0:1, (qq % 2) * 512:(qq % 2 + 1) * 512],
                                                AX.X, ALU.max))
                        nrd.wait(v, 2 * qq)
                        if qq == 0:
                            ii = v.tensor_copy(runmax[0:1, 0:1], ntmp[0:1, 0:1])
                        elif qq == 4:
                            ii = v.tensor_copy(runmax[0:1, 1:2], ntmp[0:1, 1:2])
                        else:
                            ii = v.tensor_tensor(runmax[0:1, col:col + 1], runmax[0:1, col:col + 1],
                                                 ntmp[0:1, col:col + 1], ALU.max)
                        nrd.inc(ii)
                for dj in range(8, NDJ):
                    w = WI(ti, dj)
                    if w >= 1:
                        ww = w - 1
                        vtp.wait(v, ww)
                        vsp.wait(v, ww - 2)
                        pvt = ps_vt0 if ww % 2 == 0 else ps_vt1
                        vev.inc(v.tensor_copy(
                            v_stage.ap().rearrange("p (b f) -> p b f", b=2)[:, ww % 2, :], pvt[:]))
            qq = QI(NGT - 1, 7)
            pnm.wait(v, qq)
            nrd.inc(v.tensor_reduce(ntmp[0:1, 1:2], ps_n[0:1, (qq % 2) * 512:(qq % 2 + 1) * 512],
                                    AX.X, ALU.max))
            nrd.wait(v, 2 * qq)
            nrd.inc(v.tensor_tensor(runmax[0:1, 1:2], runmax[0:1, 1:2], ntmp[0:1, 1:2], ALU.max))
            ww = WI(NGT - 1, 11)
            vtp.wait(v, ww)
            vsp.wait(v, ww - 2)
            pvt = ps_vt0 if ww % 2 == 0 else ps_vt1
            vev.inc(v.tensor_copy(v_stage.ap().rearrange("p (b f) -> p b f", b=2)[:, ww % 2, :], pvt[:]))

        @blk.scalar
        def _(s):
            kc.wait_barrier(s)
            for ti, tc in enumerate(TC_ORDER):
                for dj in range(NDJ):
                    g = ti * NDJ + dj
                    if dj < 8:
                        q = QI(ti, dj)
                        rpq.wait(s, q)
                        pnm.wait(s, q - 2)
                        sqr.inc(s.activation(sq_bf[:, (q % 2) * 512:(q % 2 + 1) * 512],
                                             qk_bf[:, (q % 2) * 512:(q % 2 + 1) * 512], AF.Square))
                    else:
                        w = WI(ti, dj)
                        pe3.wait(s, g)
                        vtp.wait(s, w - 2)
                        rpv.inc(s.copy(v_bf[:, (w % 2) * 512:(w % 2 + 1) * 512],
                                       ps_mm[:, (g % 2) * 512:(g % 2 + 1) * 512]))

    nD = Stream(kc, cls="p")
    nA1 = Stream(kc, step=16, cls="s")
    nA2 = Stream(kc, step=16, cls="s")
    nCl = Stream(kc, cls="s")
    nrm8 = kc.sb("nrm8", [8, 2], F32, s13)
    gmax = kc.sb("gmax", [1, 2], F32, s13)
    lnm = kc.sb("lnm", [1, 2], F32, s13)
    msum = kc.sb("msum", [1, 1], F32, s13)
    negs = kc.sb("negs", [1, 1], F32, s13)
    ps_nm = kc.ps_("ps_nm", [128, 64], F32, s13)
    with nc.Block() as blk:
        @blk.gpsimd
        def _(gp):
            nrd.wait_done(gp)
            nA1.inc(gp.dma_start(nrm_in, runmax[0:1, :]))
            nA1.wait(gp, 0)
            nCl.inc(gp.collective_compute("AllGather", ALU.bypass,
                                          replica_groups=[list(range(NCORES))],
                                          ins=[nrm_in.opt()], outs=[nrm_out.opt()]))
            nCl.wait(gp, 0)
            nA2.inc(gp.dma_start(nrm8[:], nrm_out))
            nA2.wait(gp, 0)
            nD.inc(gp.tensor_reduce(gmax[0:1, :], nrm8[:], AX.C, ALU.max))

        @blk.scalar
        def _(s):
            nD.wait(s, 0)
            nD.inc(s.activation(lnm[:], gmax[:], AF.Ln))

        @blk.vector
        def _(v):
            nD.wait(v, 1)
            nD.inc(v.tensor_reduce(msum[:], lnm[:], AX.X, ALU.add))

        @blk.scalar
        def _(s):
            nD.wait(s, 2)
            nD.inc(s.activation(msum[:], msum[:], AF.Exp, scale=0.5))

        @blk.vector
        def _(v):
            nD.wait(v, 3)
            nD.inc(v.tensor_scalar(negs[:], msum[:], -SCALE_A, -1.0, ALU.mult, ALU.add))

        @blk.tensor
        def _(t):
            nD.wait(t, 4)
            nD.inc(t.matmul(ps_nm[:, 0:1], sh.ones_rf[:], negs[:], start=True, stop=True))

        @blk.vector
        def _(v):
            nD.wait(v, 5)
            nD.inc(v.tensor_copy(sh.negM[:], ps_nm[:, 0:1]))

    s13.close()
    kc.recycle()

    p4 = ExitStack()
    tri_sb = kc.sb("tri4", [128, 4 * 512], F32, p4)
    q_sb = kc.sb("q_sb", [128, 2 * 512], BF16, p4)
    k_sb = kc.sb("k_sb", [128, 2 * 4096], BF16, p4)
    v_sb4 = kc.sb("v_sb4", [128, 2 * 4096], BF16, p4)
    p_bf = kc.sb("p_bf", [128, 2 * 1024], BF16, p4)
    lnl = kc.sb("lnl", [1, 512], F32, p4)
    linv = kc.sb("linv", [1, 512], F32, p4)
    lcols = kc.sb("lcols", [128, 4], F32, p4)
    ou_bf = kc.sb("ou_bf", [128, 2 * 512], BF16, p4)
    o_st = kc.sb("o_st", [128, 2 * 512], BF16, p4)
    ps_s4 = kc.ps_("ps_s4", [128, 2048], F32, p4)
    ps_o4 = kc.ps_("ps_o4", [128, 512], F32, p4)
    ps_l4 = kc.ps_("ps_l4", [1, 512], F32, p4)
    ps_lt = kc.ps_("ps_lt", [128, 4], F32, p4)
    ps_t4 = kc.ps_("ps_t4", [128, 512], BF16, p4)

    g_sb = kc.sb("g_sb", [128, 2 * 2048], BF16, p4)
    g_st = kc.sb("g_st", [128, 2 * 2048], BF16, p4)
    sh.gidx = kc.sb("gidx", [128, 128], U32, p4)
    ag2 = Stream(kc, cls="s")
    trl = Stream(kc, step=16, per_unit=2, cls="h")
    jld = Stream(kc, parity=2, step=16, per_unit=5, cls="h")
    qld = Stream(kc, parity=2, step=16, cls="h")
    pes = Stream(kc)
    msk = Stream(kc)
    exq = Stream(kc)
    pvl = Stream(kc)
    lnl1 = Stream(kc)
    fle = Stream(kc)
    oe1 = Stream(kc)
    ltr = Stream(kc)
    lcp = Stream(kc)
    otr = Stream(kc)
    nrm4 = Stream(kc)
    osp = Stream(kc, parity=2, step=16, per_unit=2, cls="h")

    NKT = {}
    NFIRST = {}
    DFIRST = {}
    PFIRST = {}
    n_ctr = 0
    d_ctr = 0
    p_ctr = 0
    for jb in range(16):
        for qb in range(8):
            u = jb * 8 + qb
            NFIRST[u] = n_ctr
            NKT[u] = 4 * (qb + 1)
            DFIRST[u] = d_ctr
            PFIRST[u] = p_ctr
            n_ctr += NKT[u]
            d_ctr += 4
            p_ctr += NKT[u] // 2

    def emit_otr(t, u):
        fle.wait(t, u)
        lcp.wait(t, u - 1)
        for sub in range(4):
            ii = t.transpose(ps_lt[:, sub:sub + 1], linv[0:1, sub * 128:(sub + 1) * 128], sh.ones_rf[0:1, 0:1])
        ltr.inc(ii)
        oe1.wait(t, u)
        nrm4.wait(t, u - 1)
        for sub in range(4):
            ii = t.transpose(ps_t4[:, sub * 128:(sub + 1) * 128],
                             ou_bf[:, (u % 2) * 512 + sub * 128:(u % 2) * 512 + (sub + 1) * 128],
                             sh.ident[:])
        otr.inc(ii)

    def ospill(sy, u):
        jb2, qb2 = u // 8, u % 8
        j2, b2 = jb2 // 4, jb2 % 4
        nrm4.wait(sy, u)
        tgt = attn_ag_in_a if j2 < 2 else attn_ag_in_b
        jj2 = j2 % 2
        for s_lo in range(2):
            osp.inc(sy.dma_start(
                AP(tgt.tensor, qb2 * 256 * 2048 + s_lo * 128 * 2048 + b2 * 512 + jj2 * 128,
                   [[2048, 128], [256, 2], [1, 128]]),
                AP(o_st.ap().tensor, (u % 2) * 512 + s_lo * 128, [[1024, 128], [256, 2], [1, 128]])))

    with nc.Block() as blk:
        @blk.gpsimd
        def _(gp):
            osp.wait(gp, 63)
            ag2.inc(gp.collective_compute("AllGather", ALU.bypass,
                                          replica_groups=[list(range(NCORES))],
                                          ins=[attn_ag_in_a.opt()], outs=[attn_ag_out_a.opt()]))
            osp.wait(gp, 126)
            osp.wait(gp, 127)
            ag2.inc(gp.collective_compute("AllGather", ALU.bypass,
                                          replica_groups=[list(range(NCORES))],
                                          ins=[attn_ag_in_b.opt()], outs=[attn_ag_out_b.opt()]))

        @blk.sync
        def _(sy):
            kc.wait_barrier(sy)
            nD.wait(sy, 6)
            trl.inc(sy.dma_start(tri_sb[:], tri_in))
            trl.inc(sy.dma_start(sh.gidx[:], gidx_in))
            for jb in range(16):
                j, b = jb // 4, jb % 4
                if jb >= 2:
                    u2 = (jb - 2) * 8 + 7
                    pvl.wait(sy, NFIRST[u2] + NKT[u2] - 1)
                jld.inc(sy.dma_start(
                    AP(k_sb.ap().tensor, (jb % 2) * 4096, [[2 * 4096, 128], [512, 8], [1, 512]]),
                    AP(kT_sc.tensor, (j * 128) * (B * T) + b * 512, [[B * T, 128], [2048, 8], [1, 512]])))
                for sub in range(4):
                    jld.inc(sy.dma_start(
                        AP(v_sb4.ap().tensor, (jb % 2) * 4096 + sub * 128,
                           [[2 * 4096, 128], [512, 8], [1, 128]]),
                        AP(v_sc.tensor, (b * 512 + sub * 128) * (HPC * HD) + j * 128,
                           [[HPC * HD, 128], [2048 * HPC * HD, 8], [1, 128]])))
                for qb in range(8):
                    u = jb * 8 + qb
                    if u >= 2:
                        u2 = u - 2
                        pes.wait(sy, NFIRST[u2] + NKT[u2] - 1)
                    qld.inc(sy.dma_start(
                        q_sb[:, (u % 2) * 512:(u % 2 + 1) * 512],
                        qT_sc[j * 128:(j + 1) * 128, qb * 2048 + b * 512:qb * 2048 + b * 512 + 512]))
                    if u >= 2:
                        ospill(sy, u - 2)
            ospill(sy, 126)
            ospill(sy, 127)

        @blk.tensor
        def _(t):
            kc.wait_barrier(t)
            nD.wait(t, 6)
            for jb in range(16):
                jld.wait(t, jb)
                for qb in range(8):
                    u = jb * 8 + qb
                    qld.wait(t, u)
                    L = NKT[u]
                    n0 = NFIRST[u]
                    M = L // 2
                    m0 = PFIRST[u]
                    for m in range(M + 1):
                        if m == 1:
                            oe1.wait(t, u - 1)
                            lnl1.wait(t, u - 1)
                        if m < M:
                            exq.wait(t, 2 * (m0 + m - 2) + 1)
                            for t2 in range(2):
                                n = n0 + 2 * m + t2
                                kt = 2 * m + t2
                                pes.inc(t.matmul(ps_s4[:, ((m % 2) * 2 + t2) * 512:((m % 2) * 2 + t2 + 1) * 512],
                                                 k_sb[:, (jb % 2) * 4096 + kt * 128:(jb % 2) * 4096 + (kt + 1) * 128],
                                                 q_sb[:, (u % 2) * 512:(u % 2 + 1) * 512],
                                                 start=True, stop=True))
                        if m >= 1:
                            for t2 in range(2):
                                exq.wait(t, 2 * (m0 + m - 1) + t2)
                                kt = 2 * (m - 1) + t2
                                n = n0 + kt
                                t.matmul(ps_o4[:],
                                         v_sb4[:, (jb % 2) * 4096 + kt * 128:(jb % 2) * 4096 + (kt + 1) * 128],
                                         p_bf[:, ((m - 1) % 2) * 1024 + t2 * 512:((m - 1) % 2) * 1024 + (t2 + 1) * 512],
                                         start=(kt == 0), stop=(kt == L - 1))
                                pvl.inc(t.matmul(ps_l4[0:1, 0:512],
                                                 sh.ones_c[:],
                                                 p_bf[:, ((m - 1) % 2) * 1024 + t2 * 512:((m - 1) % 2) * 1024 + (t2 + 1) * 512],
                                                 start=(kt == 0), stop=(kt == L - 1)))
                    if u >= 1:
                        emit_otr(t, u - 1)
            emit_otr(t, 127)

        @blk.vector
        def _(v):
            kc.wait_barrier(v)
            trl.wait(v, 0)

            def vfin(u):
                ltr.wait(v, u)
                lcp.inc(v.tensor_copy(lcols[:], ps_lt[:]))
                otr.wait(v, u)
                lcp.wait(v, u)
                osp.wait(v, u - 2)
                for sub in range(4):
                    ii = v.tensor_scalar(o_st[:, (u % 2) * 512 + sub * 128:(u % 2) * 512 + (sub + 1) * 128],
                                         ps_t4[:, sub * 128:(sub + 1) * 128],
                                         lcols[:, sub:sub + 1], None, ALU.mult)
                nrm4.inc(ii)

            def oevac(u):
                pvl.wait(v, NFIRST[u] + NKT[u] - 1)
                otr.wait(v, u - 2)
                oe1.inc(v.tensor_copy(ou_bf[:, (u % 2) * 512:(u % 2 + 1) * 512], ps_o4[:]))

            for jb in range(16):
                for qb in range(8):
                    u = jb * 8 + qb
                    L = NKT[u]
                    n0 = NFIRST[u]
                    if u >= 1:
                        oevac(u - 1)
                    for kt in range(4 * qb, L):
                        n = n0 + kt
                        pes.wait(v, n)
                        dd = kt - 4 * qb
                        mloc = (kt // 2) % 2
                        t2 = kt % 2
                        sl = (mloc * 2 + t2) * 512
                        msk.inc(v.tensor_tensor(ps_s4[:, sl:sl + 512],
                                                ps_s4[:, sl:sl + 512],
                                                tri_sb[:, dd * 512:(dd + 1) * 512], ALU.add))
                    if u >= 1:
                        vfin(u - 1)
            oevac(127)
            vfin(127)

        @blk.scalar
        def _(s):
            kc.wait_barrier(s)
            nD.wait(s, 6)
            for jb in range(16):
                for qb in range(8):
                    u = jb * 8 + qb
                    L = NKT[u]
                    n0 = NFIRST[u]
                    M = L // 2
                    m0 = PFIRST[u]
                    for m in range(M):
                        for t2 in range(2):
                            kt = 2 * m + t2
                            if kt >= 4 * qb:
                                msk.wait(s, DFIRST[u] + (kt - 4 * qb))
                            else:
                                pes.wait(s, n0 + kt)
                            pvl.wait(s, n0 + 2 * (m - 2) + t2)
                            exq.inc(s.activation(
                                p_bf[:, (m % 2) * 1024 + t2 * 512:(m % 2) * 1024 + (t2 + 1) * 512],
                                ps_s4[:, (m % 2) * 1024 + t2 * 512:(m % 2) * 1024 + (t2 + 1) * 512],
                                AF.Exp, bias=sh.negM[:], scale=SCALE_A))
                    pvl.wait(s, n0 + L - 1)
                    lnl1.inc(s.activation(lnl[0:1, :], ps_l4[0:1, 0:512], AF.Ln))
                    lnl1.wait(s, u)
                    fle.inc(s.activation(linv[0:1, :], lnl[0:1, :], AF.Exp, scale=-1.0))



    p6streams = []
    prev_gev = None
    for half, (agt, agwait) in enumerate(((attn_ag_out_a, 0), (attn_ag_out_b, 1))):
        gix = Stream(kc, parity=2, step=16, cls="s")
        gtp = Stream(kc)
        gev = Stream(kc)
        gsp = Stream(kc, parity=2, step=16, per_unit=2, cls="h")
        agt_w = agt.rearrange("(a b) f -> a (b f)", b=8)
        pg = prev_gev
        with nc.Block() as blk:
            @blk.gpsimd
            def _(gp):
                osp.wait_done(gp)
                trl.wait(gp, 0)
                ag2.wait(gp, agwait)
                for g in range(16):
                    gtp.wait(gp, (g - 2) * 4 + 3)
                    gix.inc(gp.indirect_dma_start(
                        g_sb[:, (g % 2) * 2048:(g % 2) * 2048 + 2048], None,
                        agt_w, IndirectOffsetOnAxis(sh.gidx[:, g:g + 1], 0)))

            @blk.tensor
            def _(t):
                nrm4.wait(t, 127)
                if pg is not None:
                    pg.wait(t, 15 * 4 + 3)
                for g in range(16):
                    gix.wait(t, g)
                    for q in range(4):
                        bidx = g * 4 + q
                        gev.wait(t, bidx - 1)
                        for kk in range(4 * q, 4 * q + 4):
                            ii = t.transpose(
                                ps_t4[:, (kk % 4) * 128:(kk % 4 + 1) * 128],
                                g_sb[:, (g % 2) * 2048 + kk * 128:(g % 2) * 2048 + (kk + 1) * 128],
                                sh.ident[:])
                        gtp.inc(ii)

            @blk.scalar
            def _(s):
                for g in range(16):
                    gsp.wait(s, g - 2)
                    for q in range(4):
                        bidx = g * 4 + q
                        gtp.wait(s, bidx)
                        gev.inc(s.copy(g_st[:, (g % 2) * 2048 + q * 512:(g % 2) * 2048 + (q + 1) * 512],
                                       ps_t4[:, 0:512]))

            @blk.sync
            def _(sy):
                for g in range(16):
                    op, v = g // 2, g % 2
                    gev.wait(sy, g * 4 + 3)
                    for sub in range(2):
                        gsp.inc(sy.dma_start(
                            AP(attnT_loc.tensor, (4 * op + 2 * half + sub) * 128 * TL + v * 128,
                               [[TL, 128], [256, 8], [1, 128]]),
                            AP(g_st.ap().tensor, (g % 2) * 2048 + sub * 128,
                               [[4096, 128], [256, 8], [1, 128]])))
        p6streams += [gsp, gev]
        prev_gev = gev

    p4.close()
    kc.recycle()

    p7 = ExitStack()
    emit_out_proj(kc, "op_a", sh, attnT_loc, woT, x_core, x1_sc, p7)
    p7.close()
    kc.recycle()

    s89 = ExitStack()
    lnT2 = kc.sb("lnT2", [128, 32 * TL], BF16, s89)
    emit_ln(kc, x1_sc, lnT2, sh, "ln2", s89)
    emit_ffn(kc, "ffn1", sh, lnT2, x1_sc, x2_sc, True, s89)
    s89.close()
    kc.recycle()

    s1011 = ExitStack()
    lnT3 = kc.sb("lnT3", [128, 32 * TL], BF16, s1011)
    emit_ln(kc, x2_sc, lnT3, sh, "ln3", s1011)

    wb_sb = kc.sb("wb_sb", [128, 2 * 32 * 512], BF16, s1011)
    qb_stage = kc.sb("qb_stage", [128, 2 * 512], BF16, s1011)
    ps_qb = kc.ps_("ps_qb", [128, 1024], F32, s1011)
    wlb11 = Stream(kc, parity=2, step=16, cls="s")
    pe11 = Stream(kc)
    ev11 = Stream(kc)
    st11 = Stream(kc, parity=2, step=16, cls="h")
    with nc.Block() as blk:
        @blk.gpsimd
        def _(gp):
            kc.wait_barrier(gp)
            for pc in range(12):
                pe11.wait(gp, (pc - 1) * 16 - 1)
                wlb11.inc(gp.dma_start(
                    wb_sb.ap().rearrange("p (b c d) -> p b c d", b=2, c=32)[:, pc % 2, :, :],
                    wqkvT.rearrange("(c p) d -> p c d", p=128)[:, :, pc * 512:(pc + 1) * 512]))

        @blk.tensor
        def _(t):
            kc.wait_barrier(t)
            for pc in range(12):
                wlb11.wait(t, pc)
                for tt in range(16):
                    it = pc * 16 + tt
                    ev11.wait(t, it - 2)
                    for cc in range(32):
                        ii = t.matmul(ps_qb[:, (it % 2) * 512:(it % 2 + 1) * 512],
                                      lnT3.ap().rearrange("p (c t) -> p c t", c=32)[:, cc,
                                                                                    tt * 128:(tt + 1) * 128],
                                      wb_sb.ap().rearrange("p (b c d) -> p b c d", b=2, c=32)[:, pc % 2, cc, :],
                                      start=(cc == 0), stop=(cc == 31))
                    pe11.inc(ii)

        @blk.scalar
        def _(s):
            kc.wait_barrier(s)
            for pc in range(12):
                for tt in range(16):
                    it = pc * 16 + tt
                    pe11.wait(s, it)
                    st11.wait(s, it - 2)
                    ev11.inc(s.copy(qb_stage[:, (it % 2) * 512:(it % 2 + 1) * 512],
                                    ps_qb[:, (it % 2) * 512:(it % 2 + 1) * 512]))

        @blk.sync
        def _(sy):
            kc.wait_barrier(sy)
            for pc in range(12):
                for tt in range(16):
                    it = pc * 16 + tt
                    ev11.wait(sy, it)
                    st11.inc(sy.dma_start(qkvb_sc[tt * 128:(tt + 1) * 128, pc * 512:(pc + 1) * 512],
                                          qb_stage[:, (it % 2) * 512:(it % 2 + 1) * 512]))

    s1011.close()
    kc.recycle()

    p12 = ExitStack()
    qb_sb = kc.sb("qb_sb", [128, 4 * PB], BF16, p12)
    tmp12 = kc.sb("tmp12", [128, 1024], F32, p12)
    sA = kc.sb("sA", [128, 10 * 32], F32, p12)
    mA = kc.sb("mA", [128, 32], F32, p12)
    eA = kc.sb("eA", [128, 10 * 32], F32, p12)
    pA = kc.sb("pA", [128, 10 * 32], F32, p12)
    sumA = kc.sb("sumA", [128, 32], F32, p12)
    recA = kc.sb("recA", [128, 4 * 32], F32, p12)
    oacc = kc.sb("oacc", [128, C], F32, p12)
    ob_bf = kc.sb("ob_bf", [128, 2 * C], BF16, p12)
    ob_st = kc.sb("ob_st", [128, 2 * 512], BF16, p12)
    ps_tb = kc.ps_("ps_tb", [128, 512], BF16, p12)

    PAIRS = [(i, jb) for i in range(4) for jb in range(i + 1)]
    PIDX = {p: ix for ix, p in enumerate(PAIRS)}

    qbl = Stream(kc, parity=1, step=16, per_unit=4, cls="h")
    v12 = Stream(kc)
    vck = Stream(kc)
    a12 = Stream(kc)
    obc = Stream(kc)
    tbt = Stream(kc)
    ebt = Stream(kc)
    sbb = Stream(kc, parity=2, step=16, cls="h")

    def CK(ut, k):
        return ut * 9 + k

    with nc.Block() as blk:
        @blk.sync
        def _(sy):
            kc.wait_barrier(sy)

            def bspill(w):
                ut2, i2 = w // 4, w % 4
                for grp in range(8):
                    ebt.wait(sy, w * 8 + grp)
                    sbb.inc(sy.dma_start(
                        attnbT_loc.rearrange("(c p) t -> p c t", p=128)[:, grp * 4:(grp + 1) * 4,
                                                                        i2 * 512 + ut2 * 128:i2 * 512 + ut2 * 128 + 128],
                        ob_st.ap().rearrange("p (b s f) -> p b s f", b=2, s=4)[:, (w * 8 + grp) % 2, :, :]))

            for ut in range(4):
                if ut >= 1:
                    for w in range((ut - 1) * 4, ut * 4):
                        bspill(w)
                v12.wait(sy, CK(ut - 1, 8))
                for b in range(4):
                    qbl.inc(sy.dma_start(
                        AP(qb_sb.ap().tensor, b * PB, [[4 * PB, 128], [1, PB]]),
                        qkvb_sc[b * 512 + ut * 128:b * 512 + ut * 128 + 128, :]))
            for w in range(12, 16):
                bspill(w)

        @blk.vector
        def _(v):
            kc.wait_barrier(v)

            def edge(ii):
                vck.inc(ii)
                vck.wait(v, vck.issued - 1)
                return ii

            for ut in range(4):
                qbl.wait(v, ut)
                for (i, jb) in PAIRS:
                    px = PIDX[(i, jb)]
                    for gq in range(4):
                        v.tensor_tensor(
                            tmp12[:],
                            AP(qb_sb.ap().tensor, i * PB + gq * 2 * 768,
                               [[4 * PB, 128], [768, 2], [128, 4], [1, 128]]),
                            AP(qb_sb.ap().tensor, jb * PB + gq * 2 * 768 + 512,
                               [[4 * PB, 128], [768, 2], [0, 4], [1, 128]]),
                            ALU.mult)
                        ii = v.tensor_reduce(
                            AP(sA.ap().tensor, px * 32 + gq * 8,
                               [[10 * 32, 128], [4, 2], [1, 4]]),
                            tmp12.ap().rearrange("p (g q f) -> p g q f", g=2, q=4),
                            AX.X, ALU.add)
                v12.inc(ii)
                v12.wait(v, CK(ut, 0))
                for i in range(4):
                    msl = mA[:]
                    edge(v.tensor_copy(msl, sA[:, PIDX[(i, 0)] * 32:PIDX[(i, 0)] * 32 + 32]))
                    for jb in range(1, i + 1):
                        edge(v.tensor_tensor(msl, msl, sA[:, PIDX[(i, jb)] * 32:PIDX[(i, jb)] * 32 + 32],
                                             ALU.max))
                    for jb in range(i + 1):
                        px = PIDX[(i, jb)]
                        if i == 3 and jb == 3:
                            ii = v.tensor_tensor(eA[:, px * 32:px * 32 + 32],
                                                 sA[:, px * 32:px * 32 + 32], msl, ALU.subtract)
                        else:
                            edge(v.tensor_tensor(eA[:, px * 32:px * 32 + 32],
                                                 sA[:, px * 32:px * 32 + 32], msl, ALU.subtract))
                v12.inc(ii)
                a12.wait(v, ut)
                for i in range(4):
                    ssl = sumA[:]
                    edge(v.tensor_copy(ssl, pA[:, PIDX[(i, 0)] * 32:PIDX[(i, 0)] * 32 + 32]))
                    for jb in range(1, i + 1):
                        edge(v.tensor_tensor(ssl, ssl, pA[:, PIDX[(i, jb)] * 32:PIDX[(i, jb)] * 32 + 32],
                                             ALU.add))
                    if i == 3:
                        ii = v.tensor_copy(recA[:, i * 32:(i + 1) * 32], ssl)
                    else:
                        edge(v.tensor_copy(recA[:, i * 32:(i + 1) * 32], ssl))
                v12.inc(ii)
                v12.wait(v, CK(ut, 2))
                ii = v.reciprocal(recA[:], recA[:])
                v12.inc(ii)
                v12.wait(v, CK(ut, 3))
                for ix, (i, jb) in enumerate(PAIRS):
                    px = PIDX[(i, jb)]
                    if ix == len(PAIRS) - 1:
                        ii = v.tensor_tensor(pA[:, px * 32:px * 32 + 32], pA[:, px * 32:px * 32 + 32],
                                             recA[:, i * 32:(i + 1) * 32], ALU.mult)
                    else:
                        edge(v.tensor_tensor(pA[:, px * 32:px * 32 + 32], pA[:, px * 32:px * 32 + 32],
                                             recA[:, i * 32:(i + 1) * 32], ALU.mult))
                v12.inc(ii)
                v12.wait(v, CK(ut, 4))
                for i in range(4):
                    w = ut * 4 + i
                    obc.wait(v, w - 1)
                    for jb in range(i + 1):
                        px = PIDX[(i, jb)]
                        for gq in range(4):
                            pap = AP(pA.ap().tensor, px * 32 + gq * 8,
                                     [[10 * 32, 128], [4, 2], [1, 4], [0, 128]])
                            vap = AP(qb_sb.ap().tensor, jb * PB + gq * 2 * 768 + 640,
                                     [[4 * PB, 128], [768, 2], [0, 4], [1, 128]])
                            oap = AP(oacc.ap().tensor, gq * 1024,
                                     [[C, 128], [512, 2], [128, 4], [1, 128]])
                            if jb == 0:
                                ii = v.tensor_tensor(oap, pap, vap, ALU.mult)
                            else:
                                v.tensor_tensor(tmp12[:], pap, vap, ALU.mult)
                                ii = v.tensor_tensor(
                                    oap, oap,
                                    tmp12.ap().rearrange("p (g q f) -> p g q f", g=2, q=4),
                                    ALU.add)
                    v12.inc(ii)

        @blk.scalar
        def _(s):
            kc.wait_barrier(s)
            for ut in range(4):
                v12.wait(s, CK(ut, 1))
                a12.inc(s.activation(pA[:], eA[:], AF.Exp, scale=SCALE_B))
                for i in range(4):
                    w = ut * 4 + i
                    v12.wait(s, CK(ut, 5 + i))
                    tbt.wait(s, (w - 2) * 8 + 7)
                    obc.inc(s.copy(ob_bf[:, (w % 2) * C:(w % 2 + 1) * C], oacc[:]))
                    for grp in range(8):
                        tbt.wait(s, w * 8 + grp)
                        sbb.wait(s, w * 8 + grp - 2)
                        ebt.inc(s.copy(ob_st[:, ((w * 8 + grp) % 2) * 512:((w * 8 + grp) % 2 + 1) * 512],
                                       ps_tb[:]))

        @blk.tensor
        def _(t):
            kc.wait_barrier(t)
            for w in range(16):
                obc.wait(t, w)
                for grp in range(8):
                    ebt.wait(t, w * 8 + grp - 1)
                    for k4 in range(4):
                        cc = grp * 4 + k4
                        ii = t.transpose(ps_tb[:, k4 * 128:(k4 + 1) * 128],
                                         ob_bf[:, (w % 2) * C + cc * 128:(w % 2) * C + (cc + 1) * 128],
                                         sh.ident[:])
                    tbt.inc(ii)

    p12.close()
    kc.recycle()

    p13 = ExitStack()
    emit_out_proj(kc, "op_b", sh, attnbT_loc, wprojT, x2_sc, x3_sc, p13)
    p13.close()
    kc.recycle()

    s1415 = ExitStack()
    lnT4 = kc.sb("lnT4", [128, 32 * TL], BF16, s1415)
    emit_ln(kc, x3_sc, lnT4, sh, "ln4", s1415)
    emit_ffn(kc, "ffn2", sh, lnT4, x3_sc, out_y, False, s1415, final=True)
    s1415.close()

    kc.stack.close()
    return nc


_ROPE_CACHE = {}


def _host_tables():
    if "t" in _ROPE_CACHE:
        return _ROPE_CACHE["t"]
    pos = np.empty(B * T, dtype=np.float64)
    for o in range(NCORES):
        for tl in range(TL):
            pos[o * TL + tl] = 512 * o + (tl % 512)
    thetas = 1.0 / 10000.0 ** (np.arange(0, HD, 2, dtype=np.float32).astype(np.float64) / HD)
    pt = np.outer(pos, thetas)
    pt = np.concatenate([pt, pt], axis=-1)
    cosg = np.cos(pt).astype(np.float32).T.copy()
    sinf = np.sin(pt).astype(np.float32).T
    sing = sinf.copy()
    sing[:64, :] *= -1.0
    sing = sing.copy()
    tri = np.zeros((128, 4 * 512), dtype=np.float32)
    for d in range(4):
        kk = np.arange(128)[:, None]
        uu = np.arange(512)[None, :]
        tri[:, d * 512:(d + 1) * 512] = np.where(d * 128 + kk <= uu, 0.0, -1e30)
    ident = np.eye(128, dtype=np.float32)
    _ROPE_CACHE["t"] = (cosg, sing, tri, ident)
    return _ROPE_CACHE["t"]


_NC_CACHE = {}
_LAST_RES = None


def kernel(L0_attn_a_x, L0_attn_a_wq, L0_attn_a_wk, L0_attn_a_wv, L0_attn_a_wo,
           L0_attn_b_weight_qkv, L0_attn_b_weight_proj,
           L0_ffn_A_w, L0_ffn_B_w, L0_ffn_Wq_w, L0_ffn_keys):
    x = np.asarray(L0_attn_a_x, dtype=np.float32)
    cosg, sing, tri, ident = _host_tables()
    wqT = np.ascontiguousarray(np.asarray(L0_attn_a_wq, np.float32).T)
    wkT = np.ascontiguousarray(np.asarray(L0_attn_a_wk, np.float32).T)
    wvT = np.ascontiguousarray(np.asarray(L0_attn_a_wv, np.float32).T)
    woT = np.ascontiguousarray(np.asarray(L0_attn_a_wo, np.float32).T)
    wqkvT = np.ascontiguousarray(np.asarray(L0_attn_b_weight_qkv, np.float32).T)
    wprojT = np.ascontiguousarray(np.asarray(L0_attn_b_weight_proj, np.float32).T)
    AT = np.ascontiguousarray(np.asarray(L0_ffn_A_w, np.float32).T)
    BT = np.ascontiguousarray(np.asarray(L0_ffn_B_w, np.float32).T)
    Wq_raw = np.ascontiguousarray(np.asarray(L0_ffn_Wq_w, np.float32))
    keysT = np.ascontiguousarray(np.asarray(L0_ffn_keys, np.float32).T)

    if "nc" not in _NC_CACHE:
        _NC_CACHE["nc"] = build_kernel()
    nc = _NC_CACHE["nc"]

    in_maps = []
    for r in range(NCORES):
        x_core = np.ascontiguousarray(x[:, 512 * r:512 * (r + 1), :].reshape(TL, C))
        gidx = np.zeros((128, 128), dtype=np.uint32)
        for op in range(NCORES):
            for v in range(2):
                gidx[:, op * 2 + v] = op * 2048 + 256 * r + v * 128 + np.arange(128)
        in_maps.append(dict(
            x_core=x_core,
            wq_sl=np.ascontiguousarray(wqT[:, 512 * r:512 * (r + 1)]),
            wk_sl=np.ascontiguousarray(wkT[:, 512 * r:512 * (r + 1)]),
            wv_sl=np.ascontiguousarray(wvT[:, 512 * r:512 * (r + 1)]),
            woT=woT, wqkvT=wqkvT, wprojT=wprojT,
            AT=AT, BT=BT, Wq_raw=Wq_raw, keysT=keysT,
            cosg=cosg, sing=sing, ident_in=ident, tri_in=tri, gidx_in=gidx,
        ))

    res = run_bass_kernel_spmd(nc, in_maps, core_ids=list(range(NCORES)))
    global _LAST_RES
    _LAST_RES = res
    out = np.empty((B, T, C), dtype=np.float32)
    for r in range(NCORES):
        out[:, 512 * r:512 * (r + 1), :] = res.results[r]["out_y"].reshape(B, 512, C)
    return out
